# version 06
# baseline (speedup 1.0000x reference)
import math
import numpy as np
import ml_dtypes

import concourse.bass as bass
import concourse.mybir as mybir
from concourse.bass_utils import run_bass_kernel_spmd

F32 = mybir.dt.float32
BF16 = mybir.dt.bfloat16
I32 = mybir.dt.int32
AF = mybir.ActivationFunctionType
ALU = mybir.AluOpType
AX = mybir.AxisListType
BF = ml_dtypes.bfloat16

T = 4096
D = 1024
E = 2048
S = 16
SEG = 512
NSEG = T // SEG
NCH = D // 128

ENGINES = ("pe", "act", "dve", "pool", "sp")


class Res:
    __slots__ = ("name", "writers", "readers")

    def __init__(self, name):
        self.name = name
        self.writers = {}
        self.readers = {}


class Rec:
    __slots__ = ("eng", "fn", "deps", "needs_inc", "inc_val", "idx", "is_dma")

    def __init__(self, eng, fn, deps, idx, is_dma=False):
        self.eng = eng
        self.fn = fn
        self.deps = deps
        self.needs_inc = False
        self.inc_val = None
        self.idx = idx
        self.is_dma = is_dma


class DmaSem:
    __slots__ = ("handle", "count")

    def __init__(self, handle):
        self.handle = handle
        self.count = 0


class Emitter:
    def __init__(self, nc):
        self.nc = nc
        self.recs = {e: [] for e in ENGINES}
        self.n = 0
        self.esem = {e: nc.alloc_semaphore(f"es_{e}") for e in ENGINES}
        self.dsem = {"sp": [DmaSem(nc.alloc_semaphore(f"dsp{i}")) for i in range(6)],
                     "pool": [DmaSem(nc.alloc_semaphore(f"dpl{i}")) for i in range(6)],
                     "act": [DmaSem(nc.alloc_semaphore(f"dac{i}")) for i in range(4)]}
        self.drr = {k: 0 for k in self.dsem}
        self.res = {}

    def R(self, handle):
        key = handle if isinstance(handle, str) else handle.name
        if key not in self.res:
            self.res[key] = Res(key)
        return self.res[key]

    def _dep_tokens(self, eng, reads, writes, accum=()):
        deps = []
        seen = set()

        def add(tok):
            if eng == "pe" and tok[0] == "eng" and tok[1] == "pe":
                return
            k = (tok[0], id(tok[1]), tok[2] if tok[0] == "dma" else id(tok[2]))
            if k in seen:
                return
            seen.add(k)
            deps.append(tok)

        for r in reads:
            rr = self.R(r)
            for t in rr.writers.values():
                add(t)
        for w in writes:
            ww = self.R(w)
            for t in ww.writers.values():
                add(t)
            for t in ww.readers.values():
                add(t)
        for w in accum:
            ww = self.R(w)
            for t in ww.readers.values():
                add(t)
        return deps

    @staticmethod
    def _skey(tok):
        return tok[1] if tok[0] == "eng" else id(tok[1])

    def _register(self, tok, reads, writes, accum):
        sk = self._skey(tok)
        for r in reads:
            self.R(r).readers[sk] = tok
        for w in writes:
            ww = self.R(w)
            ww.writers = {sk: tok}
            ww.readers = {}
        for w in accum:
            ww = self.R(w)
            ww.writers[sk] = tok

    def op(self, eng, fn, reads=(), writes=(), accum=()):
        deps = self._dep_tokens(eng, reads, writes, accum)
        rec = Rec(eng, fn, deps, self.n)
        self.n += 1
        self.recs[eng].append(rec)
        tok = ("eng", eng, rec)
        self._register(tok, reads, writes, accum)
        return rec

    def dma(self, eng, out_ap, in_ap, reads=(), writes=(), accum=(), **kw):
        if eng == "sp" and "DRam" in type(out_ap.tensor).__name__:
            eng = "act"
        deps = self._dep_tokens(eng, reads, writes, accum)
        pool = self.dsem[eng]
        ds = pool[self.drr[eng] % len(pool)]
        self.drr[eng] += 1
        ds.count += 16
        val = ds.count

        def fn(e, _o=out_ap, _i=in_ap, _kw=dict(kw), _ds=ds, _v=val):
            return e.dma_start(_o, _i, **_kw).then_inc(_ds.handle, 16)

        rec = Rec(eng, fn, deps, self.n, is_dma=True)
        self.n += 1
        self.recs[eng].append(rec)
        tok = ("dma", ds, val)
        self._register(tok, reads, writes, accum)
        return rec

    def barrier(self):
        marks = {}
        for e in ENGINES:
            for r in reversed(self.recs[e]):
                if not r.is_dma:
                    if r.deps is not None:
                        r.needs_inc = True
                        marks[e] = r
                    break
        snap_dma = [(ds, ds.count) for pool in self.dsem.values() for ds in pool if ds.count]

        def mk(eng):
            def fn(eobj, _marks=marks, _dmas=snap_dma):
                ins = None
                for pe, rec in _marks.items():
                    ins = eobj.wait_ge(self.esem[pe], rec.inc_val)
                for ds, v in _dmas:
                    ins = eobj.wait_ge(ds.handle, v)
                return ins
            return fn

        for e in ENGINES:
            rec = Rec(e, mk(e), [], self.n)
            self.n += 1
            self.recs[e].append(rec)
        self.res = {}

    def _mark(self):
        for e in ENGINES:
            for rec in self.recs[e]:
                for d in rec.deps:
                    if d[0] == "eng":
                        d[2].needs_inc = True
        for e in ENGINES:
            c = 0
            for rec in self.recs[e]:
                if rec.needs_inc:
                    c += 1
                    rec.inc_val = c

    def emit(self):
        self._mark()
        nc = self.nc
        hi = {e: {} for e in ENGINES}

        def body(eng, eobj):
            hw = hi[eng]
            for rec in self.recs[eng]:
                for d in rec.deps:
                    if d[0] == "eng":
                        de, drec = d[1], d[2]
                        if de == eng and drec.eng == eng and eng == "pe":
                            continue
                        sem = self.esem[de]
                        v = drec.inc_val
                        assert v is not None
                        if hw.get(sem, 0) < v:
                            eobj.wait_ge(sem, v)
                            hw[sem] = v
                    else:
                        _, ds, v = d
                        if hw.get(ds.handle, 0) < v:
                            eobj.wait_ge(ds.handle, v)
                            hw[ds.handle] = v
                ins = rec.fn(eobj)
                if rec.needs_inc and rec.inc_val is not None:
                    assert ins is not None, f"needs_inc rec on {eng} returned no instruction"
                    ins.then_inc(self.esem[eng], 1)

        with nc.Block() as block:
            block.tensor(lambda e: body("pe", e))
            block.scalar(lambda e: body("act", e))
            block.vector(lambda e: body("dve", e))
            block.gpsimd(lambda e: body("pool", e))
            block.sync(lambda e: body("sp", e))



def host_prep(ins):
    C = np.ascontiguousarray
    p = {}
    p["w_in"] = C(ins["L0_attn_in_w"].T)
    p["b_in"] = C(ins["L0_attn_in_b"].reshape(32, 128).T)
    cw = ins["L0_attn_conv_w"][:, 0, :]
    for j in range(3):
        p[f"conv_w{j}"] = C(cw[:, j].reshape(16, 128).T)
    cwd = np.zeros((3, 16, 128, 128), np.float32)
    ii = np.arange(128)
    for j in range(3):
        for c in range(16):
            cwd[j, c, ii, ii] = cw[c * 128:(c + 1) * 128, j]
    p["conv_wd"] = cwd
    p["conv_b"] = C(ins["L0_attn_conv_b"].reshape(16, 128).T)
    p["w_dt"] = C(ins["L0_attn_dt_w"].T)
    p["b_dt"] = C(ins["L0_attn_dt_b"].reshape(16, 128).T)
    p["w_bcd"] = C(np.concatenate([ins["L0_attn_B_w"], ins["L0_attn_C_w"],
                                    ins["L0_attn_D_w"]], 0).T)
    p["b_bcdB"] = C(ins["L0_attn_B_b"].reshape(16, 1))
    p["b_bcdC"] = C(ins["L0_attn_C_b"].reshape(16, 1))
    p["b_bcdD"] = C(ins["L0_attn_D_b"].reshape(16, 1))
    p["w_out0"] = C(ins["L0_attn_out_w"].T)
    p["b_out0"] = C(ins["L0_attn_out_b"].reshape(8, 128).T)
    p["ws1"] = C(ins["L0_ffn_Ws1"])
    p["bs1"] = C(ins["L0_ffn_bs1"].reshape(32, 128).T)
    p["ws2"] = C(ins["L0_ffn_Ws2"])
    p["bs2"] = C(ins["L0_ffn_bs2"].reshape(8, 128).T)
    p["wr"] = C(ins["L0_ffn_Wr"])
    p["br"] = C(ins["L0_ffn_br"].reshape(1, 12))
    p["we1"] = np.ascontiguousarray(
        ins["L0_ffn_We1"].reshape(12, 8, 128, 48, 128).transpose(0, 3, 2, 1, 4)
        .reshape(12, 48, 128, 1024))
    p["be1"] = np.stack([C(ins["L0_ffn_be1"][e].reshape(48, 128).T) for e in range(12)])
    p["we2"] = np.ascontiguousarray(
        ins["L0_ffn_We2"].reshape(12, 24, 128, 8, 128).transpose(0, 3, 2, 1, 4)
        .reshape(12, 8, 128, 3072))
    p["be2"] = C(ins["L0_ffn_be2"])
    p["qkv_w"] = C(ins["L1_attn_qkv_w"])
    p["qkv_bq"] = C(ins["L1_attn_qkv_b"][0:1024].reshape(8, 128).T)
    p["qkv_bkv"] = C(ins["L1_attn_qkv_b"][1024:3072].reshape(1, 2048))
    p["w_out1"] = C(ins["L1_attn_out_w"])
    p["b_out1"] = C(ins["L1_attn_out_b"].reshape(8, 128).T)
    p["w1"] = C(ins["L1_attn_w1"])
    p["b1"] = C(ins["L1_attn_b1"].reshape(32, 128).T)
    p["w2"] = C(ins["L1_attn_w2"])
    p["b2"] = C(ins["L1_attn_b2"].reshape(8, 128).T)
    p["g1r"] = C(ins["L1_attn_g1"].reshape(1, 1024))
    p["be1c"] = C(ins["L1_attn_be1"].reshape(8, 128).T)
    p["g2r"] = C(ins["L1_attn_g2"].reshape(1, 1024))
    p["be2c"] = C(ins["L1_attn_be2"].reshape(8, 128).T)
    p["wg"] = C(ins["L1_ffn_w_gate"].T)
    p["wu"] = C(ins["L1_ffn_w_up"].T)
    p["wd"] = C(ins["L1_ffn_w_down"].T)
    p["identf"] = np.eye(128, dtype=np.float32)
    p["onesb"] = np.ones((128, 1), dtype=BF)
    p["onesf"] = np.ones((128, 1), dtype=np.float32)
    p["ones_row"] = np.ones((1, 128), dtype=np.float32)
    p["ones_rowb"] = np.ones((1, 128), dtype=BF)
    return p


def prep_specs():
    sp = {
        "w_in": ([1024, 4096], F32), "b_in": ([128, 32], F32),
        "conv_w0": ([128, 16], F32), "conv_w1": ([128, 16], F32), "conv_w2": ([128, 16], F32),
        "conv_wd": ([3, 16, 128, 128], F32),
        "conv_b": ([128, 16], F32),
        "w_dt": ([2048, 2048], F32), "b_dt": ([128, 16], F32),
        "w_bcd": ([2048, 48], F32), "b_bcdB": ([16, 1], F32),
        "b_bcdC": ([16, 1], F32), "b_bcdD": ([16, 1], F32),
        "w_out0": ([2048, 1024], F32), "b_out0": ([128, 8], F32),
        "ws1": ([1024, 4096], F32), "bs1": ([128, 32], F32),
        "ws2": ([2048, 1024], F32), "bs2": ([128, 8], F32),
        "wr": ([1024, 12], F32), "br": ([1, 12], F32),
        "we1": ([12, 48, 128, 1024], F32), "be1": ([12, 128, 48], F32),
        "we2": ([12, 8, 128, 3072], F32), "be2": ([12, 1024], F32),
        "qkv_w": ([1024, 3072], F32), "qkv_bq": ([128, 8], F32), "qkv_bkv": ([1, 2048], F32),
        "w_out1": ([1024, 1024], F32), "b_out1": ([128, 8], F32),
        "w1": ([1024, 4096], F32), "b1": ([128, 32], F32),
        "w2": ([4096, 1024], F32), "b2": ([128, 8], F32),
        "g1r": ([1, 1024], F32), "be1c": ([128, 8], F32),
        "g2r": ([1, 1024], F32), "be2c": ([128, 8], F32),
        "wg": ([1024, 3072], I32), "wu": ([1024, 3072], I32), "wd": ([3072, 1024], I32),
        "identf": ([128, 128], F32), "onesb": ([128, 1], BF16),
        "onesf": ([128, 1], F32), "ones_row": ([1, 128], F32),
        "ones_rowb": ([1, 128], BF16),
        "x0": ([T, D], F32),
    }
    return sp


SCRATCH = {
    "XR0": ([D, T], F32), "XR1": ([D, T], F32), "XR2": ([D, T], F32), "XR3": ([D, T], F32),
    "XS0": ([D, T], BF16), "XS2": ([D, T], BF16), "XS4": ([D, T], BF16),
    "XS5": ([D, T], BF16), "XS7": ([D, T], BF16),
    "XM": ([E, T], F32), "XG": ([E, T], F32), "XC": ([E, T], BF16),
    "YS": ([D, T], F32), "WSC": ([12, T], F32),
    "QF": ([D, T], BF16), "KFD": ([T, D], BF16), "VD": ([T, D], BF16),
    "KVD": ([128, 512], BF16), "ZCD": ([128, 8], F32),
    "X3": ([D, T], F32), "HH": ([T // 1, 1], F32),
    "HU": ([3072, T], BF16),
    "RA0": ([1, T], F32), "RNM0": ([1, T], F32), "RYS": ([1, T], F32),
    "RA4": ([1, T], F32), "RNM4": ([1, T], F32),
    "DBG": ([64, 4096], F32),
}
SCRATCH["HH"] = ([4096, T], BF16)


class Ctx:
    pass


def build_program(phases, dump=(), feed=(), core_input_names=("x0",)):
    nc = bass.Bass("TRN2", target_bir_lowering=False)
    ctx = Ctx()
    ctx.nc = nc
    ctx.em = Emitter(nc)
    ctx.inp = {}
    for name, (shape, dt) in prep_specs().items():
        ctx.inp[name] = nc.dram_tensor(name, shape, dt, kind="ExternalInput").ap()
    ctx.dram = {}
    for name, (shape, dt) in SCRATCH.items():
        kind = "Internal"
        if name in dump:
            kind = "ExternalOutput"
        if name in feed:
            kind = "ExternalInput"
        ctx.dram[name] = nc.dram_tensor(name, shape, dt, kind=kind).ap()
    ctx.out = nc.dram_tensor("out", [T, D], F32, kind="ExternalOutput").ap()

    ctx.sb = {}
    def g(name, shape, dt):
        ctx.sb[name] = nc.alloc_sbuf_tensor("sb_" + name, shape, dt)
        return ctx.sb[name]
    g("identf", [128, 128], F32)
    g("onesb", [128, 1], BF16)
    g("onesf", [128, 1], F32)
    g("ones_row", [1, 128], F32)
    g("ones_rowb", [1, 128], BF16)
    em = ctx.em
    for cn in ("identf", "onesb", "onesf", "ones_row", "ones_rowb"):
        em.dma("sp", ctx.sb[cn][:], ctx.inp[cn][:, :], writes=[ctx.sb[cn]])
    cst = g("cst", [128, 4], F32)
    em.op("dve", lambda e: e.memset(cst[:, 0:1], 1e-5), writes=[cst])
    em.op("dve", lambda e: e.memset(cst[:, 1:2], 1.0), writes=[cst])
    nc.const_aps.register(F32, 1e-5, cst[:, 0:1])
    nc.const_aps.register(F32, 1.0, cst[:, 1:2])
    ctx.inp_g1r_sb = nc.alloc_sbuf_tensor("sb_g1r", [1, 1024], F32)
    em.dma("sp", ctx.inp_g1r_sb[:], ctx.inp["g1r"][:, :], writes=[ctx.inp_g1r_sb])
    ctx.inp_g2r_sb = nc.alloc_sbuf_tensor("sb_g2r", [1, 1024], F32)
    em.dma("sp", ctx.inp_g2r_sb[:], ctx.inp["g2r"][:, :], writes=[ctx.inp_g2r_sb])
    ctx.g1rb = nc.alloc_sbuf_tensor("sb_g1rb", [1, 1024], BF16)
    em.op("dve", lambda e: e.tensor_copy(ctx.g1rb[:], ctx.inp_g1r_sb[:]),
          reads=[ctx.inp_g1r_sb], writes=[ctx.g1rb])
    ctx.g2rb = nc.alloc_sbuf_tensor("sb_g2rb", [1, 1024], BF16)
    em.op("dve", lambda e: e.tensor_copy(ctx.g2rb[:], ctx.inp_g2r_sb[:]),
          reads=[ctx.inp_g2r_sb], writes=[ctx.g2rb])
    ctx.sb["_lnsq"] = nc.alloc_sbuf_tensor("_lnsq", [128, 8 * 512], BF16)
    ctx.sb["_lnrt"] = nc.alloc_sbuf_tensor("_lnrt", [1, 2 * 512], F32)
    ctx.sb["_lnxc"] = nc.alloc_sbuf_tensor("_lnxc", [128, 2 * 512], BF16)
    ctx.sb["_bcr"] = nc.alloc_sbuf_tensor("_bcr", [1, 4 * 512], BF16)
    ctx._bcr_i = 0
    em.barrier()

    ctx.P = [nc.alloc_psum_tensor(f"P{i}", [128, 512], F32) for i in range(8)]

    import contextlib
    for ph in phases:
        with contextlib.ExitStack() as stack:
            ctx.alloc = (lambda _st: lambda name, shape, dt:
                         _st.enter_context(nc.sbuf_tensor(name, shape, dt)))(stack)
            ph(ctx)
        em.barrier()
    em.emit()
    return nc



def ph0_entry(ctx):
    em, nc, sb = ctx.em, ctx.nc, ctx.sb
    x0 = ctx.inp["x0"]
    XR0 = ctx.dram["XR0"].rearrange("(c p) t -> p c t", p=128)
    xt = [ctx.alloc(f"p0_xt{i}", [128, 1024], F32) for i in range(2)]
    xf = [ctx.alloc(f"p0_xf{i}", [128, NCH * 512], F32) for i in range(2)]
    ident = sb["identf"]
    for s in range(NSEG):
        fbuf = xf[s % 2]
        for q in range(4):
            tt = s * 4 + q
            tbuf = xt[tt % 2]
            em.dma("sp", tbuf[:], x0[tt * 128:(tt + 1) * 128, :], writes=[tbuf])
            for c in range(NCH):
                ps = ctx.P[(tt * NCH + c) % 2]
                em.op("pe", lambda e, _o=ps[:, 0:128], _i=tbuf[:, c * 128:(c + 1) * 128], _id=ident[:]:
                      e.transpose(_o, _i, _id),
                      reads=[tbuf, ident], writes=[ps])
                em.op("dve", lambda e, _o=fbuf[:, c * 512 + q * 128: c * 512 + (q + 1) * 128], _i=ps[:, 0:128]:
                      e.tensor_copy(_o, _i),
                      reads=[ps], accum=[fbuf])
        em.dma("sp", XR0[:, :, s * 512:(s + 1) * 512],
               fbuf[:].rearrange("p (c t) -> p c t", c=NCH), reads=[fbuf])



def ln_stats_seg(ctx, s, src_f32, nfeat_chunks, inv_n, arow, nmrow, pretag=""):
    em, nc, sb = ctx.em, ctx.nc, ctx.sb
    tag = f"ln{pretag}_{s}"
    sq = ctx.sb["_lnsq"]; rt = ctx.sb["_lnrt"]; xbc = ctx.sb["_lnxc"]
    psum = ctx.P[6]; psq = ctx.P[7]
    onesb = sb["onesb"]
    for c in range(nfeat_chunks):
        xb = xbc[:, (c % 2) * 512:(c % 2 + 1) * 512]
        em.op("dve", lambda e, _o=xb, _i=src_f32[c]: e.tensor_copy(_o, _i),
              reads=[src_f32[c].tensor], accum=[xbc])
        em.op("act", lambda e, _o=sq[:, c * 512:(c + 1) * 512], _i=xb:
              e.activation(_o, _i, AF.Square), reads=[xbc], accum=[sq])
        em.op("pe", lambda e, _o=psum[0:1, :], _l=onesb[:], _r=xb,
              _st=(c == 0), _sp=(c == nfeat_chunks - 1):
              e.matmul(_o, _l, _r, start=_st, stop=_sp),
              reads=[xbc, onesb], accum=[psum])
        em.op("pe", lambda e, _o=psq[0:1, :], _l=onesb[:], _r=sq[:, c * 512:(c + 1) * 512],
              _st=(c == 0), _sp=(c == nfeat_chunks - 1):
              e.matmul(_o, _l, _r, start=_st, stop=_sp),
              reads=[sq, onesb], accum=[psq])
    mean = rt[0:1, 0:512]; var = rt[0:1, 512:1024]; msq = var
    em.op("dve", lambda e, _o=mean, _i=psum[0:1, :]: e.tensor_scalar_mul(_o, _i, float(inv_n)),
          reads=[psum], writes=[rt])
    em.op("act", lambda e, _o=msq, _i=mean: e.activation(_o, _i, AF.Square),
          reads=[rt], writes=[rt])
    em.op("dve", lambda e, _o=var, _i=psq[0:1, :], _s=float(inv_n), _t=msq:
          e.scalar_tensor_tensor(_o, _i, _s, _t, ALU.mult, ALU.subtract),
          reads=[psq, rt], writes=[rt])
    em.op("act", lambda e, _o=var, _i=var: e.activation(_o, _i, AF.Ln, bias=1e-5),
          reads=[rt], writes=[rt])
    em.op("act", lambda e, _o=arow, _i=var: e.activation(_o, _i, AF.Exp, scale=-0.5),
          reads=[rt], writes=[arow.tensor])
    em.op("dve", lambda e, _o=nmrow, _i=mean: e.tensor_scalar_mul(_o, _i, -1.0),
          reads=[rt], writes=[nmrow.tensor])


def bcast_row(ctx, row_ap, out_tile, psum_i, width=512, exact=False):
    em, sb = ctx.em, ctx.sb
    ps = ctx.P[psum_i]
    if exact:
        em.op("pe", lambda e, _o=ps[:, 0:width], _l=sb["ones_row"][:], _r=row_ap:
              e.matmul(_o, _l, _r, start=True, stop=True),
              reads=[row_ap.tensor, sb["ones_row"]], writes=[ps])
    else:
        i = ctx._bcr_i % 4
        ctx._bcr_i += 1
        br = ctx.sb["_bcr"][0:1, i * 512:i * 512 + width]
        em.op("dve", lambda e, _o=br, _i=row_ap: e.tensor_copy(_o, _i),
              reads=[row_ap.tensor], accum=[ctx.sb["_bcr"]])
        em.op("pe", lambda e, _o=ps[:, 0:width], _l=sb["ones_rowb"][:], _r=br:
              e.matmul(_o, _l, _r, start=True, stop=True),
              reads=[ctx.sb["_bcr"], sb["ones_rowb"]], writes=[ps])
    em.op("dve", lambda e, _o=out_tile[:, 0:width], _i=ps[:, 0:width]: e.tensor_copy(_o, _i),
          reads=[ps], writes=[out_tile])


def ph1_ln0(ctx):
    em, nc = ctx.em, ctx.nc
    XR0 = ctx.dram["XR0"].rearrange("(c p) t -> p c t", p=128)
    XS0 = ctx.dram["XS0"].rearrange("(c p) t -> p c t", p=128)
    xr = [ctx.alloc(f"p1_xr{i}", [128, NCH * 512], F32) for i in range(2)]
    xs = [ctx.alloc(f"p1_xs{i}", [128, NCH * 512], BF16) for i in range(2)]
    ab = ctx.alloc("p1_ab", [128, 512], F32)
    nmb = ctx.alloc("p1_nmb", [128, 512], F32)
    tmp = ctx.alloc("p1_tmp", [128, 512], F32)
    rows = ctx.alloc("p1_rows", [1, 2 * 512], F32)
    for s in range(NSEG):
        xrb = xr[s % 2]; xsb = xs[s % 2]
        em.dma("sp", xrb[:].rearrange("p (c t) -> p c t", c=NCH),
               XR0[:, :, s * 512:(s + 1) * 512], writes=[xrb])
        src = [xrb[:, c * 512:(c + 1) * 512] for c in range(NCH)]
        arow = rows[0:1, 0:512]; nmrow = rows[0:1, 512:1024]
        ln_stats_seg(ctx, s, src, NCH, 1.0 / D, arow, nmrow)
        em.dma("sp", ctx.dram["RA0"][0:1, s * 512:(s + 1) * 512], arow, reads=[rows])
        em.dma("sp", ctx.dram["RNM0"][0:1, s * 512:(s + 1) * 512], nmrow, reads=[rows])
        bcast_row(ctx, arow, ab, 4)
        bcast_row(ctx, nmrow, nmb, 5)
        for c in range(NCH):
            em.op("dve", lambda e, _o=tmp[:], _a=xrb[:, c * 512:(c + 1) * 512], _b=nmb[:]:
                  e.tensor_add(_o, _a, _b), reads=[xrb, nmb], writes=[tmp])
            em.op("dve", lambda e, _o=xsb[:, c * 512:(c + 1) * 512], _a=tmp[:], _b=ab[:]:
                  e.tensor_mul(_o, _a, _b), reads=[tmp, ab], accum=[xsb])
        em.dma("sp", XS0[:, :, s * 512:(s + 1) * 512],
               xsb[:].rearrange("p (c t) -> p c t", c=NCH), reads=[xsb])



def load_weight_bf16(ctx, name, dram_ap, K, M, eng="pool"):
    em = ctx.em
    w = ctx.alloc(name, [128, (K // 128) * M], BF16)
    for k in range(K // 128):
        em.dma(eng, w[:, k * M:(k + 1) * M], dram_ap[k * 128:(k + 1) * 128, :],
               accum=[w])
    return w


def lhsT(w, K, M, k, m, mw=128):
    return w[:, k * M + m * mw: k * M + m * mw + mw]



def ph2_inproj(ctx):
    em, nc = ctx.em, ctx.nc
    XS0 = ctx.dram["XS0"].rearrange("(c p) t -> p c t", p=128)
    XM = ctx.dram["XM"]
    XG = ctx.dram["XG"]
    w = load_weight_bf16(ctx, "p2_w", ctx.inp["w_in"], 1024, 4096)
    bcol = ctx.alloc("p2_b", [128, 32], F32)
    em.dma("sp", bcol[:], ctx.inp["b_in"][:, :], writes=[bcol])
    xs = [ctx.alloc(f"p2_xs{i}", [128, NCH * 512], BF16) for i in range(2)]
    stg = [ctx.alloc(f"p2_st{i}", [128, 512], F32) for i in range(4)]
    for s in range(NSEG):
        xsb = xs[s % 2]
        em.dma("sp", xsb[:].rearrange("p (c t) -> p c t", c=NCH),
               XS0[:, :, s * 512:(s + 1) * 512], writes=[xsb])
        for m in range(32):
            ps = ctx.P[m % 2]
            for k in range(NCH):
                em.op("pe", lambda e, _o=ps[:, :], _l=lhsT(w, 1024, 4096, k, m),
                      _r=xsb[:, k * 512:(k + 1) * 512], _st=(k == 0), _sp=(k == NCH - 1):
                      e.matmul(_o, _l, _r, start=_st, stop=_sp),
                      reads=[w, xsb], accum=[ps])
            st = stg[m % 4]
            em.op("dve", lambda e, _o=st[:], _i=ps[:, :], _b=bcol[:, m:m + 1]:
                  e.tensor_scalar_add(_o, _i, _b),
                  reads=[ps, bcol], writes=[st])
            dst = XM if m < 16 else XG
            mm = m if m < 16 else m - 16
            em.dma("sp", dst[mm * 128:(mm + 1) * 128, s * 512:(s + 1) * 512], st[:],
                   reads=[st])



def ph3_conv(ctx):
    em, nc = ctx.em, ctx.nc
    XM = ctx.dram["XM"]
    XC = ctx.dram["XC"]
    cw = []
    for j in range(3):
        t = ctx.alloc(f"p3_cw{j}", [128, 16], F32)
        em.dma("sp", t[:], ctx.inp[f"conv_w{j}"][:, :], writes=[t])
        cw.append(t)
    cb = ctx.alloc("p3_cb", [128, 16], F32)
    em.dma("sp", cb[:], ctx.inp["conv_b"][:, :], writes=[cb])
    buf = [ctx.alloc(f"p3_b{i}", [128, 514], F32) for i in range(3)]
    acc = [ctx.alloc(f"p3_a{i}", [128, 512], F32) for i in range(3)]
    stg = [ctx.alloc(f"p3_s{i}", [128, 512], BF16) for i in range(3)]
    ECH = E // 128
    for s in range(NSEG):
        lo = s * 512 - 1
        hi = s * 512 + 513
        for c in range(ECH):
            i = (s * ECH + c) % 3
            b, a, st = buf[i], acc[i], stg[i]
            if s == 0:
                em.op("dve", lambda e, _b=b: e.memset(_b[:, 0:1], 0.0), writes=[b])
                em.dma("sp", b[:, 1:514], XM[c * 128:(c + 1) * 128, 0:513], accum=[b])
            elif s == NSEG - 1:
                em.op("dve", lambda e, _b=b: e.memset(_b[:, 513:514], 0.0), writes=[b])
                em.dma("sp", b[:, 0:513], XM[c * 128:(c + 1) * 128, lo:lo + 513], accum=[b])
            else:
                em.dma("sp", b[:, 0:514], XM[c * 128:(c + 1) * 128, lo:hi], writes=[b])
            em.op("dve", lambda e, _o=a[:], _i=b[:, 0:512], _w=cw[0][:, c:c + 1]:
                  e.tensor_scalar_mul(_o, _i, _w), reads=[b, cw[0]], writes=[a])
            em.op("dve", lambda e, _o=a[:], _i=b[:, 1:513], _w=cw[1][:, c:c + 1], _a=a[:]:
                  e.scalar_tensor_tensor(_o, _i, _w, _a, ALU.mult, ALU.add),
                  reads=[b, cw[1], a], writes=[a])
            em.op("dve", lambda e, _o=a[:], _i=b[:, 2:514], _w=cw[2][:, c:c + 1], _a=a[:]:
                  e.scalar_tensor_tensor(_o, _i, _w, _a, ALU.mult, ALU.add),
                  reads=[b, cw[2], a], writes=[a])
            em.op("act", lambda e, _o=st[:], _i=a[:], _b=cb[:, c:c + 1]:
                  e.activation(_o, _i, AF.Silu, bias=_b),
                  reads=[a, cb], writes=[st])
            em.dma("sp", XC[c * 128:(c + 1) * 128, s * 512:(s + 1) * 512], st[:],
                   reads=[st])



def ph4_scan(ctx):
    em, nc, sb = ctx.em, ctx.nc, ctx.sb
    XM = ctx.dram["XM"]
    ECH = E // 128
    HT = T // 4
    wdt = load_weight_bf16(ctx, "p4_wdt", ctx.inp["w_dt"], 2048, 2048)
    wbcd = load_weight_bf16(ctx, "p4_wbcd", ctx.inp["w_bcd"], 2048, 48)
    bdt = ctx.alloc("p4_bdt", [128, 16], F32)
    em.dma("sp", bdt[:], ctx.inp["b_dt"][:, :], writes=[bdt])
    bbcd = []
    for i, bn in enumerate(("b_bcdB", "b_bcdC", "b_bcdD")):
        t = ctx.alloc("p4_" + bn, [16, 1], F32)
        em.dma("sp", t[:], ctx.inp[bn][:, :], writes=[t])
        bbcd.append(t)
    wcd = []
    for j in range(3):
        t = ctx.alloc(f"p4_wcd{j}", [128, 16 * 128], BF16)
        em.dma("pool", t[:].rearrange("p (c m) -> p c m", c=16),
               ctx.inp["conv_wd"][j].rearrange("c p m -> p c m"), writes=[t])
        wcd.append(t)
    cb = ctx.alloc("p4_cbb", [128, 16], F32)
    em.dma("sp", cb[:], ctx.inp["conv_b"][:, :], writes=[cb])
    cbufA = [ctx.alloc(f"p4_cba{i}", [128, 514], BF16) for i in range(2)]
    cbufB = [ctx.alloc(f"p4_cbb2{i}", [128, 512], BF16) for i in range(2)]
    xc = [ctx.alloc(f"p4_xc{i}", [128, ECH * 512], BF16) for i in range(2)]
    rl = [ctx.alloc(f"p4_rl{i}", [128, 512], BF16) for i in range(2)]
    ab = [ctx.alloc(f"p4_ab{i}", [128, 512], F32) for i in range(2)]
    ln1 = [ctx.alloc(f"p4_ln{i}", [128, 512], BF16) for i in range(2)]
    drow = ctx.alloc("p4_drow", [1, T // 2], F32)
    rt = [ctx.alloc(f"p4_rt{i}", [1, 512], F32) for i in range(2)]
    Bsb = ctx.alloc("p4_B", [16, 4 * 512], F32)
    Csb = ctx.alloc("p4_C", [16, 4 * 512], F32)
    Dsb = ctx.alloc("p4_D", [16, 4 * 512], F32)
    dbc = ctx.alloc("p4_dbc", [16, HT], F32)
    stt = ctx.alloc("p4_st", [16, HT], F32)
    ysr = ctx.alloc("p4_ys", [1, HT], F32)

    carry = ctx.alloc("p4_carry", [16, 1], F32)

    def conv_emit(s):
        xcb = xc[s % 2]
        lo = s * 512 - 1
        for c in range(ECH):
            i = (s * ECH + c) % 2
            bA = cbufA[i]
            bB = cbufB[i]
            if s == 0:
                em.dma("pool", bA[:, 1:514], XM[c * 128:(c + 1) * 128, 0:513], writes=[bA])
                em.op("dve", lambda e, _b=bA: e.memset(_b[:, 0:1], 0.0), accum=[bA])
                em.dma("pool", bB[:, 0:512], XM[c * 128:(c + 1) * 128, 0:512], writes=[bB])
            elif s == NSEG - 1:
                em.dma("pool", bA[:, 0:513], XM[c * 128:(c + 1) * 128, lo:lo + 513], writes=[bA])
                em.op("dve", lambda e, _b=bA: e.memset(_b[:, 513:514], 0.0), accum=[bA])
                em.dma("pool", bB[:, 0:512], XM[c * 128:(c + 1) * 128, lo + 1:lo + 513], writes=[bB])
            else:
                em.dma("pool", bA[:, 0:514], XM[c * 128:(c + 1) * 128, lo:lo + 514], writes=[bA])
                em.dma("pool", bB[:, 0:512], XM[c * 128:(c + 1) * 128, lo + 1:lo + 513], writes=[bB])
            ps = ctx.P[5 if (s * ECH + c) % 2 == 0 else 7]
            em.op("pe", lambda e, _o=ps[:, :], _l=wcd[0][:, c * 128:(c + 1) * 128],
                  _r=bA[:, 0:512]:
                  e.matmul(_o, _l, _r, start=True, stop=False),
                  reads=[bA, wcd[0]], accum=[ps])
            em.op("pe", lambda e, _o=ps[:, :], _l=wcd[1][:, c * 128:(c + 1) * 128],
                  _r=bB[:, 0:512]:
                  e.matmul(_o, _l, _r, start=False, stop=False),
                  reads=[bB, wcd[1]], accum=[ps])
            em.op("pe", lambda e, _o=ps[:, :], _l=wcd[2][:, c * 128:(c + 1) * 128],
                  _r=bA[:, 2:514]:
                  e.matmul(_o, _l, _r, start=False, stop=True),
                  reads=[bA, wcd[2]], accum=[ps])
            em.op("act", lambda e, _o=xcb[:, c * 512:(c + 1) * 512], _i=ps[:, :], _b=cb[:, c:c + 1]:
                  e.activation(_o, _i, AF.Silu, bias=_b),
                  reads=[ps, cb], accum=[xcb])

    def dtbcd_emit(s):
        hs = s % 4
        xcb = xc[s % 2]
        for m in range(16):
            ps = ctx.P[m % 2]
            for k in range(ECH):
                em.op("pe", lambda e, _o=ps[:, :], _l=lhsT(wdt, 2048, 2048, k, m),
                      _r=xcb[:, k * 512:(k + 1) * 512], _st=(k == 0), _sp=(k == ECH - 1):
                      e.matmul(_o, _l, _r, start=_st, stop=_sp),
                      reads=[wdt, xcb], accum=[ps])
            r, A, L = rl[m % 2], ab[m % 2], ln1[m % 2]
            em.op("act", lambda e, _o=r[:], _i=ps[:, :], _b=bdt[:, m:m + 1]:
                  e.activation(_o, _i, AF.Relu, bias=_b),
                  reads=[ps, bdt], writes=[r])
            em.op("act", lambda e, _o=A[:], _i=ps[:, :], _b=bdt[:, m:m + 1]:
                  e.activation(_o, _i, AF.Abs, bias=_b),
                  reads=[ps, bdt], writes=[A])
            em.op("act", lambda e, _o=A[:], _i=A[:]:
                  e.activation(_o, _i, AF.Exp, scale=-1.0), reads=[A], writes=[A])
            em.op("act", lambda e, _o=L[:], _i=A[:]:
                  e.activation(_o, _i, AF.Ln, bias=1.0), reads=[A], writes=[L])
            em.op("pe", lambda e, _o=ctx.P[6][0:1, :], _l=sb["onesb"][:], _r=r[:],
                  _st=(m == 0), _sp=False:
                  e.matmul(_o, _l, _r, start=_st, stop=_sp),
                  reads=[r, sb["onesb"]], accum=[ctx.P[6]])
            em.op("pe", lambda e, _o=ctx.P[6][0:1, :], _l=sb["onesb"][:], _r=L[:],
                  _st=False, _sp=(m == 15):
                  e.matmul(_o, _l, _r, start=_st, stop=_sp),
                  reads=[L, sb["onesb"]], accum=[ctx.P[6]])
        rts = rt[s % 2]
        em.op("dve", lambda e, _o=rts[0:1, :], _i=ctx.P[6][0:1, :]:
              e.tensor_scalar(_o, _i, float(-1.0 / E), -10.0, ALU.mult, ALU.max),
              reads=[ctx.P[6]], writes=[rts])
        em.op("dve", lambda e, _o=rts[0:1, :], _i=rts[0:1, :]:
              e.tensor_scalar_min(_o, _i, 10.0), reads=[rts], writes=[rts])
        em.op("act", lambda e, _o=drow[0:1, (s % 4) * 512:((s % 4) + 1) * 512], _i=rts[0:1, :]:
              e.activation(_o, _i, AF.Exp), reads=[rts], accum=[drow])
        for i, (psi, dst) in enumerate(((2, Bsb), (3, Csb), (4, Dsb))):
            ps = ctx.P[psi]
            for k in range(ECH):
                em.op("pe", lambda e, _o=ps[0:16, :],
                      _l=wbcd[:, k * 48 + i * 16: k * 48 + (i + 1) * 16],
                      _r=xcb[:, k * 512:(k + 1) * 512], _st=(k == 0), _sp=(k == ECH - 1):
                      e.matmul(_o, _l, _r, start=_st, stop=_sp),
                      reads=[wbcd, xcb], accum=[ps])
            em.op("dve", lambda e, _o=dst[:, hs * 512:(hs + 1) * 512], _i=ps[0:16, :],
                  _b=bbcd[i][:, 0:1]:
                  e.tensor_scalar_add(_o, _i, _b),
                  reads=[ps, bbcd[i]], accum=[dst])

    def scan_half(q):
        b0 = ((2 * q) % 4) * 512
        Bq = Bsb[:, b0:b0 + 1024]; Cq = Csb[:, b0:b0 + 1024]; Dq = Dsb[:, b0:b0 + 1024]
        for hs in range(2):
            s = q * 2 + hs
            em.op("pe", lambda e, _o=ctx.P[5][0:16, :], _l=sb["ones_row"][0:1, 0:16],
                  _r=drow[0:1, (s % 4) * 512:((s % 4) + 1) * 512]:
                  e.matmul(_o, _l, _r, start=True, stop=True),
                  reads=[drow, sb["ones_row"]], writes=[ctx.P[5]])
            em.op("dve", lambda e, _o=dbc[:, hs * 512:(hs + 1) * 512], _i=ctx.P[5][0:16, :]:
                  e.tensor_copy(_o, _i), reads=[ctx.P[5]], accum=[dbc])
        em.op("dve", lambda e, _B=Bq, _D=Dq: e.tensor_mul(_B, _B, _D),
              reads=[Bsb, Dsb], accum=[Bsb])
        if q == 0:
            em.op("dve", lambda e, _B=Bq: e.tensor_tensor_scan(stt[:], dbc[:], _B, 0.0,
                                                               ALU.mult, ALU.add),
                  reads=[dbc, Bsb], writes=[stt])
        else:
            em.op("dve", lambda e, _B=Bq: e.tensor_tensor_scan(stt[:], dbc[:], _B,
                                                               carry[:, 0:1], ALU.mult, ALU.add),
                  reads=[dbc, Bsb, carry], writes=[stt])
        em.op("dve", lambda e: e.tensor_copy(carry[:, 0:1], stt[:, HT - 1:HT]),
              reads=[stt], writes=[carry])
        em.op("dve", lambda e, _C=Cq: e.tensor_mul(_C, _C, stt[:]),
              reads=[Csb, stt], accum=[Csb])

        for hs in range(2):
            em.op("pe", lambda e, _o=ctx.P[7][0:1, :], _l=sb["onesf"][0:16, 0:1],
                  _r=Csb[:, b0 + hs * 512:b0 + (hs + 1) * 512]:
                  e.matmul(_o, _l, _r, start=True, stop=True),
                  reads=[Csb, sb["onesf"]], writes=[ctx.P[7]])
            em.op("dve", lambda e, _o=ysr[0:1, hs * 512:(hs + 1) * 512], _i=ctx.P[7][0:1, :]:
                  e.tensor_copy(_o, _i), reads=[ctx.P[7]], accum=[ysr])
        em.dma("sp", ctx.dram["RYS"][0:1, q * HT:(q + 1) * HT], ysr[0:1, :],
               reads=[ysr])

    conv_emit(0)
    for s in range(NSEG):
        if s + 1 < NSEG:
            conv_emit(s + 1)
        dtbcd_emit(s)
        if s >= 3 and s % 2 == 1 and s < NSEG - 1:
            scan_half((s - 3) // 2)
    scan_half(2)
    scan_half(3)



def ph6_gateout(ctx):
    em, nc, sb = ctx.em, ctx.nc, ctx.sb
    ECH = E // 128
    XG = ctx.dram["XG"].rearrange("(c p) t -> p c t", p=128)
    XR0 = ctx.dram["XR0"].rearrange("(c p) t -> p c t", p=128)
    XR1 = ctx.dram["XR1"].rearrange("(c p) t -> p c t", p=128)
    XS2 = ctx.dram["XS2"].rearrange("(c p) t -> p c t", p=128)
    wout = load_weight_bf16(ctx, "p6_w", ctx.inp["w_out0"], 2048, 1024)
    bcol = ctx.alloc("p6_b", [128, 8], F32)
    em.dma("sp", bcol[:], ctx.inp["b_out0"][:, :], writes=[bcol])
    xg = ctx.alloc("p6_xg", [128, ECH * 512], F32)
    xr0 = ctx.alloc("p6_xr0", [128, NCH * 512], F32)
    yg = ctx.alloc("p6_yg", [128, ECH * 512], BF16)
    res = ctx.alloc("p6_res", [128, NCH * 512], F32)
    xr1 = ctx.alloc("p6_xr1", [128, NCH * 512], F32)
    xs2 = ctx.alloc("p6_xs2", [128, NCH * 512], BF16)
    sil = [ctx.alloc(f"p6_sil{i}", [128, 512], F32) for i in range(2)]
    tmp = [ctx.alloc(f"p6_tmp{i}", [128, 512], F32) for i in range(2)]
    rows = ctx.alloc("p6_rows", [1, 3 * 512], F32)
    ybc = ctx.alloc("p6_ybc", [128, 512], F32)
    a0bc = ctx.alloc("p6_a0bc", [128, 512], F32)
    nm0bc = ctx.alloc("p6_nm0bc", [128, 512], F32)
    a1bc = ctx.alloc("p6_a1bc", [128, 512], F32)
    nm1bc = ctx.alloc("p6_nm1bc", [128, 512], F32)
    ysrow = ctx.alloc("p6_ysrow", [1, 512], F32)
    a0row = ctx.alloc("p6_a0row", [1, 512], F32)
    nm0row = ctx.alloc("p6_nm0row", [1, 512], F32)
    for s in range(NSEG):
        sl = slice(s * 512, (s + 1) * 512)
        em.dma("sp", xg[:].rearrange("p (c t) -> p c t", c=ECH), XG[:, :, sl], writes=[xg])
        em.dma("sp", xr0[:].rearrange("p (c t) -> p c t", c=NCH), XR0[:, :, sl], writes=[xr0])
        em.dma("sp", ysrow[0:1, :], ctx.dram["RYS"][0:1, sl], writes=[ysrow])
        em.dma("sp", a0row[0:1, :], ctx.dram["RA0"][0:1, sl], writes=[a0row])
        em.dma("sp", nm0row[0:1, :], ctx.dram["RNM0"][0:1, sl], writes=[nm0row])
        bcast_row(ctx, ysrow[0:1, :], ybc, 5)
        bcast_row(ctx, a0row[0:1, :], a0bc, 4)
        bcast_row(ctx, nm0row[0:1, :], nm0bc, 5)
        for c in range(ECH):
            si = sil[c % 2]
            em.op("act", lambda e, _o=si[:], _i=xg[:, c * 512:(c + 1) * 512]:
                  e.activation(_o, _i, AF.Silu), reads=[xg], writes=[si])
            em.op("dve", lambda e, _o=yg[:, c * 512:(c + 1) * 512], _a=si[:], _b=ybc[:]:
                  e.tensor_mul(_o, _a, _b), reads=[si, ybc], accum=[yg])
        for m in range(NCH):
            ps = ctx.P[m % 2]
            for k in range(ECH):
                em.op("pe", lambda e, _o=ps[:, :], _l=lhsT(wout, 2048, 1024, k, m),
                      _r=yg[:, k * 512:(k + 1) * 512], _st=(k == 0), _sp=(k == ECH - 1):
                      e.matmul(_o, _l, _r, start=_st, stop=_sp),
                      reads=[wout, yg], accum=[ps])
            t = tmp[m % 2]
            em.op("dve", lambda e, _o=t[:], _a=xr0[:, m * 512:(m + 1) * 512], _b=nm0bc[:]:
                  e.tensor_add(_o, _a, _b), reads=[xr0, nm0bc], writes=[t])
            em.op("dve", lambda e, _o=t[:], _a=t[:], _b=a0bc[:]:
                  e.tensor_mul(_o, _a, _b), reads=[t, a0bc], writes=[t])
            em.op("dve", lambda e, _o=res[:, m * 512:(m + 1) * 512], _i=ps[:, :],
                  _b=bcol[:, m:m + 1], _t=t[:]:
                  e.scalar_tensor_tensor(_o, _i, _b, _t, ALU.add, ALU.add),
                  reads=[ps, bcol, t], accum=[res])
        arow = rows[0:1, 0:512]; nmrow = rows[0:1, 512:1024]
        src = [res[:, c * 512:(c + 1) * 512] for c in range(NCH)]
        ln_stats_seg(ctx, s, src, NCH, 1.0 / D, arow, nmrow)
        bcast_row(ctx, arow, a1bc, 4)
        bcast_row(ctx, nmrow, nm1bc, 5)
        for c in range(NCH):
            t = tmp[c % 2]
            em.op("dve", lambda e, _o=t[:], _a=res[:, c * 512:(c + 1) * 512], _b=nm1bc[:]:
                  e.tensor_add(_o, _a, _b), reads=[res, nm1bc], writes=[t])
            em.op("dve", lambda e, _o=t[:], _a=t[:], _b=a1bc[:]:
                  e.tensor_mul(_o, _a, _b), reads=[t, a1bc], writes=[t])
            em.op("dve", lambda e, _o=xr1[:, c * 512:(c + 1) * 512], _a=t[:],
                  _b=xr0[:, c * 512:(c + 1) * 512]:
                  e.tensor_add(_o, _a, _b), reads=[t, xr0], accum=[xr1])
        em.dma("sp", XR1[:, :, sl], xr1[:].rearrange("p (c t) -> p c t", c=NCH),
               reads=[xr1])
        src = [xr1[:, c * 512:(c + 1) * 512] for c in range(NCH)]
        ln_stats_seg(ctx, s, src, NCH, 1.0 / D, arow, nmrow)
        bcast_row(ctx, arow, a1bc, 4)
        bcast_row(ctx, nmrow, nm1bc, 5)
        for c in range(NCH):
            t = tmp[c % 2]
            em.op("dve", lambda e, _o=t[:], _a=xr1[:, c * 512:(c + 1) * 512], _b=nm1bc[:]:
                  e.tensor_add(_o, _a, _b), reads=[xr1, nm1bc], writes=[t])
            em.op("dve", lambda e, _o=xs2[:, c * 512:(c + 1) * 512], _a=t[:], _b=a1bc[:]:
                  e.tensor_mul(_o, _a, _b), reads=[t, a1bc], accum=[xs2])
        em.dma("sp", XS2[:, :, sl], xs2[:].rearrange("p (c t) -> p c t", c=NCH),
               reads=[xs2])



def ph7_dense(ctx):
    em, nc, sb = ctx.em, ctx.nc, ctx.sb
    XS2 = ctx.dram["XS2"].rearrange("(c p) t -> p c t", p=128)
    YS = ctx.dram["YS"]
    ws1 = load_weight_bf16(ctx, "p7_ws1", ctx.inp["ws1"], 1024, 4096)
    ws2 = load_weight_bf16(ctx, "p7_ws2", ctx.inp["ws2"], 2048, 1024)
    wr = load_weight_bf16(ctx, "p7_wr", ctx.inp["wr"], 1024, 12)
    bs1 = ctx.alloc("p7_bs1", [128, 32], F32)
    em.dma("sp", bs1[:], ctx.inp["bs1"][:, :], writes=[bs1])
    bs2 = ctx.alloc("p7_bs2", [128, 8], F32)
    em.dma("sp", bs2[:], ctx.inp["bs2"][:, :], writes=[bs2])
    brrow = ctx.alloc("p7_br", [1, 12], F32)
    em.dma("sp", brrow[:], ctx.inp["br"][:, :], writes=[brrow])
    brbc = ctx.alloc("p7_brbc", [128, 12], F32)
    bcast_row(ctx, brrow[0:1, :], brbc, 6, width=12)
    xs = [ctx.alloc(f"p7_xs{i}", [128, NCH * 512], BF16) for i in range(2)]
    hh = ctx.alloc("p7_hh", [128, 16 * 512], BF16)
    gt = [ctx.alloc(f"p7_gt{i}", [128, 512], F32) for i in range(2)]
    at = [ctx.alloc(f"p7_at{i}", [128, 512], F32) for i in range(2)]
    stg = [ctx.alloc(f"p7_st{i}", [128, 512], F32) for i in range(3)]
    wsc = ctx.alloc("p7_wsc", [12, 512], F32)
    rt_ = ctx.alloc("p7_rt", [128, 12 * 4], F32)
    rc = ctx.alloc("p7_rc", [128, 8], F32)
    for s in range(NSEG):
        sl = slice(s * 512, (s + 1) * 512)
        xsb = xs[s % 2]
        em.dma("sp", xsb[:].rearrange("p (c t) -> p c t", c=NCH), XS2[:, :, sl], writes=[xsb])
        for m in range(16):
            psa, psg = ctx.P[(m % 2) * 2], ctx.P[(m % 2) * 2 + 1]
            for k in range(NCH):
                em.op("pe", lambda e, _o=psa[:, :], _l=lhsT(ws1, 1024, 4096, k, m),
                      _r=xsb[:, k * 512:(k + 1) * 512], _st=(k == 0), _sp=(k == NCH - 1):
                      e.matmul(_o, _l, _r, start=_st, stop=_sp),
                      reads=[ws1, xsb], accum=[psa])
                em.op("pe", lambda e, _o=psg[:, :], _l=lhsT(ws1, 1024, 4096, k, m + 16),
                      _r=xsb[:, k * 512:(k + 1) * 512], _st=(k == 0), _sp=(k == NCH - 1):
                      e.matmul(_o, _l, _r, start=_st, stop=_sp),
                      reads=[ws1, xsb], accum=[psg])
            g, a = gt[m % 2], at[m % 2]
            em.op("act", lambda e, _o=g[:], _i=psg[:, :], _b=bs1[:, m + 16:m + 17]:
                  e.activation(_o, _i, AF.Gelu, bias=_b), reads=[psg, bs1], writes=[g])
            em.op("dve", lambda e, _o=a[:], _i=psa[:, :], _b=bs1[:, m:m + 1]:
                  e.tensor_scalar_add(_o, _i, _b), reads=[psa, bs1], writes=[a])
            em.op("dve", lambda e, _o=hh[:, m * 512:(m + 1) * 512], _a=a[:], _b=g[:]:
                  e.tensor_mul(_o, _a, _b), reads=[a, g], accum=[hh])
        for m in range(NCH):
            ps = ctx.P[4 + m % 2]
            for k in range(16):
                em.op("pe", lambda e, _o=ps[:, :], _l=lhsT(ws2, 2048, 1024, k, m),
                      _r=hh[:, k * 512:(k + 1) * 512], _st=(k == 0), _sp=(k == 15):
                      e.matmul(_o, _l, _r, start=_st, stop=_sp),
                      reads=[ws2, hh], accum=[ps])
            st = stg[m % 3]
            em.op("dve", lambda e, _o=st[:], _i=ps[:, :], _b=bs2[:, m:m + 1]:
                  e.tensor_scalar_add(_o, _i, _b), reads=[ps, bs2], writes=[st])
            em.dma("sp", YS[m * 128:(m + 1) * 128, sl], st[:], reads=[st])
        for tc in range(4):
            ps = ctx.P[6]
            for k in range(NCH):
                em.op("pe", lambda e, _o=ps[:, 0:12],
                      _l=xsb[:, k * 512 + tc * 128: k * 512 + (tc + 1) * 128],
                      _r=wr[:, k * 12:(k + 1) * 12], _st=(k == 0), _sp=(k == NCH - 1):
                      e.matmul(_o, _l, _r, start=_st, stop=_sp),
                      reads=[xsb, wr], accum=[ps])
            L = rt_[:, 0:12]; msk = rt_[:, 12:24]; eq1 = rt_[:, 24:36]; wv = rt_[:, 36:48]
            m1 = rc[:, 0:1]; m2 = rc[:, 1:2]; dd = rc[:, 2:3]; w1 = rc[:, 3:4]; w2 = rc[:, 4:5]
            em.op("dve", lambda e, _o=L, _a=ps[:, 0:12], _b=brbc[:]:
                  e.tensor_add(_o, _a, _b), reads=[ps, brbc], writes=[rt_])
            em.op("dve", lambda e, _o=m1, _i=L: e.reduce_max(_o, _i, AX.X),
                  reads=[rt_], writes=[rc])
            em.op("dve", lambda e, _o=eq1, _i=L, _s=m1:
                  e.tensor_single_scalar(_o, _i, _s, ALU.is_ge), reads=[rt_, rc], accum=[rt_])
            em.op("dve", lambda e, _o=msk, _i=eq1, _t=L:
                  e.scalar_tensor_tensor(_o, _i, -1e30, _t, ALU.mult, ALU.add),
                  reads=[rt_], accum=[rt_])
            em.op("dve", lambda e, _o=m2, _i=msk: e.reduce_max(_o, _i, AX.X),
                  reads=[rt_], accum=[rc])
            em.op("dve", lambda e, _o=dd, _a=m1, _b=m2: e.tensor_sub(_o, _a, _b),
                  reads=[rc], accum=[rc])
            em.op("act", lambda e, _o=w1, _i=dd:
                  e.activation(_o, _i, AF.Sigmoid, scale=float(1.0 / 1.5)),
                  reads=[rc], accum=[rc])
            em.op("dve", lambda e, _o=w2, _i=w1:
                  e.tensor_scalar(_o, _i, -1.0, 1.0, ALU.mult, ALU.add),
                  reads=[rc], accum=[rc])
            em.op("dve", lambda e, _o=wv, _i=eq1, _s=w1:
                  e.tensor_single_scalar(_o, _i, _s, ALU.mult), reads=[rt_, rc], accum=[rt_])
            em.op("dve", lambda e, _o=rt_[:, 12:24], _i=msk, _s=m2:
                  e.tensor_single_scalar(_o, _i, _s, ALU.is_ge), reads=[rt_, rc], accum=[rt_])
            em.op("dve", lambda e, _o=wv, _i=rt_[:, 12:24], _s=w2, _t=wv:
                  e.scalar_tensor_tensor(_o, _i, _s, _t, ALU.mult, ALU.add),
                  reads=[rt_, rc], accum=[rt_])
            em.op("pe", lambda e, _o=ctx.P[7][0:12, 0:128], _i=wv, _id=sb["identf"][:]:
                  e.transpose(_o, _i, _id), reads=[rt_, sb["identf"]], writes=[ctx.P[7]])
            em.op("dve", lambda e, _o=wsc[:, tc * 128:(tc + 1) * 128], _i=ctx.P[7][0:12, 0:128]:
                  e.tensor_copy(_o, _i), reads=[ctx.P[7]], accum=[wsc])
        em.dma("sp", ctx.dram["WSC"][:, sl], wsc[0:12, :], reads=[wsc])



def ph8_experts(ctx):
    em, nc, sb = ctx.em, ctx.nc, ctx.sb
    BT = 1024
    NB = T // BT
    XS2 = ctx.dram["XS2"].rearrange("(c p) t -> p c t", p=128)
    XR1 = ctx.dram["XR1"].rearrange("(c p) t -> p c t", p=128)
    XR2 = ctx.dram["XR2"].rearrange("(c p) t -> p c t", p=128)
    XS4 = ctx.dram["XS4"].rearrange("(c p) t -> p c t", p=128)
    YS = ctx.dram["YS"].rearrange("(c p) t -> p c t", p=128)
    ysyr = ctx.alloc("p8_ysyr", [128, NCH * BT], F32)
    xs2b = ctx.alloc("p8_xs2", [128, NCH * BT], BF16)
    act = ctx.alloc("p8_act", [128, 24 * BT], BF16)
    we1a = [ctx.alloc(f"p8_w1a{i}", [128, 1024], BF16) for i in range(2)]
    we1g = [ctx.alloc(f"p8_w1g{i}", [128, 1024], BF16) for i in range(2)]
    we2m = [ctx.alloc(f"p8_w2{i}", [128, 3072], BF16) for i in range(2)]
    wrow = ctx.alloc("p8_wrow", [1, BT], F32)
    wrowb = ctx.alloc("p8_wrowb", [1, BT], BF16)
    be2rb = ctx.alloc("p8_be2rb", [1, 1024], BF16)
    wbc = ctx.alloc("p8_wbc", [128, BT], F32)
    be1sb = ctx.alloc("p8_be1", [128, 48], F32)
    be2r = ctx.alloc("p8_be2", [1, 1024], F32)
    gtmp = [ctx.alloc(f"p8_g{i}", [128, 512], F32) for i in range(2)]
    atmp = [ctx.alloc(f"p8_a{i}", [128, 512], F32) for i in range(2)]
    xr1c = [ctx.alloc(f"p8_x1c{i}", [128, 512], F32) for i in range(2)]
    xs4c = [ctx.alloc(f"p8_x4c{i}", [128, 512], BF16) for i in range(2)]
    xr2t = ctx.alloc("p8_xr2", [128, NCH * 512], F32)
    rows = ctx.alloc("p8_rows", [1, 2 * 512], F32)
    abc = ctx.alloc("p8_abc", [128, 512], F32)
    nmbc = ctx.alloc("p8_nmbc", [128, 512], F32)
    gabc = ctx.alloc("p8_gabc", [128, 512], F32)
    be1c = ctx.alloc("p8_be1c", [128, 8], F32)
    em.dma("sp", be1c[:], ctx.inp["be1c"][:, :], writes=[be1c])
    for b in range(NB):
        bl = slice(b * BT, (b + 1) * BT)
        em.dma("sp", ysyr[:].rearrange("p (c t) -> p c t", c=NCH), YS[:, :, bl],
               writes=[ysyr])
        em.dma("sp", xs2b[:].rearrange("p (c t) -> p c t", c=NCH), XS2[:, :, bl],
               writes=[xs2b])
        for e in range(12):
            em.dma("sp", wrow[0:1, :], ctx.dram["WSC"][e:e + 1, bl], writes=[wrow])
            em.dma("sp", be2r[0:1, :], ctx.inp["be2"][e:e + 1, :], writes=[be2r])
            em.op("dve", lambda e_: e_.tensor_copy(wrowb[0:1, :], wrow[0:1, :]),
                  reads=[wrow], writes=[wrowb])
            em.op("dve", lambda e_: e_.tensor_copy(be2rb[0:1, :], be2r[0:1, :]),
                  reads=[be2r], writes=[be2rb])
            em.dma("sp", be1sb[:], ctx.inp["be1"][e, :, :], writes=[be1sb])
            for ss in range(2):
                em.op("pe", lambda e_, _o=ctx.P[6][:, :], _l=sb["ones_rowb"][0:1, :],
                      _r=wrowb[0:1, ss * 512:(ss + 1) * 512]:
                      e_.matmul(_o, _l, _r, start=True, stop=True),
                      reads=[wrowb, sb["ones_rowb"]], writes=[ctx.P[6]])
                em.op("dve", lambda e_, _o=wbc[:, ss * 512:(ss + 1) * 512], _i=ctx.P[6][:, :]:
                      e_.tensor_copy(_o, _i), reads=[ctx.P[6]], accum=[wbc])
            for j in range(24):
                wa, wg = we1a[j % 2], we1g[j % 2]
                em.dma("pool", wa[:], ctx.inp["we1"][e, j, :, :], writes=[wa])
                em.dma("pool", wg[:], ctx.inp["we1"][e, 24 + j, :, :], writes=[wg])
                for ss in range(2):
                    psa = ctx.P[(ss % 2) * 2]
                    psg = ctx.P[(ss % 2) * 2 + 1]
                    for k in range(NCH):
                        em.op("pe", lambda e_, _o=psa[:, :], _l=wa[:, k * 128:(k + 1) * 128],
                              _r=xs2b[:, k * BT + ss * 512: k * BT + (ss + 1) * 512],
                              _st=(k == 0), _sp=(k == NCH - 1):
                              e_.matmul(_o, _l, _r, start=_st, stop=_sp),
                              reads=[wa, xs2b], accum=[psa])
                        em.op("pe", lambda e_, _o=psg[:, :], _l=wg[:, k * 128:(k + 1) * 128],
                              _r=xs2b[:, k * BT + ss * 512: k * BT + (ss + 1) * 512],
                              _st=(k == 0), _sp=(k == NCH - 1):
                              e_.matmul(_o, _l, _r, start=_st, stop=_sp),
                              reads=[wg, xs2b], accum=[psg])
                    g, a = gtmp[ss], atmp[ss]
                    em.op("act", lambda e_, _o=g[:], _i=psg[:, :], _b=be1sb[:, 24 + j:25 + j]:
                          e_.activation(_o, _i, AF.Gelu, bias=_b),
                          reads=[psg, be1sb], writes=[g])
                    em.op("dve", lambda e_, _o=a[:], _i=psa[:, :], _b=be1sb[:, j:j + 1]:
                          e_.tensor_scalar_add(_o, _i, _b), reads=[psa, be1sb], writes=[a])
                    em.op("dve", lambda e_, _o=a[:], _a=a[:], _b=g[:]:
                          e_.tensor_mul(_o, _a, _b), reads=[a, g], writes=[a])
                    em.op("dve", lambda e_, _o=act[:, j * BT + ss * 512: j * BT + (ss + 1) * 512],
                          _a=a[:], _b=wbc[:, ss * 512:(ss + 1) * 512]:
                          e_.tensor_mul(_o, _a, _b), reads=[a, wbc], accum=[act])
            for m in range(NCH):
                w2 = we2m[m % 2]
                em.dma("pool", w2[:], ctx.inp["we2"][e, m, :, :], writes=[w2])
                for ss in range(2):
                    ps = ctx.P[4 + (m * 2 + ss) % 2]
                    for k in range(24):
                        em.op("pe", lambda e_, _o=ps[:, :], _l=w2[:, k * 128:(k + 1) * 128],
                              _r=act[:, k * BT + ss * 512: k * BT + (ss + 1) * 512],
                              _st=(k == 0), _sp=False:
                              e_.matmul(_o, _l, _r, start=_st, stop=_sp),
                              reads=[w2, act], accum=[ps])
                    em.op("pe", lambda e_, _o=ps[:, :], _l=be2rb[0:1, m * 128:(m + 1) * 128],
                          _r=wrowb[0:1, ss * 512:(ss + 1) * 512]:
                          e_.matmul(_o, _l, _r, start=False, stop=True),
                          reads=[be2rb, wrowb], accum=[ps])
                    em.op("dve", lambda e_, _o=ysyr[:, m * BT + ss * 512: m * BT + (ss + 1) * 512],
                          _i=ysyr[:, m * BT + ss * 512: m * BT + (ss + 1) * 512], _p=ps[:, :]:
                          e_.tensor_add(_o, _i, _p), reads=[ysyr, ps], accum=[ysyr])
        for ss2 in range(2):
            s = b * 2 + ss2
            sl = slice(s * 512, (s + 1) * 512)
            ysl = [ysyr[:, c * BT + ss2 * 512: c * BT + (ss2 + 1) * 512] for c in range(NCH)]
            arow = rows[0:1, 0:512]; nmrow = rows[0:1, 512:1024]
            ln_stats_seg(ctx, s, ysl, NCH, 1.0 / D, arow, nmrow)
            bcast_row(ctx, arow, abc, 4)
            bcast_row(ctx, nmrow, nmbc, 5)
            for c in range(NCH):
                x1 = xr1c[c % 2]
                em.dma("sp", x1[:], ctx.dram["XR1"][c * 128:(c + 1) * 128, sl], writes=[x1])
                t = gtmp[c % 2]
                em.op("dve", lambda e_, _o=t[:], _a=ysl[c], _b=nmbc[:]:
                      e_.tensor_add(_o, _a, _b), reads=[ysyr, nmbc], writes=[t])
                em.op("dve", lambda e_, _o=t[:], _a=t[:], _b=abc[:]:
                      e_.tensor_mul(_o, _a, _b), reads=[t, abc], writes=[t])
                em.op("dve", lambda e_, _o=xr2t[:, c * 512:(c + 1) * 512], _a=t[:],
                      _b=x1[:]:
                      e_.tensor_add(_o, _a, _b), reads=[t, x1], accum=[xr2t])
            em.dma("sp", XR2[:, :, sl], xr2t[:].rearrange("p (c t) -> p c t", c=NCH),
                   reads=[xr2t])
            srcl = [xr2t[:, c * 512:(c + 1) * 512] for c in range(NCH)]
            ln_stats_seg(ctx, s, srcl, NCH, 1.0 / D, arow, nmrow)
            em.dma("sp", ctx.dram["RA4"][0:1, sl], arow, reads=[rows])
            em.dma("sp", ctx.dram["RNM4"][0:1, sl], nmrow, reads=[rows])
            bcast_row(ctx, nmrow, nmbc, 5)
            em.op("dve", lambda e_: e_.tensor_copy(ctx.sb["_bcr"][0:1, 0:512], arow),
                  reads=[rows], accum=[ctx.sb["_bcr"]])
            for c in range(NCH):
                em.op("pe", lambda e_, _o=ctx.P[4][:, :], _l=ctx.g1rb[0:1, c * 128:(c + 1) * 128],
                      _r=ctx.sb["_bcr"][0:1, 0:512]:
                      e_.matmul(_o, _l, _r, start=True, stop=True),
                      reads=[ctx.sb["_bcr"], ctx.g1rb], writes=[ctx.P[4]])
                em.op("dve", lambda e_, _o=gabc[:], _i=ctx.P[4][:, :]:
                      e_.tensor_copy(_o, _i), reads=[ctx.P[4]], writes=[gabc])
                t = gtmp[c % 2]
                em.op("dve", lambda e_, _o=t[:], _a=xr2t[:, c * 512:(c + 1) * 512], _b=nmbc[:]:
                      e_.tensor_add(_o, _a, _b), reads=[xr2t, nmbc], writes=[t])
                em.op("dve", lambda e_, _o=t[:], _a=t[:], _b=gabc[:]:
                      e_.tensor_mul(_o, _a, _b), reads=[t, gabc], writes=[t])
                x4 = xs4c[c % 2]
                em.op("dve", lambda e_, _o=x4[:], _i=t[:],
                      _b=be1c[:, c:c + 1]:
                      e_.tensor_scalar_add(_o, _i, _b), reads=[t, be1c], writes=[x4])
                em.dma("sp", ctx.dram["XS4"][c * 128:(c + 1) * 128, sl], x4[:], reads=[x4])



def ph9_qkv(ctx):
    em, nc, sb = ctx.em, ctx.nc, ctx.sb
    XS4 = ctx.dram["XS4"].rearrange("(c p) t -> p c t", p=128)
    QF = ctx.dram["QF"]
    wq = load_weight_bf16(ctx, "p9_w", ctx.inp["qkv_w"], 1024, 3072)
    bq = ctx.alloc("p9_bq", [128, 8], F32)
    em.dma("sp", bq[:], ctx.inp["qkv_bq"][:, :], writes=[bq])
    bkv = ctx.alloc("p9_bkv", [1, 2048], F32)
    em.dma("sp", bkv[:], ctx.inp["qkv_bkv"][:, :], writes=[bkv])
    bkvb = ctx.alloc("p9_bkvb", [1, 2048], BF16)
    em.op("dve", lambda e: e.tensor_copy(bkvb[:], bkv[:]), reads=[bkv], writes=[bkvb])
    xs = [ctx.alloc(f"p9_xs{i}", [128, NCH * 512], BF16) for i in range(2)]
    t1 = [ctx.alloc(f"p9_t1{i}", [128, 512], F32) for i in range(2)]
    t2 = [ctx.alloc(f"p9_t2{i}", [128, 512], F32) for i in range(2)]
    t3 = [ctx.alloc(f"p9_t3{i}", [128, 512], F32) for i in range(2)]
    ob = [ctx.alloc(f"p9_ob{i}", [128, 512], BF16) for i in range(4)]
    kft = [ctx.alloc(f"p9_kf{i}", [128, 1024], BF16) for i in range(2)]
    vt = [ctx.alloc(f"p9_v{i}", [128, 1024], BF16) for i in range(2)]
    kvacc = ctx.alloc("p9_kvacc", [128, 8 * 64], F32)
    kvall = ctx.alloc("p9_kvall", [128, 8 * 64], BF16)
    zacc = ctx.alloc("p9_z", [1, 1024], F32)
    zc = ctx.alloc("p9_zc", [128, 8], F32)
    em.op("dve", lambda e: e.memset(kvacc[:], 0.0), writes=[kvacc])
    em.op("dve", lambda e: e.memset(zacc[:], 0.0), writes=[zacc])

    def elup1(ps, bias_fn, out_bf, i, extra_reads, out_res):
        a, n, p_ = t1[i % 2], t2[i % 2], t3[i % 2]
        em.op("dve", bias_fn(a, ps), reads=[ps] + extra_reads, writes=[a])
        em.op("dve", lambda e, _o=n[:], _i=a[:]: e.tensor_scalar_min(_o, _i, 0.0),
              reads=[a], writes=[n])
        em.op("act", lambda e, _o=n[:], _i=n[:]: e.activation(_o, _i, AF.Exp),
              reads=[n], writes=[n])
        em.op("dve", lambda e, _o=p_[:], _i=a[:]: e.tensor_scalar_max(_o, _i, 0.0),
              reads=[a], writes=[p_])
        em.op("dve", lambda e, _o=out_bf, _a=n[:], _b=p_[:]:
              e.tensor_add(_o, _a, _b), reads=[n, p_], accum=[out_res])

    for s in range(NSEG):
        sl = slice(s * 512, (s + 1) * 512)
        xsb = xs[s % 2]
        em.dma("sp", xsb[:].rearrange("p (c t) -> p c t", c=NCH), XS4[:, :, sl], writes=[xsb])
        for m in range(NCH):
            ps = ctx.P[m % 2]
            for k in range(NCH):
                em.op("pe", lambda e, _o=ps[:, :], _l=lhsT(wq, 1024, 3072, k, m),
                      _r=xsb[:, k * 512:(k + 1) * 512], _st=(k == 0), _sp=(k == NCH - 1):
                      e.matmul(_o, _l, _r, start=_st, stop=_sp),
                      reads=[wq, xsb], accum=[ps])
            o = ob[m % 4]
            elup1(ps, lambda a, p: (lambda e, _o=a[:], _i=p[:, :], _b=bq[:, m:m + 1]:
                                    e.tensor_scalar_add(_o, _i, _b)), o[:], m, [bq], o)
            em.dma("sp", QF[m * 128:(m + 1) * 128, sl], o[:], reads=[o])
        for tc in range(4):
            kf = kft[tc % 2]
            v = vt[tc % 2]
            for kv_i, off in ((0, 1024), (1, 2048)):
                for rs in range(2):
                    ps = ctx.P[2 + (tc * 4 + kv_i * 2 + rs) % 4]
                    for k in range(NCH):
                        em.op("pe", lambda e, _o=ps[:, :],
                              _l=xsb[:, k * 512 + tc * 128: k * 512 + (tc + 1) * 128],
                              _r=wq[:, k * 3072 + off + rs * 512: k * 3072 + off + (rs + 1) * 512],
                              _st=(k == 0), _sp=False:
                              e.matmul(_o, _l, _r, start=_st, stop=_sp),
                              reads=[wq, xsb], accum=[ps])
                    em.op("pe", lambda e, _o=ps[:, :], _l=sb["ones_rowb"][0:1, :],
                          _r=bkvb[0:1, (off - 1024) + rs * 512:(off - 1024) + (rs + 1) * 512]:
                          e.matmul(_o, _l, _r, start=False, stop=True),
                          reads=[bkvb, sb["ones_rowb"]], accum=[ps])
                    if kv_i == 0:
                        elup1(ps, lambda a, p: (lambda e, _o=a[:], _i=p[:, :]:
                                                e.tensor_copy(_o, _i)),
                              kf[:, rs * 512:(rs + 1) * 512], tc * 2 + rs, [], kf)
                    else:
                        em.op("dve", lambda e, _o=v[:, rs * 512:(rs + 1) * 512], _i=ps[:, :]:
                              e.tensor_copy(_o, _i), reads=[ps], accum=[v])
            for h in range(16):
                rb0 = (h % 2) * 64
                ps = ctx.P[6 + (h // 2) % 2]
                em.op("pe", lambda e, _o=ps[rb0:rb0 + 64, 0:64],
                      _l=kf[:, h * 64:(h + 1) * 64], _r=v[:, h * 64:(h + 1) * 64]:
                      e.matmul(_o, _l, _r, start=True, stop=True),
                      reads=[kf, v], accum=[ps])
                em.op("dve", lambda e, _o=kvacc[rb0:rb0 + 64, (h // 2) * 64:(h // 2 + 1) * 64],
                      _i=ps[rb0:rb0 + 64, 0:64],
                      _a=kvacc[rb0:rb0 + 64, (h // 2) * 64:(h // 2 + 1) * 64]:
                      e.tensor_add(_o, _a, _i), reads=[ps, kvacc], accum=[kvacc])
            for rs in range(2):
                em.op("pe", lambda e, _o=ctx.P[6][0:1, 0:512], _l=sb["onesb"][:],
                      _r=kf[:, rs * 512:(rs + 1) * 512]:
                      e.matmul(_o, _l, _r, start=True, stop=True),
                      reads=[kf, sb["onesb"]], writes=[ctx.P[6]])
                em.op("dve", lambda e, _o=zacc[0:1, rs * 512:(rs + 1) * 512],
                      _i=zacc[0:1, rs * 512:(rs + 1) * 512], _p=ctx.P[6][0:1, 0:512]:
                      e.tensor_add(_o, _i, _p), reads=[zacc, ctx.P[6]], writes=[zacc])
    em.op("dve", lambda e: e.tensor_copy(kvall[:], kvacc[:]), reads=[kvacc], writes=[kvall])
    em.dma("sp", ctx.dram["KVD"][:, :], kvall[:], reads=[kvall])
    for c in range(8):
        em.op("pe", lambda e, _o=ctx.P[1][:, 0:1], _i=zacc[0:1, c * 128:(c + 1) * 128],
              _id=sb["identf"][0:1, 0:1]:
              e.transpose(_o, _i, _id), reads=[zacc, sb["identf"]], writes=[ctx.P[1]])
        em.op("dve", lambda e, _o=zc[:, c:c + 1], _i=ctx.P[1][:, 0:1]:
              e.tensor_copy(_o, _i), reads=[ctx.P[1]], accum=[zc])
    em.dma("sp", ctx.dram["ZCD"][:, :], zc[:], reads=[zc])



def ph11_attnout(ctx):
    em, nc, sb = ctx.em, ctx.nc, ctx.sb
    QF = ctx.dram["QF"].rearrange("(c p) t -> p c t", p=128)
    XR2 = ctx.dram["XR2"].rearrange("(c p) t -> p c t", p=128)
    X3 = ctx.dram["X3"].rearrange("(c p) t -> p c t", p=128)
    XS5 = ctx.dram["XS5"]
    wout = load_weight_bf16(ctx, "p11_w", ctx.inp["w_out1"], 1024, 1024)
    kvall = ctx.alloc("p11_kv", [128, 8 * 64], BF16)
    em.dma("sp", kvall[:], ctx.dram["KVD"][:, :], writes=[kvall])
    zc = ctx.alloc("p11_zc", [128, 8], F32)
    em.dma("sp", zc[:], ctx.dram["ZCD"][:, :], writes=[zc])
    bo = ctx.alloc("p11_bo", [128, 8], F32)
    em.dma("sp", bo[:], ctx.inp["b_out1"][:, :], writes=[bo])
    be2c = ctx.alloc("p11_be2c", [128, 8], F32)
    em.dma("sp", be2c[:], ctx.inp["be2c"][:, :], writes=[be2c])
    qf = [ctx.alloc(f"p11_qf{i}", [128, NCH * 512], BF16) for i in range(2)]
    xr2 = ctx.alloc("p11_xr2", [128, NCH * 512], F32)
    ybf = ctx.alloc("p11_y", [128, NCH * 512], BF16)
    x3t = ctx.alloc("p11_x3", [128, NCH * 512], F32)
    dent = ctx.alloc("p11_dent", [128, 512], BF16)
    rrow = ctx.alloc("p11_rr", [1, 512], F32)
    rrowb = ctx.alloc("p11_rrb", [1, 512], BF16)
    rb = ctx.alloc("p11_rb", [128, 512], F32)
    rows = ctx.alloc("p11_rows", [1, 2 * 512], F32)
    a4r = ctx.alloc("p11_a4r", [1, 512], F32)
    nm4r = ctx.alloc("p11_nm4r", [1, 512], F32)
    abc = ctx.alloc("p11_abc", [128, 512], F32)
    nmbc = ctx.alloc("p11_nmbc", [128, 512], F32)
    gabc = ctx.alloc("p11_gabc", [128, 512], F32)
    tt = [ctx.alloc(f"p11_t{i}", [128, 512], F32) for i in range(2)]
    xs5c = [ctx.alloc(f"p11_x5{i}", [128, 512], BF16) for i in range(2)]
    for s in range(NSEG):
        sl = slice(s * 512, (s + 1) * 512)
        qfb = qf[s % 2]
        em.dma("sp", qfb[:].rearrange("p (c t) -> p c t", c=NCH), QF[:, :, sl], writes=[qfb])
        em.dma("sp", xr2[:].rearrange("p (c t) -> p c t", c=NCH), XR2[:, :, sl], writes=[xr2])
        em.dma("sp", a4r[0:1, :], ctx.dram["RA4"][0:1, sl], writes=[a4r])
        em.dma("sp", nm4r[0:1, :], ctx.dram["RNM4"][0:1, sl], writes=[nm4r])
        for h in range(16):
            c = h // 2
            rb0 = (h % 2) * 64
            psY = ctx.P[c % 2]
            qslab = qfb[rb0:rb0 + 64, c * 512:(c + 1) * 512]
            em.op("pe", lambda e, _o=psY[rb0:rb0 + 64, :],
                  _l=kvall[rb0:rb0 + 64, (h // 2) * 64:(h // 2 + 1) * 64],
                  _r=qslab:
                  e.matmul(_o, _l, _r, start=True, stop=True),
                  reads=[kvall, qfb], accum=[psY])
            em.op("dve", lambda e, _o=dent[rb0:rb0 + 64, :], _i=qslab,
                  _zc=zc[rb0:rb0 + 64, c:c + 1]:
                  e.tensor_scalar_mul(_o, _i, _zc), reads=[qfb, zc], writes=[dent])
            em.op("pe", lambda e, _o=ctx.P[4][0:1, :], _l=sb["onesb"][rb0:rb0 + 64, 0:1],
                  _r=dent[rb0:rb0 + 64, :]:
                  e.matmul(_o, _l, _r, start=True, stop=True),
                  reads=[dent, sb["onesb"]], writes=[ctx.P[4]])
            em.op("dve", lambda e, _o=rrow[0:1, :], _i=ctx.P[4][0:1, :]:
                  e.tensor_scalar_max(_o, _i, 1e-6), reads=[ctx.P[4]], writes=[rrow])
            em.op("act", lambda e, _o=rrow[0:1, :], _i=rrow[0:1, :]:
                  e.activation(_o, _i, AF.Ln), reads=[rrow], writes=[rrow])
            em.op("act", lambda e, _o=rrow[0:1, :], _i=rrow[0:1, :]:
                  e.activation(_o, _i, AF.Exp, scale=-1.0), reads=[rrow], writes=[rrow])
            em.op("dve", lambda e: e.tensor_copy(rrowb[0:1, :], rrow[0:1, :]),
                  reads=[rrow], writes=[rrowb])
            em.op("pe", lambda e, _o=ctx.P[5][rb0:rb0 + 64, :], _l=sb["ones_rowb"][0:1, 0:64],
                  _r=rrowb[0:1, :]:
                  e.matmul(_o, _l, _r, start=True, stop=True),
                  reads=[rrowb, sb["ones_rowb"]], writes=[ctx.P[5]])
            em.op("dve", lambda e, _o=rb[rb0:rb0 + 64, :], _i=ctx.P[5][rb0:rb0 + 64, :]:
                  e.tensor_copy(_o, _i), reads=[ctx.P[5]], writes=[rb])
            em.op("dve", lambda e, _o=ybf[rb0:rb0 + 64, c * 512:(c + 1) * 512],
                  _a=psY[rb0:rb0 + 64, :], _b=rb[rb0:rb0 + 64, :]:
                  e.tensor_mul(_o, _a, _b), reads=[psY, rb], accum=[ybf])
        arow = rows[0:1, 0:512]; nmrow = rows[0:1, 512:1024]
        bcast_row(ctx, a4r[0:1, :], abc, 4)
        bcast_row(ctx, nm4r[0:1, :], nmbc, 5)
        for m in range(NCH):
            ps = ctx.P[2 + m % 2]
            for k in range(NCH):
                em.op("pe", lambda e, _o=ps[:, :], _l=lhsT(wout, 1024, 1024, k, m),
                      _r=ybf[:, k * 512:(k + 1) * 512], _st=(k == 0), _sp=(k == NCH - 1):
                      e.matmul(_o, _l, _r, start=_st, stop=_sp),
                      reads=[wout, ybf], accum=[ps])
            t = tt[m % 2]
            em.op("dve", lambda e, _o=t[:], _a=xr2[:, m * 512:(m + 1) * 512], _b=nmbc[:]:
                  e.tensor_add(_o, _a, _b), reads=[xr2, nmbc], writes=[t])
            em.op("dve", lambda e, _o=t[:], _a=t[:], _b=abc[:]:
                  e.tensor_mul(_o, _a, _b), reads=[t, abc], writes=[t])
            em.op("dve", lambda e, _o=x3t[:, m * 512:(m + 1) * 512], _i=ps[:, :],
                  _b=bo[:, m:m + 1], _t=t[:]:
                  e.scalar_tensor_tensor(_o, _i, _b, _t, ALU.add, ALU.add),
                  reads=[ps, bo, t], accum=[x3t])
        em.dma("sp", X3[:, :, sl], x3t[:].rearrange("p (c t) -> p c t", c=NCH),
               reads=[x3t])
        srcl = [x3t[:, c * 512:(c + 1) * 512] for c in range(NCH)]
        ln_stats_seg(ctx, s, srcl, NCH, 1.0 / D, arow, nmrow)
        bcast_row(ctx, nmrow, nmbc, 5)
        em.op("dve", lambda e: e.tensor_copy(ctx.sb["_bcr"][0:1, 0:512], arow),
              reads=[rows], accum=[ctx.sb["_bcr"]])
        for c in range(NCH):
            em.op("pe", lambda e, _o=ctx.P[4][:, :], _l=ctx.g2rb[0:1, c * 128:(c + 1) * 128],
                  _r=ctx.sb["_bcr"][0:1, 0:512]:
                  e.matmul(_o, _l, _r, start=True, stop=True),
                  reads=[ctx.sb["_bcr"], ctx.g2rb], writes=[ctx.P[4]])
            em.op("dve", lambda e, _o=gabc[:], _i=ctx.P[4][:, :]:
                  e.tensor_copy(_o, _i), reads=[ctx.P[4]], writes=[gabc])
            t = tt[c % 2]
            em.op("dve", lambda e, _o=t[:], _a=x3t[:, c * 512:(c + 1) * 512], _b=nmbc[:]:
                  e.tensor_add(_o, _a, _b), reads=[x3t, nmbc], writes=[t])
            em.op("dve", lambda e, _o=t[:], _a=t[:], _b=gabc[:]:
                  e.tensor_mul(_o, _a, _b), reads=[t, gabc], writes=[t])
            x5 = xs5c[c % 2]
            em.op("dve", lambda e, _o=x5[:], _i=t[:], _b=be2c[:, c:c + 1]:
                  e.tensor_scalar_add(_o, _i, _b), reads=[t, be2c], writes=[x5])
            em.dma("sp", XS5[c * 128:(c + 1) * 128, sl], x5[:], reads=[x5])



def ph12a_mlp1(ctx):
    em, nc = ctx.em, ctx.nc
    XS5 = ctx.dram["XS5"].rearrange("(c p) t -> p c t", p=128)
    HH = ctx.dram["HH"]
    w1 = load_weight_bf16(ctx, "p12_w1", ctx.inp["w1"], 1024, 4096)
    b1 = ctx.alloc("p12_b1", [128, 32], F32)
    em.dma("sp", b1[:], ctx.inp["b1"][:, :], writes=[b1])
    xs = [ctx.alloc(f"p12_xs{i}", [128, NCH * 512], BF16) for i in range(2)]
    stg = [ctx.alloc(f"p12_st{i}", [128, 512], BF16) for i in range(4)]
    for s in range(NSEG):
        sl = slice(s * 512, (s + 1) * 512)
        xsb = xs[s % 2]
        em.dma("sp", xsb[:].rearrange("p (c t) -> p c t", c=NCH), XS5[:, :, sl], writes=[xsb])
        for m in range(32):
            ps = ctx.P[m % 4]
            for k in range(NCH):
                em.op("pe", lambda e, _o=ps[:, :], _l=lhsT(w1, 1024, 4096, k, m),
                      _r=xsb[:, k * 512:(k + 1) * 512], _st=(k == 0), _sp=(k == NCH - 1):
                      e.matmul(_o, _l, _r, start=_st, stop=_sp),
                      reads=[w1, xsb], accum=[ps])
            st = stg[m % 4]
            em.op("act", lambda e, _o=st[:], _i=ps[:, :], _b=b1[:, m:m + 1]:
                  e.activation(_o, _i, AF.Gelu, bias=_b), reads=[ps, b1], writes=[st])
            em.dma("sp", HH[m * 128:(m + 1) * 128, sl], st[:], reads=[st])



def ph12b_mlp2(ctx):
    em, nc, sb = ctx.em, ctx.nc, ctx.sb
    HH = ctx.dram["HH"].rearrange("(c p) t -> p c t", p=128)
    XR3 = ctx.dram["XR3"].rearrange("(c p) t -> p c t", p=128)
    XS7 = ctx.dram["XS7"]
    w2 = load_weight_bf16(ctx, "p12_w2", ctx.inp["w2"], 4096, 1024)
    b2 = ctx.alloc("p12_b2", [128, 8], F32)
    em.dma("sp", b2[:], ctx.inp["b2"][:, :], writes=[b2])
    hh = ctx.alloc("p12_hh", [128, 32 * 512], BF16)
    x3c = [ctx.alloc(f"p12_x3{i}", [128, 512], F32) for i in range(2)]
    xr2c = [ctx.alloc(f"p12_xr2{i}", [128, 512], F32) for i in range(2)]
    o1t = ctx.alloc("p12_o1", [128, NCH * 512], F32)
    xr3t = ctx.alloc("p12_xr3", [128, NCH * 512], F32)
    xs7c = [ctx.alloc(f"p12_x7{i}", [128, 512], BF16) for i in range(2)]
    tt = [ctx.alloc(f"p12_t{i}", [128, 512], F32) for i in range(2)]
    rows = ctx.alloc("p12_rows", [1, 2 * 512], F32)
    abc = ctx.alloc("p12_abc", [128, 512], F32)
    nmbc = ctx.alloc("p12_nmbc", [128, 512], F32)
    for s in range(NSEG):
        sl = slice(s * 512, (s + 1) * 512)
        em.dma("sp", hh[:].rearrange("p (c t) -> p c t", c=32), HH[:, :, sl], writes=[hh])
        for m in range(NCH):
            ps = ctx.P[m % 2]
            for k in range(32):
                em.op("pe", lambda e, _o=ps[:, :], _l=lhsT(w2, 4096, 1024, k, m),
                      _r=hh[:, k * 512:(k + 1) * 512], _st=(k == 0), _sp=(k == 31):
                      e.matmul(_o, _l, _r, start=_st, stop=_sp),
                      reads=[w2, hh], accum=[ps])
            x3 = x3c[m % 2]
            em.dma("sp", x3[:], ctx.dram["X3"][m * 128:(m + 1) * 128, sl], writes=[x3])
            em.op("dve", lambda e, _o=o1t[:, m * 512:(m + 1) * 512], _i=ps[:, :],
                  _b=b2[:, m:m + 1], _t=x3[:]:
                  e.scalar_tensor_tensor(_o, _i, _b, _t, ALU.add, ALU.add),
                  reads=[ps, b2, x3], accum=[o1t])
        arow = rows[0:1, 0:512]; nmrow = rows[0:1, 512:1024]
        srcl = [o1t[:, c * 512:(c + 1) * 512] for c in range(NCH)]
        ln_stats_seg(ctx, s, srcl, NCH, 1.0 / D, arow, nmrow)
        bcast_row(ctx, arow, abc, 4)
        bcast_row(ctx, nmrow, nmbc, 5)
        for c in range(NCH):
            x2 = xr2c[c % 2]
            em.dma("sp", x2[:], ctx.dram["XR2"][c * 128:(c + 1) * 128, sl], writes=[x2])
            t = tt[c % 2]
            em.op("dve", lambda e, _o=t[:], _a=o1t[:, c * 512:(c + 1) * 512], _b=nmbc[:]:
                  e.tensor_add(_o, _a, _b), reads=[o1t, nmbc], writes=[t])
            em.op("dve", lambda e, _o=t[:], _a=t[:], _b=abc[:]:
                  e.tensor_mul(_o, _a, _b), reads=[t, abc], writes=[t])
            em.op("dve", lambda e, _o=xr3t[:, c * 512:(c + 1) * 512], _a=t[:], _b=x2[:]:
                  e.tensor_add(_o, _a, _b), reads=[t, x2], accum=[xr3t])
        em.dma("sp", XR3[:, :, sl], xr3t[:].rearrange("p (c t) -> p c t", c=NCH),
               reads=[xr3t])
        srcl = [xr3t[:, c * 512:(c + 1) * 512] for c in range(NCH)]
        ln_stats_seg(ctx, s, srcl, NCH, 1.0 / D, arow, nmrow)
        bcast_row(ctx, arow, abc, 4)
        bcast_row(ctx, nmrow, nmbc, 5)
        for c in range(NCH):
            t = tt[c % 2]
            em.op("dve", lambda e, _o=t[:], _a=xr3t[:, c * 512:(c + 1) * 512], _b=nmbc[:]:
                  e.tensor_add(_o, _a, _b), reads=[xr3t, nmbc], writes=[t])
            x7 = xs7c[c % 2]
            em.op("dve", lambda e, _o=x7[:], _a=t[:], _b=abc[:]:
                  e.tensor_mul(_o, _a, _b), reads=[t, abc], writes=[x7])
            em.dma("sp", XS7[c * 128:(c + 1) * 128, sl], x7[:], reads=[x7])



def load_weight_int(ctx, name, dram_ap, K, M):
    em = ctx.em
    w = ctx.alloc(name, [128, (K // 128) * M], BF16)
    wi = [ctx.alloc(name + f"_i{j}", [128, M], I32) for j in range(2)]
    for k in range(K // 128):
        t = wi[k % 2]
        em.dma("sp", t[:], dram_ap[k * 128:(k + 1) * 128, :], writes=[t])
        em.op("act", lambda e, _o=w[:, k * M:(k + 1) * M], _i=t[:]:
              e.activation(_o, _i, AF.Copy, scale=float(1.0 / 127.0)),
              reads=[t], accum=[w])
    return w


def ph13_gateup(ctx):
    em, nc = ctx.em, ctx.nc
    XS7 = ctx.dram["XS7"].rearrange("(c p) t -> p c t", p=128)
    HU = ctx.dram["HU"]
    wg = load_weight_int(ctx, "p13_wg", ctx.inp["wg"], 1024, 3072)
    wu = load_weight_int(ctx, "p13_wu", ctx.inp["wu"], 1024, 3072)
    xs = [ctx.alloc(f"p13_xs{i}", [128, NCH * 512], BF16) for i in range(2)]
    sg = [ctx.alloc(f"p13_sg{i}", [128, 512], F32) for i in range(2)]
    stg = [ctx.alloc(f"p13_st{i}", [128, 512], BF16) for i in range(4)]
    for s in range(NSEG):
        sl = slice(s * 512, (s + 1) * 512)
        xsb = xs[s % 2]
        em.dma("sp", xsb[:].rearrange("p (c t) -> p c t", c=NCH), XS7[:, :, sl], writes=[xsb])
        for m in range(24):
            psg, psu = ctx.P[(m % 2) * 2], ctx.P[(m % 2) * 2 + 1]
            for k in range(NCH):
                em.op("pe", lambda e, _o=psg[:, :], _l=lhsT(wg, 1024, 3072, k, m),
                      _r=xsb[:, k * 512:(k + 1) * 512], _st=(k == 0), _sp=(k == NCH - 1):
                      e.matmul(_o, _l, _r, start=_st, stop=_sp),
                      reads=[wg, xsb], accum=[psg])
                em.op("pe", lambda e, _o=psu[:, :], _l=lhsT(wu, 1024, 3072, k, m),
                      _r=xsb[:, k * 512:(k + 1) * 512], _st=(k == 0), _sp=(k == NCH - 1):
                      e.matmul(_o, _l, _r, start=_st, stop=_sp),
                      reads=[wu, xsb], accum=[psu])
            g = sg[m % 2]
            em.op("act", lambda e, _o=g[:], _i=psg[:, :]:
                  e.activation(_o, _i, AF.Silu), reads=[psg], writes=[g])
            st = stg[m % 4]
            em.op("dve", lambda e, _o=st[:], _a=g[:], _b=psu[:, :]:
                  e.tensor_mul(_o, _a, _b), reads=[g, psu], writes=[st])
            em.dma("sp", HU[m * 128:(m + 1) * 128, sl], st[:], reads=[st])



def ph14_down(ctx):
    em, nc, sb = ctx.em, ctx.nc, ctx.sb
    HU = ctx.dram["HU"].rearrange("(c p) t -> p c t", p=128)
    wd = load_weight_int(ctx, "p14_wd", ctx.inp["wd"], 3072, 1024)
    hu = [ctx.alloc(f"p14_hu{i}", [128, 24 * 512], BF16) for i in range(2)]
    o2t = ctx.alloc("p14_o2", [128, NCH * 512], F32)
    xr3c = [ctx.alloc(f"p14_x3{i}", [128, 512], F32) for i in range(2)]
    xr4t = ctx.alloc("p14_xr4", [128, NCH * 512], F32)
    ot = [ctx.alloc(f"p14_ot{i}", [128, 1024], F32) for i in range(2)]
    tt = [ctx.alloc(f"p14_t{i}", [128, 512], F32) for i in range(2)]
    rows = ctx.alloc("p14_rows", [1, 2 * 512], F32)
    abc = ctx.alloc("p14_abc", [128, 512], F32)
    nmbc = ctx.alloc("p14_nmbc", [128, 512], F32)
    for s in range(NSEG):
        sl = slice(s * 512, (s + 1) * 512)
        hub = hu[s % 2]
        em.dma("sp", hub[:].rearrange("p (c t) -> p c t", c=24), HU[:, :, sl], writes=[hub])
        for m in range(NCH):
            ps = ctx.P[m % 2]
            for k in range(24):
                em.op("pe", lambda e, _o=ps[:, :], _l=lhsT(wd, 3072, 1024, k, m),
                      _r=hub[:, k * 512:(k + 1) * 512], _st=(k == 0), _sp=(k == 23):
                      e.matmul(_o, _l, _r, start=_st, stop=_sp),
                      reads=[wd, hub], accum=[ps])
            em.op("dve", lambda e, _o=o2t[:, m * 512:(m + 1) * 512], _i=ps[:, :]:
                  e.tensor_copy(_o, _i), reads=[ps], accum=[o2t])
        arow = rows[0:1, 0:512]; nmrow = rows[0:1, 512:1024]
        srcl = [o2t[:, c * 512:(c + 1) * 512] for c in range(NCH)]
        ln_stats_seg(ctx, s, srcl, NCH, 1.0 / D, arow, nmrow)
        bcast_row(ctx, arow, abc, 4)
        bcast_row(ctx, nmrow, nmbc, 5)
        for c in range(NCH):
            x3 = xr3c[c % 2]
            em.dma("sp", x3[:], ctx.dram["XR3"][c * 128:(c + 1) * 128, sl], writes=[x3])
            t = tt[c % 2]
            em.op("dve", lambda e, _o=t[:], _a=o2t[:, c * 512:(c + 1) * 512], _b=nmbc[:]:
                  e.tensor_add(_o, _a, _b), reads=[o2t, nmbc], writes=[t])
            em.op("dve", lambda e, _o=t[:], _a=t[:], _b=abc[:]:
                  e.tensor_mul(_o, _a, _b), reads=[t, abc], writes=[t])
            em.op("dve", lambda e, _o=xr4t[:, c * 512:(c + 1) * 512], _a=t[:], _b=x3[:]:
                  e.tensor_add(_o, _a, _b), reads=[t, x3], accum=[xr4t])
        for tc in range(4):
            o = ot[tc % 2]
            for c in range(NCH):
                ps = ctx.P[2 + (tc * NCH + c) % 2]
                em.op("pe", lambda e, _o=ps[:, 0:128],
                      _i=xr4t[:, c * 512 + tc * 128: c * 512 + (tc + 1) * 128],
                      _id=sb["identf"][:]:
                      e.transpose(_o, _i, _id), reads=[xr4t, sb["identf"]], writes=[ps])
                em.op("dve", lambda e, _o=o[:, c * 128:(c + 1) * 128], _i=ps[:, 0:128]:
                      e.tensor_copy(_o, _i), reads=[ps], accum=[o])
            em.dma("sp", ctx.out[s * 512 + tc * 128: s * 512 + (tc + 1) * 128, :], o[:],
                   reads=[o])



def ph012_entry(ctx):
    em, nc, sb = ctx.em, ctx.nc, ctx.sb
    x0 = ctx.inp["x0"]
    XR0 = ctx.dram["XR0"].rearrange("(c p) t -> p c t", p=128)
    XM = ctx.dram["XM"]
    XG = ctx.dram["XG"]
    w = load_weight_bf16(ctx, "p012_w", ctx.inp["w_in"], 1024, 4096)
    bcol = ctx.alloc("p012_b", [128, 32], F32)
    em.dma("sp", bcol[:], ctx.inp["b_in"][:, :], writes=[bcol])
    xt = [ctx.alloc(f"p012_xt{i}", [128, 1024], F32) for i in range(2)]
    xf = [ctx.alloc(f"p012_xf{i}", [128, NCH * 512], F32) for i in range(2)]
    xs = [ctx.alloc(f"p012_xs{i}", [128, NCH * 512], BF16) for i in range(2)]
    stg = [ctx.alloc(f"p012_st{i}", [128, 512], F32) for i in range(4)]
    ab = ctx.alloc("p012_ab", [128, 512], F32)
    nmb = ctx.alloc("p012_nmb", [128, 512], F32)
    tmp = [ctx.alloc(f"p012_tmp{i}", [128, 512], F32) for i in range(2)]
    rows = ctx.alloc("p012_rows", [1, 2 * 512], F32)
    ident = sb["identf"]

    def entry_stats(s):
        sl = slice(s * 512, (s + 1) * 512)
        fbuf = xf[s % 2]
        xsb = xs[s % 2]
        for q in range(4):
            tt = s * 4 + q
            tbuf = xt[tt % 2]
            em.dma("sp", tbuf[:], x0[tt * 128:(tt + 1) * 128, :], writes=[tbuf])
            for c in range(NCH):
                ps = ctx.P[2 + (tt * NCH + c) % 2]
                em.op("pe", lambda e, _o=ps[:, 0:128], _i=tbuf[:, c * 128:(c + 1) * 128], _id=ident[:]:
                      e.transpose(_o, _i, _id),
                      reads=[tbuf, ident], writes=[ps])
                em.op("dve", lambda e, _o=fbuf[:, c * 512 + q * 128: c * 512 + (q + 1) * 128], _i=ps[:, 0:128]:
                      e.tensor_copy(_o, _i),
                      reads=[ps], accum=[fbuf])
        em.dma("sp", XR0[:, :, sl], fbuf[:].rearrange("p (c t) -> p c t", c=NCH), reads=[fbuf])
        src = [fbuf[:, c * 512:(c + 1) * 512] for c in range(NCH)]
        arow = rows[0:1, 0:512]; nmrow = rows[0:1, 512:1024]
        ln_stats_seg(ctx, s, src, NCH, 1.0 / D, arow, nmrow)
        em.dma("sp", ctx.dram["RA0"][0:1, sl], arow, reads=[rows])
        em.dma("sp", ctx.dram["RNM0"][0:1, sl], nmrow, reads=[rows])
        bcast_row(ctx, arow, ab, 4)
        bcast_row(ctx, nmrow, nmb, 5)
        for c in range(NCH):
            t = tmp[c % 2]
            em.op("dve", lambda e, _o=t[:], _a=fbuf[:, c * 512:(c + 1) * 512], _b=nmb[:]:
                  e.tensor_add(_o, _a, _b), reads=[fbuf, nmb], writes=[t])
            em.op("dve", lambda e, _o=xsb[:, c * 512:(c + 1) * 512], _a=t[:], _b=ab[:]:
                  e.tensor_mul(_o, _a, _b), reads=[t, ab], accum=[xsb])

    def inproj(s):
        xsb = xs[s % 2]
        for m in range(32):
            ps = ctx.P[m % 2]
            for k in range(NCH):
                em.op("pe", lambda e, _o=ps[:, :], _l=lhsT(w, 1024, 4096, k, m),
                      _r=xsb[:, k * 512:(k + 1) * 512], _st=(k == 0), _sp=(k == NCH - 1):
                      e.matmul(_o, _l, _r, start=_st, stop=_sp),
                      reads=[w, xsb], accum=[ps])
            st = stg[m % 4]
            em.op("dve", lambda e, _o=st[:], _i=ps[:, :], _b=bcol[:, m:m + 1]:
                  e.tensor_scalar_add(_o, _i, _b),
                  reads=[ps, bcol], writes=[st])
            dst = XM if m < 16 else XG
            mm = m if m < 16 else m - 16
            em.dma("sp", dst[mm * 128:(mm + 1) * 128, s * 512:(s + 1) * 512], st[:],
                   reads=[st])

    entry_stats(0)
    for s in range(1, NSEG):
        inproj(s - 1)
        entry_stats(s)
    inproj(NSEG - 1)





ALL_PHASES = None


def kernel(**inputs):
    global ALL_PHASES
    ins = {k: np.asarray(v) for k, v in inputs.items()}
    prep = host_prep(ins)
    phases = [ph012_entry, ph4_scan, ph6_gateout,
              ph7_dense, ph8_experts, ph9_qkv, ph11_attnout,
              ph12a_mlp1, ph12b_mlp2, ph13_gateup, ph14_down]
    nc = build_program(phases)
    in_maps = []
    x = ins["L0_attn_x"]
    for b in range(8):
        m = dict(prep)
        m["x0"] = np.ascontiguousarray(x[b])
        in_maps.append(m)
    res = run_bass_kernel_spmd(nc, in_maps, core_ids=list(range(8)))
    out = np.stack([np.asarray(res.results[b]["out"]) for b in range(8)], axis=0)
    return out.astype(np.float32)


# version 07 (final)
# speedup vs baseline: 1.0311x; 1.0311x over previous
import math
import numpy as np
import ml_dtypes

import concourse.bass as bass
import concourse.mybir as mybir
from concourse.bass_utils import run_bass_kernel_spmd

F32 = mybir.dt.float32
BF16 = mybir.dt.bfloat16
I32 = mybir.dt.int32
AF = mybir.ActivationFunctionType
ALU = mybir.AluOpType
AX = mybir.AxisListType
BF = ml_dtypes.bfloat16

T = 4096
D = 1024
E = 2048
S = 16
SEG = 512
NSEG = T // SEG
NCH = D // 128

ENGINES = ("pe", "act", "dve", "pool", "sp")


class Res:
    __slots__ = ("name", "writers", "readers")

    def __init__(self, name):
        self.name = name
        self.writers = {}
        self.readers = {}


class Rec:
    __slots__ = ("eng", "fn", "deps", "needs_inc", "inc_val", "idx", "is_dma")

    def __init__(self, eng, fn, deps, idx, is_dma=False):
        self.eng = eng
        self.fn = fn
        self.deps = deps
        self.needs_inc = False
        self.inc_val = None
        self.idx = idx
        self.is_dma = is_dma


class DmaSem:
    __slots__ = ("handle", "count")

    def __init__(self, handle):
        self.handle = handle
        self.count = 0


class Emitter:
    def __init__(self, nc):
        self.nc = nc
        self.recs = {e: [] for e in ENGINES}
        self.n = 0
        self.esem = {e: nc.alloc_semaphore(f"es_{e}") for e in ENGINES}
        self.dsem = {"sp": [DmaSem(nc.alloc_semaphore(f"dsp{i}")) for i in range(6)],
                     "pool": [DmaSem(nc.alloc_semaphore(f"dpl{i}")) for i in range(6)],
                     "act": [DmaSem(nc.alloc_semaphore(f"dac{i}")) for i in range(4)]}
        self.drr = {k: 0 for k in self.dsem}
        self.res = {}

    def R(self, handle):
        key = handle if isinstance(handle, str) else handle.name
        if key not in self.res:
            self.res[key] = Res(key)
        return self.res[key]

    def _dep_tokens(self, eng, reads, writes, accum=()):
        deps = []
        seen = set()

        def add(tok):
            if eng == "pe" and tok[0] == "eng" and tok[1] == "pe":
                return
            k = (tok[0], id(tok[1]), tok[2] if tok[0] == "dma" else id(tok[2]))
            if k in seen:
                return
            seen.add(k)
            deps.append(tok)

        for r in reads:
            rr = self.R(r)
            for t in rr.writers.values():
                add(t)
        for w in writes:
            ww = self.R(w)
            for t in ww.writers.values():
                add(t)
            for t in ww.readers.values():
                add(t)
        for w in accum:
            ww = self.R(w)
            for t in ww.readers.values():
                add(t)
        return deps

    @staticmethod
    def _skey(tok):
        return tok[1] if tok[0] == "eng" else id(tok[1])

    def _register(self, tok, reads, writes, accum):
        sk = self._skey(tok)
        for r in reads:
            self.R(r).readers[sk] = tok
        for w in writes:
            ww = self.R(w)
            ww.writers = {sk: tok}
            ww.readers = {}
        for w in accum:
            ww = self.R(w)
            ww.writers[sk] = tok

    def op(self, eng, fn, reads=(), writes=(), accum=()):
        deps = self._dep_tokens(eng, reads, writes, accum)
        rec = Rec(eng, fn, deps, self.n)
        self.n += 1
        self.recs[eng].append(rec)
        tok = ("eng", eng, rec)
        self._register(tok, reads, writes, accum)
        return rec

    def dma(self, eng, out_ap, in_ap, reads=(), writes=(), accum=(), **kw):
        if eng == "sp" and "DRam" in type(out_ap.tensor).__name__:
            eng = "act"
        deps = self._dep_tokens(eng, reads, writes, accum)
        pool = self.dsem[eng]
        ds = pool[self.drr[eng] % len(pool)]
        self.drr[eng] += 1
        ds.count += 16
        val = ds.count

        def fn(e, _o=out_ap, _i=in_ap, _kw=dict(kw), _ds=ds, _v=val):
            return e.dma_start(_o, _i, **_kw).then_inc(_ds.handle, 16)

        rec = Rec(eng, fn, deps, self.n, is_dma=True)
        self.n += 1
        self.recs[eng].append(rec)
        tok = ("dma", ds, val)
        self._register(tok, reads, writes, accum)
        return rec

    def barrier(self):
        marks = {}
        for e in ENGINES:
            for r in reversed(self.recs[e]):
                if not r.is_dma:
                    if r.deps is not None:
                        r.needs_inc = True
                        marks[e] = r
                    break
        snap_dma = [(ds, ds.count) for pool in self.dsem.values() for ds in pool if ds.count]

        def mk(eng):
            def fn(eobj, _marks=marks, _dmas=snap_dma):
                ins = None
                for pe, rec in _marks.items():
                    ins = eobj.wait_ge(self.esem[pe], rec.inc_val)
                for ds, v in _dmas:
                    ins = eobj.wait_ge(ds.handle, v)
                return ins
            return fn

        for e in ENGINES:
            rec = Rec(e, mk(e), [], self.n)
            self.n += 1
            self.recs[e].append(rec)
        self.res = {}

    def _mark(self):
        for e in ENGINES:
            for rec in self.recs[e]:
                for d in rec.deps:
                    if d[0] == "eng":
                        d[2].needs_inc = True
        for e in ENGINES:
            c = 0
            for rec in self.recs[e]:
                if rec.needs_inc:
                    c += 1
                    rec.inc_val = c

    def emit(self):
        self._mark()
        nc = self.nc
        hi = {e: {} for e in ENGINES}

        def body(eng, eobj):
            hw = hi[eng]
            for rec in self.recs[eng]:
                for d in rec.deps:
                    if d[0] == "eng":
                        de, drec = d[1], d[2]
                        if de == eng and drec.eng == eng and eng == "pe":
                            continue
                        sem = self.esem[de]
                        v = drec.inc_val
                        assert v is not None
                        if hw.get(sem, 0) < v:
                            eobj.wait_ge(sem, v)
                            hw[sem] = v
                    else:
                        _, ds, v = d
                        if hw.get(ds.handle, 0) < v:
                            eobj.wait_ge(ds.handle, v)
                            hw[ds.handle] = v
                ins = rec.fn(eobj)
                if rec.needs_inc and rec.inc_val is not None:
                    assert ins is not None, f"needs_inc rec on {eng} returned no instruction"
                    ins.then_inc(self.esem[eng], 1)

        with nc.Block() as block:
            block.tensor(lambda e: body("pe", e))
            block.scalar(lambda e: body("act", e))
            block.vector(lambda e: body("dve", e))
            block.gpsimd(lambda e: body("pool", e))
            block.sync(lambda e: body("sp", e))



def host_prep(ins):
    C = np.ascontiguousarray
    p = {}
    p["w_in"] = C(ins["L0_attn_in_w"].T)
    p["b_in"] = C(ins["L0_attn_in_b"].reshape(32, 128).T)
    cw = ins["L0_attn_conv_w"][:, 0, :]
    for j in range(3):
        p[f"conv_w{j}"] = C(cw[:, j].reshape(16, 128).T)
    cwd = np.zeros((3, 16, 128, 128), np.float32)
    ii = np.arange(128)
    for j in range(3):
        for c in range(16):
            cwd[j, c, ii, ii] = cw[c * 128:(c + 1) * 128, j]
    p["conv_wd"] = cwd
    p["conv_b"] = C(ins["L0_attn_conv_b"].reshape(16, 128).T)
    p["w_dt"] = C(ins["L0_attn_dt_w"].T)
    p["b_dt"] = C(ins["L0_attn_dt_b"].reshape(16, 128).T)
    p["w_bcd"] = C(np.concatenate([ins["L0_attn_B_w"], ins["L0_attn_C_w"],
                                    ins["L0_attn_D_w"]], 0).T)
    p["b_bcdB"] = C(ins["L0_attn_B_b"].reshape(16, 1))
    p["b_bcdC"] = C(ins["L0_attn_C_b"].reshape(16, 1))
    p["b_bcdD"] = C(ins["L0_attn_D_b"].reshape(16, 1))
    p["w_out0"] = C(ins["L0_attn_out_w"].T)
    p["b_out0"] = C(ins["L0_attn_out_b"].reshape(8, 128).T)
    p["ws1"] = C(ins["L0_ffn_Ws1"])
    p["bs1"] = C(ins["L0_ffn_bs1"].reshape(32, 128).T)
    p["ws2"] = C(ins["L0_ffn_Ws2"])
    p["bs2"] = C(ins["L0_ffn_bs2"].reshape(8, 128).T)
    p["wr"] = C(ins["L0_ffn_Wr"])
    p["br"] = C(ins["L0_ffn_br"].reshape(1, 12))
    p["we1"] = np.ascontiguousarray(
        ins["L0_ffn_We1"].reshape(12, 8, 128, 48, 128).transpose(0, 3, 2, 1, 4)
        .reshape(12, 48, 128, 1024))
    p["be1"] = np.stack([C(ins["L0_ffn_be1"][e].reshape(48, 128).T) for e in range(12)])
    p["we2"] = np.ascontiguousarray(
        ins["L0_ffn_We2"].reshape(12, 24, 128, 8, 128).transpose(0, 3, 2, 1, 4)
        .reshape(12, 8, 128, 3072))
    p["be2"] = C(ins["L0_ffn_be2"])
    p["qkv_w"] = C(ins["L1_attn_qkv_w"])
    p["qkv_bq"] = C(ins["L1_attn_qkv_b"][0:1024].reshape(8, 128).T)
    p["qkv_bkv"] = C(ins["L1_attn_qkv_b"][1024:3072].reshape(1, 2048))
    p["w_out1"] = C(ins["L1_attn_out_w"])
    p["b_out1"] = C(ins["L1_attn_out_b"].reshape(8, 128).T)
    p["w1"] = C(ins["L1_attn_w1"])
    p["b1"] = C(ins["L1_attn_b1"].reshape(32, 128).T)
    p["w2"] = C(ins["L1_attn_w2"])
    p["b2"] = C(ins["L1_attn_b2"].reshape(8, 128).T)
    p["g1r"] = C(ins["L1_attn_g1"].reshape(1, 1024))
    p["be1c"] = C(ins["L1_attn_be1"].reshape(8, 128).T)
    p["g2r"] = C(ins["L1_attn_g2"].reshape(1, 1024))
    p["be2c"] = C(ins["L1_attn_be2"].reshape(8, 128).T)
    p["wg"] = C(ins["L1_ffn_w_gate"].T)
    p["wu"] = C(ins["L1_ffn_w_up"].T)
    p["wd"] = C(ins["L1_ffn_w_down"].T)
    p["identf"] = np.eye(128, dtype=np.float32)
    p["onesb"] = np.ones((128, 1), dtype=BF)
    p["onesf"] = np.ones((128, 1), dtype=np.float32)
    p["ones_row"] = np.ones((1, 128), dtype=np.float32)
    p["ones_rowb"] = np.ones((1, 128), dtype=BF)
    return p


def prep_specs():
    sp = {
        "w_in": ([1024, 4096], F32), "b_in": ([128, 32], F32),
        "conv_w0": ([128, 16], F32), "conv_w1": ([128, 16], F32), "conv_w2": ([128, 16], F32),
        "conv_wd": ([3, 16, 128, 128], F32),
        "conv_b": ([128, 16], F32),
        "w_dt": ([2048, 2048], F32), "b_dt": ([128, 16], F32),
        "w_bcd": ([2048, 48], F32), "b_bcdB": ([16, 1], F32),
        "b_bcdC": ([16, 1], F32), "b_bcdD": ([16, 1], F32),
        "w_out0": ([2048, 1024], F32), "b_out0": ([128, 8], F32),
        "ws1": ([1024, 4096], F32), "bs1": ([128, 32], F32),
        "ws2": ([2048, 1024], F32), "bs2": ([128, 8], F32),
        "wr": ([1024, 12], F32), "br": ([1, 12], F32),
        "we1": ([12, 48, 128, 1024], F32), "be1": ([12, 128, 48], F32),
        "we2": ([12, 8, 128, 3072], F32), "be2": ([12, 1024], F32),
        "qkv_w": ([1024, 3072], F32), "qkv_bq": ([128, 8], F32), "qkv_bkv": ([1, 2048], F32),
        "w_out1": ([1024, 1024], F32), "b_out1": ([128, 8], F32),
        "w1": ([1024, 4096], F32), "b1": ([128, 32], F32),
        "w2": ([4096, 1024], F32), "b2": ([128, 8], F32),
        "g1r": ([1, 1024], F32), "be1c": ([128, 8], F32),
        "g2r": ([1, 1024], F32), "be2c": ([128, 8], F32),
        "wg": ([1024, 3072], I32), "wu": ([1024, 3072], I32), "wd": ([3072, 1024], I32),
        "identf": ([128, 128], F32), "onesb": ([128, 1], BF16),
        "onesf": ([128, 1], F32), "ones_row": ([1, 128], F32),
        "ones_rowb": ([1, 128], BF16),
        "x0": ([T, D], F32),
    }
    return sp


SCRATCH = {
    "XR0": ([D, T], F32), "XR1": ([D, T], F32), "XR2": ([D, T], F32), "XR3": ([D, T], F32),
    "XS0": ([D, T], BF16), "XS2": ([D, T], BF16), "XS4": ([D, T], BF16),
    "XS5": ([D, T], BF16), "XS7": ([D, T], BF16),
    "XM": ([E, T], F32), "XG": ([E, T], F32), "XC": ([E, T], BF16),
    "YS": ([D, T], F32), "WSC": ([12, T], F32),
    "QF": ([D, T], BF16), "KFD": ([T, D], BF16), "VD": ([T, D], BF16),
    "KVD": ([128, 512], BF16), "ZCD": ([128, 8], F32),
    "X3": ([D, T], F32), "HH": ([T // 1, 1], F32),
    "HU": ([3072, T], BF16),
    "RA0": ([1, T], F32), "RNM0": ([1, T], F32), "RYS": ([1, T], F32),
    "RA4": ([1, T], F32), "RNM4": ([1, T], F32),
    "DBG": ([64, 4096], F32),
}
SCRATCH["HH"] = ([4096, T], BF16)


class Ctx:
    pass


def build_program(phases, dump=(), feed=(), core_input_names=("x0",)):
    nc = bass.Bass("TRN2", target_bir_lowering=False)
    ctx = Ctx()
    ctx.nc = nc
    ctx.em = Emitter(nc)
    ctx.inp = {}
    for name, (shape, dt) in prep_specs().items():
        ctx.inp[name] = nc.dram_tensor(name, shape, dt, kind="ExternalInput").ap()
    ctx.dram = {}
    for name, (shape, dt) in SCRATCH.items():
        kind = "Internal"
        if name in dump:
            kind = "ExternalOutput"
        if name in feed:
            kind = "ExternalInput"
        ctx.dram[name] = nc.dram_tensor(name, shape, dt, kind=kind).ap()
    ctx.out = nc.dram_tensor("out", [T, D], F32, kind="ExternalOutput").ap()

    ctx.sb = {}
    def g(name, shape, dt):
        ctx.sb[name] = nc.alloc_sbuf_tensor("sb_" + name, shape, dt)
        return ctx.sb[name]
    g("identf", [128, 128], F32)
    g("onesb", [128, 1], BF16)
    g("onesf", [128, 1], F32)
    g("ones_row", [1, 128], F32)
    g("ones_rowb", [1, 128], BF16)
    em = ctx.em
    for cn in ("identf", "onesb", "onesf", "ones_row", "ones_rowb"):
        em.dma("sp", ctx.sb[cn][:], ctx.inp[cn][:, :], writes=[ctx.sb[cn]])
    cst = g("cst", [128, 4], F32)
    em.op("dve", lambda e: e.memset(cst[:, 0:1], 1e-5), writes=[cst])
    em.op("dve", lambda e: e.memset(cst[:, 1:2], 1.0), writes=[cst])
    nc.const_aps.register(F32, 1e-5, cst[:, 0:1])
    nc.const_aps.register(F32, 1.0, cst[:, 1:2])
    ctx.inp_g1r_sb = nc.alloc_sbuf_tensor("sb_g1r", [1, 1024], F32)
    em.dma("sp", ctx.inp_g1r_sb[:], ctx.inp["g1r"][:, :], writes=[ctx.inp_g1r_sb])
    ctx.inp_g2r_sb = nc.alloc_sbuf_tensor("sb_g2r", [1, 1024], F32)
    em.dma("sp", ctx.inp_g2r_sb[:], ctx.inp["g2r"][:, :], writes=[ctx.inp_g2r_sb])
    ctx.g1rb = nc.alloc_sbuf_tensor("sb_g1rb", [1, 1024], BF16)
    em.op("dve", lambda e: e.tensor_copy(ctx.g1rb[:], ctx.inp_g1r_sb[:]),
          reads=[ctx.inp_g1r_sb], writes=[ctx.g1rb])
    ctx.g2rb = nc.alloc_sbuf_tensor("sb_g2rb", [1, 1024], BF16)
    em.op("dve", lambda e: e.tensor_copy(ctx.g2rb[:], ctx.inp_g2r_sb[:]),
          reads=[ctx.inp_g2r_sb], writes=[ctx.g2rb])
    ctx.sb["_lnsq"] = nc.alloc_sbuf_tensor("_lnsq", [128, 8 * 512], BF16)
    ctx.sb["_lnrt"] = nc.alloc_sbuf_tensor("_lnrt", [1, 2 * 512], F32)
    ctx.sb["_lnxc"] = nc.alloc_sbuf_tensor("_lnxc", [128, 2 * 512], BF16)
    ctx.sb["_bcr"] = nc.alloc_sbuf_tensor("_bcr", [1, 4 * 512], BF16)
    ctx._bcr_i = 0
    em.barrier()

    ctx.P = [nc.alloc_psum_tensor(f"P{i}", [128, 512], F32) for i in range(8)]

    import contextlib
    for ph in phases:
        with contextlib.ExitStack() as stack:
            ctx.alloc = (lambda _st: lambda name, shape, dt:
                         _st.enter_context(nc.sbuf_tensor(name, shape, dt)))(stack)
            ph(ctx)
        em.barrier()
    em.emit()
    return nc



def ph0_entry(ctx):
    em, nc, sb = ctx.em, ctx.nc, ctx.sb
    x0 = ctx.inp["x0"]
    XR0 = ctx.dram["XR0"].rearrange("(c p) t -> p c t", p=128)
    xt = [ctx.alloc(f"p0_xt{i}", [128, 1024], F32) for i in range(2)]
    xf = [ctx.alloc(f"p0_xf{i}", [128, NCH * 512], F32) for i in range(2)]
    ident = sb["identf"]
    for s in range(NSEG):
        fbuf = xf[s % 2]
        for q in range(4):
            tt = s * 4 + q
            tbuf = xt[tt % 2]
            em.dma("sp", tbuf[:], x0[tt * 128:(tt + 1) * 128, :], writes=[tbuf])
            for c in range(NCH):
                ps = ctx.P[(tt * NCH + c) % 2]
                em.op("pe", lambda e, _o=ps[:, 0:128], _i=tbuf[:, c * 128:(c + 1) * 128], _id=ident[:]:
                      e.transpose(_o, _i, _id),
                      reads=[tbuf, ident], writes=[ps])
                em.op("dve", lambda e, _o=fbuf[:, c * 512 + q * 128: c * 512 + (q + 1) * 128], _i=ps[:, 0:128]:
                      e.tensor_copy(_o, _i),
                      reads=[ps], accum=[fbuf])
        em.dma("sp", XR0[:, :, s * 512:(s + 1) * 512],
               fbuf[:].rearrange("p (c t) -> p c t", c=NCH), reads=[fbuf])



def ln_stats_seg(ctx, s, src_f32, nfeat_chunks, inv_n, arow, nmrow, pretag=""):
    em, nc, sb = ctx.em, ctx.nc, ctx.sb
    tag = f"ln{pretag}_{s}"
    sq = ctx.sb["_lnsq"]; rt = ctx.sb["_lnrt"]; xbc = ctx.sb["_lnxc"]
    psum = ctx.P[6]; psq = ctx.P[7]
    onesb = sb["onesb"]
    for c in range(nfeat_chunks):
        xb = xbc[:, (c % 2) * 512:(c % 2 + 1) * 512]
        em.op("dve", lambda e, _o=xb, _i=src_f32[c]: e.tensor_copy(_o, _i),
              reads=[src_f32[c].tensor], accum=[xbc])
        em.op("act", lambda e, _o=sq[:, c * 512:(c + 1) * 512], _i=xb:
              e.activation(_o, _i, AF.Square), reads=[xbc], accum=[sq])
        em.op("pe", lambda e, _o=psum[0:1, :], _l=onesb[:], _r=xb,
              _st=(c == 0), _sp=(c == nfeat_chunks - 1):
              e.matmul(_o, _l, _r, start=_st, stop=_sp),
              reads=[xbc, onesb], accum=[psum])
        em.op("pe", lambda e, _o=psq[0:1, :], _l=onesb[:], _r=sq[:, c * 512:(c + 1) * 512],
              _st=(c == 0), _sp=(c == nfeat_chunks - 1):
              e.matmul(_o, _l, _r, start=_st, stop=_sp),
              reads=[sq, onesb], accum=[psq])
    mean = rt[0:1, 0:512]; var = rt[0:1, 512:1024]; msq = var
    em.op("dve", lambda e, _o=mean, _i=psum[0:1, :]: e.tensor_scalar_mul(_o, _i, float(inv_n)),
          reads=[psum], writes=[rt])
    em.op("act", lambda e, _o=msq, _i=mean: e.activation(_o, _i, AF.Square),
          reads=[rt], writes=[rt])
    em.op("dve", lambda e, _o=var, _i=psq[0:1, :], _s=float(inv_n), _t=msq:
          e.scalar_tensor_tensor(_o, _i, _s, _t, ALU.mult, ALU.subtract),
          reads=[psq, rt], writes=[rt])
    em.op("act", lambda e, _o=var, _i=var: e.activation(_o, _i, AF.Ln, bias=1e-5),
          reads=[rt], writes=[rt])
    em.op("act", lambda e, _o=arow, _i=var: e.activation(_o, _i, AF.Exp, scale=-0.5),
          reads=[rt], writes=[arow.tensor])
    em.op("dve", lambda e, _o=nmrow, _i=mean: e.tensor_scalar_mul(_o, _i, -1.0),
          reads=[rt], writes=[nmrow.tensor])


def bcast_row(ctx, row_ap, out_tile, psum_i, width=512, exact=False):
    em, sb = ctx.em, ctx.sb
    ps = ctx.P[psum_i]
    if exact:
        em.op("pe", lambda e, _o=ps[:, 0:width], _l=sb["ones_row"][:], _r=row_ap:
              e.matmul(_o, _l, _r, start=True, stop=True),
              reads=[row_ap.tensor, sb["ones_row"]], writes=[ps])
    else:
        i = ctx._bcr_i % 4
        ctx._bcr_i += 1
        br = ctx.sb["_bcr"][0:1, i * 512:i * 512 + width]
        em.op("dve", lambda e, _o=br, _i=row_ap: e.tensor_copy(_o, _i),
              reads=[row_ap.tensor], accum=[ctx.sb["_bcr"]])
        em.op("pe", lambda e, _o=ps[:, 0:width], _l=sb["ones_rowb"][:], _r=br:
              e.matmul(_o, _l, _r, start=True, stop=True),
              reads=[ctx.sb["_bcr"], sb["ones_rowb"]], writes=[ps])
    em.op("dve", lambda e, _o=out_tile[:, 0:width], _i=ps[:, 0:width]: e.tensor_copy(_o, _i),
          reads=[ps], writes=[out_tile])


def ph1_ln0(ctx):
    em, nc = ctx.em, ctx.nc
    XR0 = ctx.dram["XR0"].rearrange("(c p) t -> p c t", p=128)
    XS0 = ctx.dram["XS0"].rearrange("(c p) t -> p c t", p=128)
    xr = [ctx.alloc(f"p1_xr{i}", [128, NCH * 512], F32) for i in range(2)]
    xs = [ctx.alloc(f"p1_xs{i}", [128, NCH * 512], BF16) for i in range(2)]
    ab = ctx.alloc("p1_ab", [128, 512], F32)
    nmb = ctx.alloc("p1_nmb", [128, 512], F32)
    tmp = ctx.alloc("p1_tmp", [128, 512], F32)
    rows = ctx.alloc("p1_rows", [1, 2 * 512], F32)
    for s in range(NSEG):
        xrb = xr[s % 2]; xsb = xs[s % 2]
        em.dma("sp", xrb[:].rearrange("p (c t) -> p c t", c=NCH),
               XR0[:, :, s * 512:(s + 1) * 512], writes=[xrb])
        src = [xrb[:, c * 512:(c + 1) * 512] for c in range(NCH)]
        arow = rows[0:1, 0:512]; nmrow = rows[0:1, 512:1024]
        ln_stats_seg(ctx, s, src, NCH, 1.0 / D, arow, nmrow)
        em.dma("sp", ctx.dram["RA0"][0:1, s * 512:(s + 1) * 512], arow, reads=[rows])
        em.dma("sp", ctx.dram["RNM0"][0:1, s * 512:(s + 1) * 512], nmrow, reads=[rows])
        bcast_row(ctx, arow, ab, 4)
        bcast_row(ctx, nmrow, nmb, 5)
        for c in range(NCH):
            em.op("dve", lambda e, _o=tmp[:], _a=xrb[:, c * 512:(c + 1) * 512], _b=nmb[:]:
                  e.tensor_add(_o, _a, _b), reads=[xrb, nmb], writes=[tmp])
            em.op("dve", lambda e, _o=xsb[:, c * 512:(c + 1) * 512], _a=tmp[:], _b=ab[:]:
                  e.tensor_mul(_o, _a, _b), reads=[tmp, ab], accum=[xsb])
        em.dma("sp", XS0[:, :, s * 512:(s + 1) * 512],
               xsb[:].rearrange("p (c t) -> p c t", c=NCH), reads=[xsb])



def load_weight_bf16(ctx, name, dram_ap, K, M, eng="pool"):
    em = ctx.em
    w = ctx.alloc(name, [128, (K // 128) * M], BF16)
    for k in range(K // 128):
        em.dma(eng, w[:, k * M:(k + 1) * M], dram_ap[k * 128:(k + 1) * 128, :],
               accum=[w])
    return w


def lhsT(w, K, M, k, m, mw=128):
    return w[:, k * M + m * mw: k * M + m * mw + mw]



def ph2_inproj(ctx):
    em, nc = ctx.em, ctx.nc
    XS0 = ctx.dram["XS0"].rearrange("(c p) t -> p c t", p=128)
    XM = ctx.dram["XM"]
    XG = ctx.dram["XG"]
    w = load_weight_bf16(ctx, "p2_w", ctx.inp["w_in"], 1024, 4096)
    bcol = ctx.alloc("p2_b", [128, 32], F32)
    em.dma("sp", bcol[:], ctx.inp["b_in"][:, :], writes=[bcol])
    xs = [ctx.alloc(f"p2_xs{i}", [128, NCH * 512], BF16) for i in range(2)]
    stg = [ctx.alloc(f"p2_st{i}", [128, 512], F32) for i in range(4)]
    for s in range(NSEG):
        xsb = xs[s % 2]
        em.dma("sp", xsb[:].rearrange("p (c t) -> p c t", c=NCH),
               XS0[:, :, s * 512:(s + 1) * 512], writes=[xsb])
        for m in range(32):
            ps = ctx.P[m % 2]
            for k in range(NCH):
                em.op("pe", lambda e, _o=ps[:, :], _l=lhsT(w, 1024, 4096, k, m),
                      _r=xsb[:, k * 512:(k + 1) * 512], _st=(k == 0), _sp=(k == NCH - 1):
                      e.matmul(_o, _l, _r, start=_st, stop=_sp),
                      reads=[w, xsb], accum=[ps])
            st = stg[m % 4]
            em.op("dve", lambda e, _o=st[:], _i=ps[:, :], _b=bcol[:, m:m + 1]:
                  e.tensor_scalar_add(_o, _i, _b),
                  reads=[ps, bcol], writes=[st])
            dst = XM if m < 16 else XG
            mm = m if m < 16 else m - 16
            em.dma("sp", dst[mm * 128:(mm + 1) * 128, s * 512:(s + 1) * 512], st[:],
                   reads=[st])



def ph3_conv(ctx):
    em, nc = ctx.em, ctx.nc
    XM = ctx.dram["XM"]
    XC = ctx.dram["XC"]
    cw = []
    for j in range(3):
        t = ctx.alloc(f"p3_cw{j}", [128, 16], F32)
        em.dma("sp", t[:], ctx.inp[f"conv_w{j}"][:, :], writes=[t])
        cw.append(t)
    cb = ctx.alloc("p3_cb", [128, 16], F32)
    em.dma("sp", cb[:], ctx.inp["conv_b"][:, :], writes=[cb])
    buf = [ctx.alloc(f"p3_b{i}", [128, 514], F32) for i in range(3)]
    acc = [ctx.alloc(f"p3_a{i}", [128, 512], F32) for i in range(3)]
    stg = [ctx.alloc(f"p3_s{i}", [128, 512], BF16) for i in range(3)]
    ECH = E // 128
    for s in range(NSEG):
        lo = s * 512 - 1
        hi = s * 512 + 513
        for c in range(ECH):
            i = (s * ECH + c) % 3
            b, a, st = buf[i], acc[i], stg[i]
            if s == 0:
                em.op("dve", lambda e, _b=b: e.memset(_b[:, 0:1], 0.0), writes=[b])
                em.dma("sp", b[:, 1:514], XM[c * 128:(c + 1) * 128, 0:513], accum=[b])
            elif s == NSEG - 1:
                em.op("dve", lambda e, _b=b: e.memset(_b[:, 513:514], 0.0), writes=[b])
                em.dma("sp", b[:, 0:513], XM[c * 128:(c + 1) * 128, lo:lo + 513], accum=[b])
            else:
                em.dma("sp", b[:, 0:514], XM[c * 128:(c + 1) * 128, lo:hi], writes=[b])
            em.op("dve", lambda e, _o=a[:], _i=b[:, 0:512], _w=cw[0][:, c:c + 1]:
                  e.tensor_scalar_mul(_o, _i, _w), reads=[b, cw[0]], writes=[a])
            em.op("dve", lambda e, _o=a[:], _i=b[:, 1:513], _w=cw[1][:, c:c + 1], _a=a[:]:
                  e.scalar_tensor_tensor(_o, _i, _w, _a, ALU.mult, ALU.add),
                  reads=[b, cw[1], a], writes=[a])
            em.op("dve", lambda e, _o=a[:], _i=b[:, 2:514], _w=cw[2][:, c:c + 1], _a=a[:]:
                  e.scalar_tensor_tensor(_o, _i, _w, _a, ALU.mult, ALU.add),
                  reads=[b, cw[2], a], writes=[a])
            em.op("act", lambda e, _o=st[:], _i=a[:], _b=cb[:, c:c + 1]:
                  e.activation(_o, _i, AF.Silu, bias=_b),
                  reads=[a, cb], writes=[st])
            em.dma("sp", XC[c * 128:(c + 1) * 128, s * 512:(s + 1) * 512], st[:],
                   reads=[st])



def ph4_scan(ctx):
    em, nc, sb = ctx.em, ctx.nc, ctx.sb
    XM = ctx.dram["XM"]
    ECH = E // 128
    HT = T // 4
    wdt = load_weight_bf16(ctx, "p4_wdt", ctx.inp["w_dt"], 2048, 2048)
    wbcd = load_weight_bf16(ctx, "p4_wbcd", ctx.inp["w_bcd"], 2048, 48)
    bdt = ctx.alloc("p4_bdt", [128, 16], F32)
    em.dma("sp", bdt[:], ctx.inp["b_dt"][:, :], writes=[bdt])
    bbcd = []
    for i, bn in enumerate(("b_bcdB", "b_bcdC", "b_bcdD")):
        t = ctx.alloc("p4_" + bn, [16, 1], F32)
        em.dma("sp", t[:], ctx.inp[bn][:, :], writes=[t])
        bbcd.append(t)
    wcd = []
    for j in range(3):
        t = ctx.alloc(f"p4_wcd{j}", [128, 16 * 128], BF16)
        em.dma("pool", t[:].rearrange("p (c m) -> p c m", c=16),
               ctx.inp["conv_wd"][j].rearrange("c p m -> p c m"), writes=[t])
        wcd.append(t)
    cb = ctx.alloc("p4_cbb", [128, 16], F32)
    em.dma("sp", cb[:], ctx.inp["conv_b"][:, :], writes=[cb])
    cbufA = [ctx.alloc(f"p4_cba{i}", [128, 514], BF16) for i in range(2)]
    cbufB = [ctx.alloc(f"p4_cbb2{i}", [128, 512], BF16) for i in range(2)]
    xc = [ctx.alloc(f"p4_xc{i}", [128, ECH * 512], BF16) for i in range(2)]
    rl = [ctx.alloc(f"p4_rl{i}", [128, 512], BF16) for i in range(2)]
    ab = [ctx.alloc(f"p4_ab{i}", [128, 512], F32) for i in range(2)]
    ln1 = [ctx.alloc(f"p4_ln{i}", [128, 512], BF16) for i in range(2)]
    drow = ctx.alloc("p4_drow", [1, T // 2], F32)
    rt = [ctx.alloc(f"p4_rt{i}", [1, 512], F32) for i in range(2)]
    Bsb = ctx.alloc("p4_B", [16, 4 * 512], F32)
    Csb = ctx.alloc("p4_C", [16, 4 * 512], F32)
    Dsb = ctx.alloc("p4_D", [16, 4 * 512], F32)
    dbc = ctx.alloc("p4_dbc", [16, HT], F32)
    stt = ctx.alloc("p4_st", [16, HT], F32)
    ysr = ctx.alloc("p4_ys", [1, HT], F32)

    carry = ctx.alloc("p4_carry", [16, 1], F32)

    def conv_emit(s):
        xcb = xc[s % 2]
        lo = s * 512 - 1
        for c in range(ECH):
            i = (s * ECH + c) % 2
            bA = cbufA[i]
            bB = cbufB[i]
            if s == 0:
                em.dma("pool", bA[:, 1:514], XM[c * 128:(c + 1) * 128, 0:513], writes=[bA])
                em.op("dve", lambda e, _b=bA: e.memset(_b[:, 0:1], 0.0), accum=[bA])
                em.dma("pool", bB[:, 0:512], XM[c * 128:(c + 1) * 128, 0:512], writes=[bB])
            elif s == NSEG - 1:
                em.dma("pool", bA[:, 0:513], XM[c * 128:(c + 1) * 128, lo:lo + 513], writes=[bA])
                em.op("dve", lambda e, _b=bA: e.memset(_b[:, 513:514], 0.0), accum=[bA])
                em.dma("pool", bB[:, 0:512], XM[c * 128:(c + 1) * 128, lo + 1:lo + 513], writes=[bB])
            else:
                em.dma("pool", bA[:, 0:514], XM[c * 128:(c + 1) * 128, lo:lo + 514], writes=[bA])
                em.dma("pool", bB[:, 0:512], XM[c * 128:(c + 1) * 128, lo + 1:lo + 513], writes=[bB])
            ps = ctx.P[5 if (s * ECH + c) % 2 == 0 else 7]
            em.op("pe", lambda e, _o=ps[:, :], _l=wcd[0][:, c * 128:(c + 1) * 128],
                  _r=bA[:, 0:512]:
                  e.matmul(_o, _l, _r, start=True, stop=False),
                  reads=[bA, wcd[0]], accum=[ps])
            em.op("pe", lambda e, _o=ps[:, :], _l=wcd[1][:, c * 128:(c + 1) * 128],
                  _r=bB[:, 0:512]:
                  e.matmul(_o, _l, _r, start=False, stop=False),
                  reads=[bB, wcd[1]], accum=[ps])
            em.op("pe", lambda e, _o=ps[:, :], _l=wcd[2][:, c * 128:(c + 1) * 128],
                  _r=bA[:, 2:514]:
                  e.matmul(_o, _l, _r, start=False, stop=True),
                  reads=[bA, wcd[2]], accum=[ps])
            em.op("act", lambda e, _o=xcb[:, c * 512:(c + 1) * 512], _i=ps[:, :], _b=cb[:, c:c + 1]:
                  e.activation(_o, _i, AF.Silu, bias=_b),
                  reads=[ps, cb], accum=[xcb])

    def dtbcd_emit(s):
        hs = s % 4
        xcb = xc[s % 2]
        for m in range(16):
            ps = ctx.P[m % 2]
            for k in range(ECH):
                em.op("pe", lambda e, _o=ps[:, :], _l=lhsT(wdt, 2048, 2048, k, m),
                      _r=xcb[:, k * 512:(k + 1) * 512], _st=(k == 0), _sp=(k == ECH - 1):
                      e.matmul(_o, _l, _r, start=_st, stop=_sp),
                      reads=[wdt, xcb], accum=[ps])
            r, A, L = rl[m % 2], ab[m % 2], ln1[m % 2]
            em.op("act", lambda e, _o=r[:], _i=ps[:, :], _b=bdt[:, m:m + 1]:
                  e.activation(_o, _i, AF.Relu, bias=_b),
                  reads=[ps, bdt], writes=[r])
            em.op("act", lambda e, _o=A[:], _i=ps[:, :], _b=bdt[:, m:m + 1]:
                  e.activation(_o, _i, AF.Abs, bias=_b),
                  reads=[ps, bdt], writes=[A])
            em.op("act", lambda e, _o=A[:], _i=A[:]:
                  e.activation(_o, _i, AF.Exp, scale=-1.0), reads=[A], writes=[A])
            em.op("act", lambda e, _o=L[:], _i=A[:]:
                  e.activation(_o, _i, AF.Ln, bias=1.0), reads=[A], writes=[L])
            em.op("pe", lambda e, _o=ctx.P[6][0:1, :], _l=sb["onesb"][:], _r=r[:],
                  _st=(m == 0), _sp=False:
                  e.matmul(_o, _l, _r, start=_st, stop=_sp),
                  reads=[r, sb["onesb"]], accum=[ctx.P[6]])
            em.op("pe", lambda e, _o=ctx.P[6][0:1, :], _l=sb["onesb"][:], _r=L[:],
                  _st=False, _sp=(m == 15):
                  e.matmul(_o, _l, _r, start=_st, stop=_sp),
                  reads=[L, sb["onesb"]], accum=[ctx.P[6]])
        rts = rt[s % 2]
        em.op("dve", lambda e, _o=rts[0:1, :], _i=ctx.P[6][0:1, :]:
              e.tensor_scalar(_o, _i, float(-1.0 / E), -10.0, ALU.mult, ALU.max),
              reads=[ctx.P[6]], writes=[rts])
        em.op("dve", lambda e, _o=rts[0:1, :], _i=rts[0:1, :]:
              e.tensor_scalar_min(_o, _i, 10.0), reads=[rts], writes=[rts])
        em.op("act", lambda e, _o=drow[0:1, (s % 4) * 512:((s % 4) + 1) * 512], _i=rts[0:1, :]:
              e.activation(_o, _i, AF.Exp), reads=[rts], accum=[drow])
        for i, (psi, dst) in enumerate(((2, Bsb), (3, Csb), (4, Dsb))):
            ps = ctx.P[psi]
            for k in range(ECH):
                em.op("pe", lambda e, _o=ps[0:16, :],
                      _l=wbcd[:, k * 48 + i * 16: k * 48 + (i + 1) * 16],
                      _r=xcb[:, k * 512:(k + 1) * 512], _st=(k == 0), _sp=(k == ECH - 1):
                      e.matmul(_o, _l, _r, start=_st, stop=_sp),
                      reads=[wbcd, xcb], accum=[ps])
            em.op("dve", lambda e, _o=dst[:, hs * 512:(hs + 1) * 512], _i=ps[0:16, :],
                  _b=bbcd[i][:, 0:1]:
                  e.tensor_scalar_add(_o, _i, _b),
                  reads=[ps, bbcd[i]], accum=[dst])

    def scan_half(q):
        b0 = ((2 * q) % 4) * 512
        Bq = Bsb[:, b0:b0 + 1024]; Cq = Csb[:, b0:b0 + 1024]; Dq = Dsb[:, b0:b0 + 1024]
        for hs in range(2):
            s = q * 2 + hs
            em.op("pe", lambda e, _o=ctx.P[5][0:16, :], _l=sb["ones_row"][0:1, 0:16],
                  _r=drow[0:1, (s % 4) * 512:((s % 4) + 1) * 512]:
                  e.matmul(_o, _l, _r, start=True, stop=True),
                  reads=[drow, sb["ones_row"]], writes=[ctx.P[5]])
            em.op("dve", lambda e, _o=dbc[:, hs * 512:(hs + 1) * 512], _i=ctx.P[5][0:16, :]:
                  e.tensor_copy(_o, _i), reads=[ctx.P[5]], accum=[dbc])
        em.op("dve", lambda e, _B=Bq, _D=Dq: e.tensor_mul(_B, _B, _D),
              reads=[Bsb, Dsb], accum=[Bsb])
        if q == 0:
            em.op("dve", lambda e, _B=Bq: e.tensor_tensor_scan(stt[:], dbc[:], _B, 0.0,
                                                               ALU.mult, ALU.add),
                  reads=[dbc, Bsb], writes=[stt])
        else:
            em.op("dve", lambda e, _B=Bq: e.tensor_tensor_scan(stt[:], dbc[:], _B,
                                                               carry[:, 0:1], ALU.mult, ALU.add),
                  reads=[dbc, Bsb, carry], writes=[stt])
        em.op("dve", lambda e: e.tensor_copy(carry[:, 0:1], stt[:, HT - 1:HT]),
              reads=[stt], writes=[carry])
        em.op("dve", lambda e, _C=Cq: e.tensor_mul(_C, _C, stt[:]),
              reads=[Csb, stt], accum=[Csb])

        for hs in range(2):
            em.op("pe", lambda e, _o=ctx.P[7][0:1, :], _l=sb["onesf"][0:16, 0:1],
                  _r=Csb[:, b0 + hs * 512:b0 + (hs + 1) * 512]:
                  e.matmul(_o, _l, _r, start=True, stop=True),
                  reads=[Csb, sb["onesf"]], writes=[ctx.P[7]])
            em.op("dve", lambda e, _o=ysr[0:1, hs * 512:(hs + 1) * 512], _i=ctx.P[7][0:1, :]:
                  e.tensor_copy(_o, _i), reads=[ctx.P[7]], accum=[ysr])
        em.dma("sp", ctx.dram["RYS"][0:1, q * HT:(q + 1) * HT], ysr[0:1, :],
               reads=[ysr])

    conv_emit(0)
    for s in range(NSEG):
        if s + 1 < NSEG:
            conv_emit(s + 1)
        dtbcd_emit(s)
        if s >= 3 and s % 2 == 1 and s < NSEG - 1:
            scan_half((s - 3) // 2)
    scan_half(2)
    scan_half(3)



def ph6_gateout(ctx):
    em, nc, sb = ctx.em, ctx.nc, ctx.sb
    ECH = E // 128
    XG = ctx.dram["XG"].rearrange("(c p) t -> p c t", p=128)
    XR0 = ctx.dram["XR0"].rearrange("(c p) t -> p c t", p=128)
    XR1 = ctx.dram["XR1"].rearrange("(c p) t -> p c t", p=128)
    XS2 = ctx.dram["XS2"].rearrange("(c p) t -> p c t", p=128)
    wout = load_weight_bf16(ctx, "p6_w", ctx.inp["w_out0"], 2048, 1024)
    bcol = ctx.alloc("p6_b", [128, 8], F32)
    em.dma("sp", bcol[:], ctx.inp["b_out0"][:, :], writes=[bcol])
    xg = ctx.alloc("p6_xg", [128, ECH * 512], F32)
    xr0 = ctx.alloc("p6_xr0", [128, NCH * 512], F32)
    yg = ctx.alloc("p6_yg", [128, ECH * 512], BF16)
    res = ctx.alloc("p6_res", [128, NCH * 512], F32)
    xr1 = ctx.alloc("p6_xr1", [128, NCH * 512], F32)
    xs2 = ctx.alloc("p6_xs2", [128, NCH * 512], BF16)
    sil = [ctx.alloc(f"p6_sil{i}", [128, 512], F32) for i in range(2)]
    tmp = [ctx.alloc(f"p6_tmp{i}", [128, 512], F32) for i in range(2)]
    rows = ctx.alloc("p6_rows", [1, 3 * 512], F32)
    ybc = ctx.alloc("p6_ybc", [128, 512], F32)
    a0bc = ctx.alloc("p6_a0bc", [128, 512], F32)
    nm0bc = ctx.alloc("p6_nm0bc", [128, 512], F32)
    a1bc = ctx.alloc("p6_a1bc", [128, 512], F32)
    nm1bc = ctx.alloc("p6_nm1bc", [128, 512], F32)
    ysrow = ctx.alloc("p6_ysrow", [1, 512], F32)
    a0row = ctx.alloc("p6_a0row", [1, 512], F32)
    nm0row = ctx.alloc("p6_nm0row", [1, 512], F32)
    for s in range(NSEG):
        sl = slice(s * 512, (s + 1) * 512)
        em.dma("sp", xg[:].rearrange("p (c t) -> p c t", c=ECH), XG[:, :, sl], writes=[xg])
        em.dma("sp", xr0[:].rearrange("p (c t) -> p c t", c=NCH), XR0[:, :, sl], writes=[xr0])
        em.dma("sp", ysrow[0:1, :], ctx.dram["RYS"][0:1, sl], writes=[ysrow])
        em.dma("sp", a0row[0:1, :], ctx.dram["RA0"][0:1, sl], writes=[a0row])
        em.dma("sp", nm0row[0:1, :], ctx.dram["RNM0"][0:1, sl], writes=[nm0row])
        bcast_row(ctx, ysrow[0:1, :], ybc, 5)
        bcast_row(ctx, a0row[0:1, :], a0bc, 4)
        bcast_row(ctx, nm0row[0:1, :], nm0bc, 5)
        for c in range(ECH):
            si = sil[c % 2]
            em.op("act", lambda e, _o=si[:], _i=xg[:, c * 512:(c + 1) * 512]:
                  e.activation(_o, _i, AF.Silu), reads=[xg], writes=[si])
            em.op("dve", lambda e, _o=yg[:, c * 512:(c + 1) * 512], _a=si[:], _b=ybc[:]:
                  e.tensor_mul(_o, _a, _b), reads=[si, ybc], accum=[yg])
        for m in range(NCH):
            ps = ctx.P[m % 4]
            for k in range(ECH):
                em.op("pe", lambda e, _o=ps[:, :], _l=lhsT(wout, 2048, 1024, k, m),
                      _r=yg[:, k * 512:(k + 1) * 512], _st=(k == 0), _sp=(k == ECH - 1):
                      e.matmul(_o, _l, _r, start=_st, stop=_sp),
                      reads=[wout, yg], accum=[ps])
            t = tmp[m % 2]
            em.op("dve", lambda e, _o=t[:], _a=xr0[:, m * 512:(m + 1) * 512], _b=nm0bc[:]:
                  e.tensor_add(_o, _a, _b), reads=[xr0, nm0bc], writes=[t])
            em.op("dve", lambda e, _o=t[:], _a=t[:], _b=a0bc[:]:
                  e.tensor_mul(_o, _a, _b), reads=[t, a0bc], writes=[t])
            em.op("dve", lambda e, _o=res[:, m * 512:(m + 1) * 512], _i=ps[:, :],
                  _b=bcol[:, m:m + 1], _t=t[:]:
                  e.scalar_tensor_tensor(_o, _i, _b, _t, ALU.add, ALU.add),
                  reads=[ps, bcol, t], accum=[res])
        arow = rows[0:1, 0:512]; nmrow = rows[0:1, 512:1024]
        src = [res[:, c * 512:(c + 1) * 512] for c in range(NCH)]
        ln_stats_seg(ctx, s, src, NCH, 1.0 / D, arow, nmrow)
        bcast_row(ctx, arow, a1bc, 4)
        bcast_row(ctx, nmrow, nm1bc, 5)
        for c in range(NCH):
            t = tmp[c % 2]
            em.op("dve", lambda e, _o=t[:], _a=res[:, c * 512:(c + 1) * 512], _b=nm1bc[:]:
                  e.tensor_add(_o, _a, _b), reads=[res, nm1bc], writes=[t])
            em.op("dve", lambda e, _o=t[:], _a=t[:], _b=a1bc[:]:
                  e.tensor_mul(_o, _a, _b), reads=[t, a1bc], writes=[t])
            em.op("dve", lambda e, _o=xr1[:, c * 512:(c + 1) * 512], _a=t[:],
                  _b=xr0[:, c * 512:(c + 1) * 512]:
                  e.tensor_add(_o, _a, _b), reads=[t, xr0], accum=[xr1])
        em.dma("sp", XR1[:, :, sl], xr1[:].rearrange("p (c t) -> p c t", c=NCH),
               reads=[xr1])
        src = [xr1[:, c * 512:(c + 1) * 512] for c in range(NCH)]
        ln_stats_seg(ctx, s, src, NCH, 1.0 / D, arow, nmrow)
        bcast_row(ctx, arow, a1bc, 4)
        bcast_row(ctx, nmrow, nm1bc, 5)
        for c in range(NCH):
            t = tmp[c % 2]
            em.op("dve", lambda e, _o=t[:], _a=xr1[:, c * 512:(c + 1) * 512], _b=nm1bc[:]:
                  e.tensor_add(_o, _a, _b), reads=[xr1, nm1bc], writes=[t])
            em.op("dve", lambda e, _o=xs2[:, c * 512:(c + 1) * 512], _a=t[:], _b=a1bc[:]:
                  e.tensor_mul(_o, _a, _b), reads=[t, a1bc], accum=[xs2])
        em.dma("sp", XS2[:, :, sl], xs2[:].rearrange("p (c t) -> p c t", c=NCH),
               reads=[xs2])



def ph7_dense(ctx):
    em, nc, sb = ctx.em, ctx.nc, ctx.sb
    XS2 = ctx.dram["XS2"].rearrange("(c p) t -> p c t", p=128)
    YS = ctx.dram["YS"]
    ws1 = load_weight_bf16(ctx, "p7_ws1", ctx.inp["ws1"], 1024, 4096)
    ws2 = load_weight_bf16(ctx, "p7_ws2", ctx.inp["ws2"], 2048, 1024)
    wr = load_weight_bf16(ctx, "p7_wr", ctx.inp["wr"], 1024, 12)
    bs1 = ctx.alloc("p7_bs1", [128, 32], F32)
    em.dma("sp", bs1[:], ctx.inp["bs1"][:, :], writes=[bs1])
    bs2 = ctx.alloc("p7_bs2", [128, 8], F32)
    em.dma("sp", bs2[:], ctx.inp["bs2"][:, :], writes=[bs2])
    brrow = ctx.alloc("p7_br", [1, 12], F32)
    em.dma("sp", brrow[:], ctx.inp["br"][:, :], writes=[brrow])
    brbc = ctx.alloc("p7_brbc", [128, 12], F32)
    bcast_row(ctx, brrow[0:1, :], brbc, 6, width=12)
    xs = [ctx.alloc(f"p7_xs{i}", [128, NCH * 512], BF16) for i in range(2)]
    hh = ctx.alloc("p7_hh", [128, 16 * 512], BF16)
    gt = [ctx.alloc(f"p7_gt{i}", [128, 512], F32) for i in range(2)]
    at = [ctx.alloc(f"p7_at{i}", [128, 512], F32) for i in range(2)]
    stg = [ctx.alloc(f"p7_st{i}", [128, 512], F32) for i in range(3)]
    wsc = ctx.alloc("p7_wsc", [12, 512], F32)
    rt_ = ctx.alloc("p7_rt", [128, 12 * 4], F32)
    rc = ctx.alloc("p7_rc", [128, 8], F32)
    for s in range(NSEG):
        sl = slice(s * 512, (s + 1) * 512)
        xsb = xs[s % 2]
        em.dma("sp", xsb[:].rearrange("p (c t) -> p c t", c=NCH), XS2[:, :, sl], writes=[xsb])
        for m in range(16):
            psa, psg = ctx.P[(m % 2) * 2], ctx.P[(m % 2) * 2 + 1]
            for k in range(NCH):
                em.op("pe", lambda e, _o=psa[:, :], _l=lhsT(ws1, 1024, 4096, k, m),
                      _r=xsb[:, k * 512:(k + 1) * 512], _st=(k == 0), _sp=(k == NCH - 1):
                      e.matmul(_o, _l, _r, start=_st, stop=_sp),
                      reads=[ws1, xsb], accum=[psa])
                em.op("pe", lambda e, _o=psg[:, :], _l=lhsT(ws1, 1024, 4096, k, m + 16),
                      _r=xsb[:, k * 512:(k + 1) * 512], _st=(k == 0), _sp=(k == NCH - 1):
                      e.matmul(_o, _l, _r, start=_st, stop=_sp),
                      reads=[ws1, xsb], accum=[psg])
            g, a = gt[m % 2], at[m % 2]
            em.op("act", lambda e, _o=g[:], _i=psg[:, :], _b=bs1[:, m + 16:m + 17]:
                  e.activation(_o, _i, AF.Gelu, bias=_b), reads=[psg, bs1], writes=[g])
            em.op("dve", lambda e, _o=a[:], _i=psa[:, :], _b=bs1[:, m:m + 1]:
                  e.tensor_scalar_add(_o, _i, _b), reads=[psa, bs1], writes=[a])
            em.op("dve", lambda e, _o=hh[:, m * 512:(m + 1) * 512], _a=a[:], _b=g[:]:
                  e.tensor_mul(_o, _a, _b), reads=[a, g], accum=[hh])
        for m in range(NCH):
            ps = ctx.P[4 + m % 2]
            for k in range(16):
                em.op("pe", lambda e, _o=ps[:, :], _l=lhsT(ws2, 2048, 1024, k, m),
                      _r=hh[:, k * 512:(k + 1) * 512], _st=(k == 0), _sp=(k == 15):
                      e.matmul(_o, _l, _r, start=_st, stop=_sp),
                      reads=[ws2, hh], accum=[ps])
            st = stg[m % 3]
            em.op("dve", lambda e, _o=st[:], _i=ps[:, :], _b=bs2[:, m:m + 1]:
                  e.tensor_scalar_add(_o, _i, _b), reads=[ps, bs2], writes=[st])
            em.dma("sp", YS[m * 128:(m + 1) * 128, sl], st[:], reads=[st])
        for tc in range(4):
            ps = ctx.P[6]
            for k in range(NCH):
                em.op("pe", lambda e, _o=ps[:, 0:12],
                      _l=xsb[:, k * 512 + tc * 128: k * 512 + (tc + 1) * 128],
                      _r=wr[:, k * 12:(k + 1) * 12], _st=(k == 0), _sp=(k == NCH - 1):
                      e.matmul(_o, _l, _r, start=_st, stop=_sp),
                      reads=[xsb, wr], accum=[ps])
            L = rt_[:, 0:12]; msk = rt_[:, 12:24]; eq1 = rt_[:, 24:36]; wv = rt_[:, 36:48]
            m1 = rc[:, 0:1]; m2 = rc[:, 1:2]; dd = rc[:, 2:3]; w1 = rc[:, 3:4]; w2 = rc[:, 4:5]
            em.op("dve", lambda e, _o=L, _a=ps[:, 0:12], _b=brbc[:]:
                  e.tensor_add(_o, _a, _b), reads=[ps, brbc], writes=[rt_])
            em.op("dve", lambda e, _o=m1, _i=L: e.reduce_max(_o, _i, AX.X),
                  reads=[rt_], writes=[rc])
            em.op("dve", lambda e, _o=eq1, _i=L, _s=m1:
                  e.tensor_single_scalar(_o, _i, _s, ALU.is_ge), reads=[rt_, rc], accum=[rt_])
            em.op("dve", lambda e, _o=msk, _i=eq1, _t=L:
                  e.scalar_tensor_tensor(_o, _i, -1e30, _t, ALU.mult, ALU.add),
                  reads=[rt_], accum=[rt_])
            em.op("dve", lambda e, _o=m2, _i=msk: e.reduce_max(_o, _i, AX.X),
                  reads=[rt_], accum=[rc])
            em.op("dve", lambda e, _o=dd, _a=m1, _b=m2: e.tensor_sub(_o, _a, _b),
                  reads=[rc], accum=[rc])
            em.op("act", lambda e, _o=w1, _i=dd:
                  e.activation(_o, _i, AF.Sigmoid, scale=float(1.0 / 1.5)),
                  reads=[rc], accum=[rc])
            em.op("dve", lambda e, _o=w2, _i=w1:
                  e.tensor_scalar(_o, _i, -1.0, 1.0, ALU.mult, ALU.add),
                  reads=[rc], accum=[rc])
            em.op("dve", lambda e, _o=wv, _i=eq1, _s=w1:
                  e.tensor_single_scalar(_o, _i, _s, ALU.mult), reads=[rt_, rc], accum=[rt_])
            em.op("dve", lambda e, _o=rt_[:, 12:24], _i=msk, _s=m2:
                  e.tensor_single_scalar(_o, _i, _s, ALU.is_ge), reads=[rt_, rc], accum=[rt_])
            em.op("dve", lambda e, _o=wv, _i=rt_[:, 12:24], _s=w2, _t=wv:
                  e.scalar_tensor_tensor(_o, _i, _s, _t, ALU.mult, ALU.add),
                  reads=[rt_, rc], accum=[rt_])
            em.op("pe", lambda e, _o=ctx.P[7][0:12, 0:128], _i=wv, _id=sb["identf"][:]:
                  e.transpose(_o, _i, _id), reads=[rt_, sb["identf"]], writes=[ctx.P[7]])
            em.op("dve", lambda e, _o=wsc[:, tc * 128:(tc + 1) * 128], _i=ctx.P[7][0:12, 0:128]:
                  e.tensor_copy(_o, _i), reads=[ctx.P[7]], accum=[wsc])
        em.dma("sp", ctx.dram["WSC"][:, sl], wsc[0:12, :], reads=[wsc])



def ph8_experts(ctx):
    em, nc, sb = ctx.em, ctx.nc, ctx.sb
    BT = 1024
    NB = T // BT
    XS2 = ctx.dram["XS2"].rearrange("(c p) t -> p c t", p=128)
    XR1 = ctx.dram["XR1"].rearrange("(c p) t -> p c t", p=128)
    XR2 = ctx.dram["XR2"].rearrange("(c p) t -> p c t", p=128)
    XS4 = ctx.dram["XS4"].rearrange("(c p) t -> p c t", p=128)
    YS = ctx.dram["YS"].rearrange("(c p) t -> p c t", p=128)
    ysyr = ctx.alloc("p8_ysyr", [128, NCH * BT], F32)
    xs2b = ctx.alloc("p8_xs2", [128, NCH * BT], BF16)
    act = ctx.alloc("p8_act", [128, 24 * BT], BF16)
    we1a = [ctx.alloc(f"p8_w1a{i}", [128, 1024], BF16) for i in range(2)]
    we1g = [ctx.alloc(f"p8_w1g{i}", [128, 1024], BF16) for i in range(2)]
    we2m = [ctx.alloc(f"p8_w2{i}", [128, 3072], BF16) for i in range(2)]
    wrow = ctx.alloc("p8_wrow", [1, BT], F32)
    wrowb = ctx.alloc("p8_wrowb", [1, BT], BF16)
    be2rb = ctx.alloc("p8_be2rb", [1, 1024], BF16)
    wbc = ctx.alloc("p8_wbc", [128, BT], F32)
    be1sb = ctx.alloc("p8_be1", [128, 48], F32)
    be2r = ctx.alloc("p8_be2", [1, 1024], F32)
    gtmp = [ctx.alloc(f"p8_g{i}", [128, 512], F32) for i in range(2)]
    atmp = [ctx.alloc(f"p8_a{i}", [128, 512], F32) for i in range(2)]
    xr1c = [ctx.alloc(f"p8_x1c{i}", [128, 512], F32) for i in range(2)]
    xs4c = [ctx.alloc(f"p8_x4c{i}", [128, 512], BF16) for i in range(2)]
    xr2t = ctx.alloc("p8_xr2", [128, NCH * 512], F32)
    rows = ctx.alloc("p8_rows", [1, 2 * 512], F32)
    abc = ctx.alloc("p8_abc", [128, 512], F32)
    nmbc = ctx.alloc("p8_nmbc", [128, 512], F32)
    gabc = ctx.alloc("p8_gabc", [128, 512], F32)
    be1c = ctx.alloc("p8_be1c", [128, 8], F32)
    em.dma("sp", be1c[:], ctx.inp["be1c"][:, :], writes=[be1c])
    for b in range(NB):
        bl = slice(b * BT, (b + 1) * BT)
        em.dma("sp", ysyr[:].rearrange("p (c t) -> p c t", c=NCH), YS[:, :, bl],
               writes=[ysyr])
        em.dma("sp", xs2b[:].rearrange("p (c t) -> p c t", c=NCH), XS2[:, :, bl],
               writes=[xs2b])
        for e in range(12):
            em.dma("sp", wrow[0:1, :], ctx.dram["WSC"][e:e + 1, bl], writes=[wrow])
            em.dma("sp", be2r[0:1, :], ctx.inp["be2"][e:e + 1, :], writes=[be2r])
            em.op("dve", lambda e_: e_.tensor_copy(wrowb[0:1, :], wrow[0:1, :]),
                  reads=[wrow], writes=[wrowb])
            em.op("dve", lambda e_: e_.tensor_copy(be2rb[0:1, :], be2r[0:1, :]),
                  reads=[be2r], writes=[be2rb])
            em.dma("sp", be1sb[:], ctx.inp["be1"][e, :, :], writes=[be1sb])
            for ss in range(2):
                em.op("pe", lambda e_, _o=ctx.P[6][:, :], _l=sb["ones_rowb"][0:1, :],
                      _r=wrowb[0:1, ss * 512:(ss + 1) * 512]:
                      e_.matmul(_o, _l, _r, start=True, stop=True),
                      reads=[wrowb, sb["ones_rowb"]], writes=[ctx.P[6]])
                em.op("dve", lambda e_, _o=wbc[:, ss * 512:(ss + 1) * 512], _i=ctx.P[6][:, :]:
                      e_.tensor_copy(_o, _i), reads=[ctx.P[6]], accum=[wbc])
            for j in range(24):
                wa, wg = we1a[j % 2], we1g[j % 2]
                em.dma("pool", wa[:], ctx.inp["we1"][e, j, :, :], writes=[wa])
                em.dma("pool", wg[:], ctx.inp["we1"][e, 24 + j, :, :], writes=[wg])
                for ss in range(2):
                    psa = ctx.P[(ss % 2) * 2]
                    psg = ctx.P[(ss % 2) * 2 + 1]
                    for k in range(NCH):
                        em.op("pe", lambda e_, _o=psa[:, :], _l=wa[:, k * 128:(k + 1) * 128],
                              _r=xs2b[:, k * BT + ss * 512: k * BT + (ss + 1) * 512],
                              _st=(k == 0), _sp=(k == NCH - 1):
                              e_.matmul(_o, _l, _r, start=_st, stop=_sp),
                              reads=[wa, xs2b], accum=[psa])
                        em.op("pe", lambda e_, _o=psg[:, :], _l=wg[:, k * 128:(k + 1) * 128],
                              _r=xs2b[:, k * BT + ss * 512: k * BT + (ss + 1) * 512],
                              _st=(k == 0), _sp=(k == NCH - 1):
                              e_.matmul(_o, _l, _r, start=_st, stop=_sp),
                              reads=[wg, xs2b], accum=[psg])
                    g, a = gtmp[ss], atmp[ss]
                    em.op("act", lambda e_, _o=g[:], _i=psg[:, :], _b=be1sb[:, 24 + j:25 + j]:
                          e_.activation(_o, _i, AF.Gelu, bias=_b),
                          reads=[psg, be1sb], writes=[g])
                    em.op("dve", lambda e_, _o=a[:], _i=psa[:, :], _b=be1sb[:, j:j + 1]:
                          e_.tensor_scalar_add(_o, _i, _b), reads=[psa, be1sb], writes=[a])
                    em.op("dve", lambda e_, _o=a[:], _a=a[:], _b=g[:]:
                          e_.tensor_mul(_o, _a, _b), reads=[a, g], writes=[a])
                    em.op("dve", lambda e_, _o=act[:, j * BT + ss * 512: j * BT + (ss + 1) * 512],
                          _a=a[:], _b=wbc[:, ss * 512:(ss + 1) * 512]:
                          e_.tensor_mul(_o, _a, _b), reads=[a, wbc], accum=[act])
            for m in range(NCH):
                w2 = we2m[m % 2]
                em.dma("pool", w2[:], ctx.inp["we2"][e, m, :, :], writes=[w2])
                for ss in range(2):
                    ps = ctx.P[4 + (m * 2 + ss) % 2]
                    for k in range(24):
                        em.op("pe", lambda e_, _o=ps[:, :], _l=w2[:, k * 128:(k + 1) * 128],
                              _r=act[:, k * BT + ss * 512: k * BT + (ss + 1) * 512],
                              _st=(k == 0), _sp=False:
                              e_.matmul(_o, _l, _r, start=_st, stop=_sp),
                              reads=[w2, act], accum=[ps])
                    em.op("pe", lambda e_, _o=ps[:, :], _l=be2rb[0:1, m * 128:(m + 1) * 128],
                          _r=wrowb[0:1, ss * 512:(ss + 1) * 512]:
                          e_.matmul(_o, _l, _r, start=False, stop=True),
                          reads=[be2rb, wrowb], accum=[ps])
                    em.op("dve", lambda e_, _o=ysyr[:, m * BT + ss * 512: m * BT + (ss + 1) * 512],
                          _i=ysyr[:, m * BT + ss * 512: m * BT + (ss + 1) * 512], _p=ps[:, :]:
                          e_.tensor_add(_o, _i, _p), reads=[ysyr, ps], accum=[ysyr])
        for ss2 in range(2):
            s = b * 2 + ss2
            sl = slice(s * 512, (s + 1) * 512)
            ysl = [ysyr[:, c * BT + ss2 * 512: c * BT + (ss2 + 1) * 512] for c in range(NCH)]
            arow = rows[0:1, 0:512]; nmrow = rows[0:1, 512:1024]
            ln_stats_seg(ctx, s, ysl, NCH, 1.0 / D, arow, nmrow)
            bcast_row(ctx, arow, abc, 4)
            bcast_row(ctx, nmrow, nmbc, 5)
            for c in range(NCH):
                x1 = xr1c[c % 2]
                em.dma("sp", x1[:], ctx.dram["XR1"][c * 128:(c + 1) * 128, sl], writes=[x1])
                t = gtmp[c % 2]
                em.op("dve", lambda e_, _o=t[:], _a=ysl[c], _b=nmbc[:]:
                      e_.tensor_add(_o, _a, _b), reads=[ysyr, nmbc], writes=[t])
                em.op("dve", lambda e_, _o=t[:], _a=t[:], _b=abc[:]:
                      e_.tensor_mul(_o, _a, _b), reads=[t, abc], writes=[t])
                em.op("dve", lambda e_, _o=xr2t[:, c * 512:(c + 1) * 512], _a=t[:],
                      _b=x1[:]:
                      e_.tensor_add(_o, _a, _b), reads=[t, x1], accum=[xr2t])
            em.dma("sp", XR2[:, :, sl], xr2t[:].rearrange("p (c t) -> p c t", c=NCH),
                   reads=[xr2t])
            srcl = [xr2t[:, c * 512:(c + 1) * 512] for c in range(NCH)]
            ln_stats_seg(ctx, s, srcl, NCH, 1.0 / D, arow, nmrow)
            em.dma("sp", ctx.dram["RA4"][0:1, sl], arow, reads=[rows])
            em.dma("sp", ctx.dram["RNM4"][0:1, sl], nmrow, reads=[rows])
            bcast_row(ctx, nmrow, nmbc, 5)
            em.op("dve", lambda e_: e_.tensor_copy(ctx.sb["_bcr"][0:1, 0:512], arow),
                  reads=[rows], accum=[ctx.sb["_bcr"]])
            for c in range(NCH):
                em.op("pe", lambda e_, _o=ctx.P[4][:, :], _l=ctx.g1rb[0:1, c * 128:(c + 1) * 128],
                      _r=ctx.sb["_bcr"][0:1, 0:512]:
                      e_.matmul(_o, _l, _r, start=True, stop=True),
                      reads=[ctx.sb["_bcr"], ctx.g1rb], writes=[ctx.P[4]])
                em.op("dve", lambda e_, _o=gabc[:], _i=ctx.P[4][:, :]:
                      e_.tensor_copy(_o, _i), reads=[ctx.P[4]], writes=[gabc])
                t = gtmp[c % 2]
                em.op("dve", lambda e_, _o=t[:], _a=xr2t[:, c * 512:(c + 1) * 512], _b=nmbc[:]:
                      e_.tensor_add(_o, _a, _b), reads=[xr2t, nmbc], writes=[t])
                em.op("dve", lambda e_, _o=t[:], _a=t[:], _b=gabc[:]:
                      e_.tensor_mul(_o, _a, _b), reads=[t, gabc], writes=[t])
                x4 = xs4c[c % 2]
                em.op("dve", lambda e_, _o=x4[:], _i=t[:],
                      _b=be1c[:, c:c + 1]:
                      e_.tensor_scalar_add(_o, _i, _b), reads=[t, be1c], writes=[x4])
                em.dma("sp", ctx.dram["XS4"][c * 128:(c + 1) * 128, sl], x4[:], reads=[x4])



def ph9_qkv(ctx):
    em, nc, sb = ctx.em, ctx.nc, ctx.sb
    XS4 = ctx.dram["XS4"].rearrange("(c p) t -> p c t", p=128)
    QF = ctx.dram["QF"]
    wq = load_weight_bf16(ctx, "p9_w", ctx.inp["qkv_w"], 1024, 3072)
    bq = ctx.alloc("p9_bq", [128, 8], F32)
    em.dma("sp", bq[:], ctx.inp["qkv_bq"][:, :], writes=[bq])
    bkv = ctx.alloc("p9_bkv", [1, 2048], F32)
    em.dma("sp", bkv[:], ctx.inp["qkv_bkv"][:, :], writes=[bkv])
    bkvb = ctx.alloc("p9_bkvb", [1, 2048], BF16)
    em.op("dve", lambda e: e.tensor_copy(bkvb[:], bkv[:]), reads=[bkv], writes=[bkvb])
    xs = [ctx.alloc(f"p9_xs{i}", [128, NCH * 512], BF16) for i in range(2)]
    t1 = [ctx.alloc(f"p9_t1{i}", [128, 512], F32) for i in range(2)]
    t2 = [ctx.alloc(f"p9_t2{i}", [128, 512], F32) for i in range(2)]
    t3 = [ctx.alloc(f"p9_t3{i}", [128, 512], F32) for i in range(2)]
    ob = [ctx.alloc(f"p9_ob{i}", [128, 512], BF16) for i in range(4)]
    kft = [ctx.alloc(f"p9_kf{i}", [128, 1024], BF16) for i in range(2)]
    vt = [ctx.alloc(f"p9_v{i}", [128, 1024], BF16) for i in range(2)]
    kvacc = ctx.alloc("p9_kvacc", [128, 8 * 64], F32)
    kvall = ctx.alloc("p9_kvall", [128, 8 * 64], BF16)
    zacc = ctx.alloc("p9_z", [1, 1024], F32)
    zc = ctx.alloc("p9_zc", [128, 8], F32)
    em.op("dve", lambda e: e.memset(kvacc[:], 0.0), writes=[kvacc])
    em.op("dve", lambda e: e.memset(zacc[:], 0.0), writes=[zacc])

    def elup1(ps, bias_fn, out_bf, i, extra_reads, out_res):
        a, n, p_ = t1[i % 2], t2[i % 2], t3[i % 2]
        em.op("dve", bias_fn(a, ps), reads=[ps] + extra_reads, writes=[a])
        em.op("dve", lambda e, _o=n[:], _i=a[:]: e.tensor_scalar_min(_o, _i, 0.0),
              reads=[a], writes=[n])
        em.op("act", lambda e, _o=n[:], _i=n[:]: e.activation(_o, _i, AF.Exp),
              reads=[n], writes=[n])
        em.op("dve", lambda e, _o=p_[:], _i=a[:]: e.tensor_scalar_max(_o, _i, 0.0),
              reads=[a], writes=[p_])
        em.op("dve", lambda e, _o=out_bf, _a=n[:], _b=p_[:]:
              e.tensor_add(_o, _a, _b), reads=[n, p_], accum=[out_res])

    for s in range(NSEG):
        sl = slice(s * 512, (s + 1) * 512)
        xsb = xs[s % 2]
        em.dma("sp", xsb[:].rearrange("p (c t) -> p c t", c=NCH), XS4[:, :, sl], writes=[xsb])
        for m in range(NCH):
            ps = ctx.P[m % 2]
            for k in range(NCH):
                em.op("pe", lambda e, _o=ps[:, :], _l=lhsT(wq, 1024, 3072, k, m),
                      _r=xsb[:, k * 512:(k + 1) * 512], _st=(k == 0), _sp=(k == NCH - 1):
                      e.matmul(_o, _l, _r, start=_st, stop=_sp),
                      reads=[wq, xsb], accum=[ps])
            o = ob[m % 4]
            elup1(ps, lambda a, p: (lambda e, _o=a[:], _i=p[:, :], _b=bq[:, m:m + 1]:
                                    e.tensor_scalar_add(_o, _i, _b)), o[:], m, [bq], o)
            em.dma("sp", QF[m * 128:(m + 1) * 128, sl], o[:], reads=[o])
        for tc in range(4):
            kf = kft[tc % 2]
            v = vt[tc % 2]
            for kv_i, off in ((0, 1024), (1, 2048)):
                for rs in range(2):
                    ps = ctx.P[2 + (tc * 4 + kv_i * 2 + rs) % 4]
                    for k in range(NCH):
                        em.op("pe", lambda e, _o=ps[:, :],
                              _l=xsb[:, k * 512 + tc * 128: k * 512 + (tc + 1) * 128],
                              _r=wq[:, k * 3072 + off + rs * 512: k * 3072 + off + (rs + 1) * 512],
                              _st=(k == 0), _sp=False:
                              e.matmul(_o, _l, _r, start=_st, stop=_sp),
                              reads=[wq, xsb], accum=[ps])
                    em.op("pe", lambda e, _o=ps[:, :], _l=sb["ones_rowb"][0:1, :],
                          _r=bkvb[0:1, (off - 1024) + rs * 512:(off - 1024) + (rs + 1) * 512]:
                          e.matmul(_o, _l, _r, start=False, stop=True),
                          reads=[bkvb, sb["ones_rowb"]], accum=[ps])
                    if kv_i == 0:
                        elup1(ps, lambda a, p: (lambda e, _o=a[:], _i=p[:, :]:
                                                e.tensor_copy(_o, _i)),
                              kf[:, rs * 512:(rs + 1) * 512], tc * 2 + rs, [], kf)
                    else:
                        em.op("dve", lambda e, _o=v[:, rs * 512:(rs + 1) * 512], _i=ps[:, :]:
                              e.tensor_copy(_o, _i), reads=[ps], accum=[v])
            for h in range(16):
                rb0 = (h % 2) * 64
                ps = ctx.P[6 + (h // 2) % 2]
                em.op("pe", lambda e, _o=ps[rb0:rb0 + 64, 0:64],
                      _l=kf[:, h * 64:(h + 1) * 64], _r=v[:, h * 64:(h + 1) * 64]:
                      e.matmul(_o, _l, _r, start=True, stop=True),
                      reads=[kf, v], accum=[ps])
                em.op("dve", lambda e, _o=kvacc[rb0:rb0 + 64, (h // 2) * 64:(h // 2 + 1) * 64],
                      _i=ps[rb0:rb0 + 64, 0:64],
                      _a=kvacc[rb0:rb0 + 64, (h // 2) * 64:(h // 2 + 1) * 64]:
                      e.tensor_add(_o, _a, _i), reads=[ps, kvacc], accum=[kvacc])
            for rs in range(2):
                em.op("pe", lambda e, _o=ctx.P[6][0:1, 0:512], _l=sb["onesb"][:],
                      _r=kf[:, rs * 512:(rs + 1) * 512]:
                      e.matmul(_o, _l, _r, start=True, stop=True),
                      reads=[kf, sb["onesb"]], writes=[ctx.P[6]])
                em.op("dve", lambda e, _o=zacc[0:1, rs * 512:(rs + 1) * 512],
                      _i=zacc[0:1, rs * 512:(rs + 1) * 512], _p=ctx.P[6][0:1, 0:512]:
                      e.tensor_add(_o, _i, _p), reads=[zacc, ctx.P[6]], writes=[zacc])
    em.op("dve", lambda e: e.tensor_copy(kvall[:], kvacc[:]), reads=[kvacc], writes=[kvall])
    em.dma("sp", ctx.dram["KVD"][:, :], kvall[:], reads=[kvall])
    for c in range(8):
        em.op("pe", lambda e, _o=ctx.P[1][:, 0:1], _i=zacc[0:1, c * 128:(c + 1) * 128],
              _id=sb["identf"][0:1, 0:1]:
              e.transpose(_o, _i, _id), reads=[zacc, sb["identf"]], writes=[ctx.P[1]])
        em.op("dve", lambda e, _o=zc[:, c:c + 1], _i=ctx.P[1][:, 0:1]:
              e.tensor_copy(_o, _i), reads=[ctx.P[1]], accum=[zc])
    em.dma("sp", ctx.dram["ZCD"][:, :], zc[:], reads=[zc])



def ph11_attnout(ctx):
    em, nc, sb = ctx.em, ctx.nc, ctx.sb
    QF = ctx.dram["QF"].rearrange("(c p) t -> p c t", p=128)
    XR2 = ctx.dram["XR2"].rearrange("(c p) t -> p c t", p=128)
    X3 = ctx.dram["X3"].rearrange("(c p) t -> p c t", p=128)
    XS5 = ctx.dram["XS5"]
    wout = load_weight_bf16(ctx, "p11_w", ctx.inp["w_out1"], 1024, 1024)
    kvall = ctx.alloc("p11_kv", [128, 8 * 64], BF16)
    em.dma("sp", kvall[:], ctx.dram["KVD"][:, :], writes=[kvall])
    zc = ctx.alloc("p11_zc", [128, 8], F32)
    em.dma("sp", zc[:], ctx.dram["ZCD"][:, :], writes=[zc])
    bo = ctx.alloc("p11_bo", [128, 8], F32)
    em.dma("sp", bo[:], ctx.inp["b_out1"][:, :], writes=[bo])
    be2c = ctx.alloc("p11_be2c", [128, 8], F32)
    em.dma("sp", be2c[:], ctx.inp["be2c"][:, :], writes=[be2c])
    qf = [ctx.alloc(f"p11_qf{i}", [128, NCH * 512], BF16) for i in range(2)]
    xr2 = ctx.alloc("p11_xr2", [128, NCH * 512], F32)
    ybf = ctx.alloc("p11_y", [128, NCH * 512], BF16)
    x3t = ctx.alloc("p11_x3", [128, NCH * 512], F32)
    dent = ctx.alloc("p11_dent", [128, 512], BF16)
    rrow = ctx.alloc("p11_rr", [1, 512], F32)
    rrowb = ctx.alloc("p11_rrb", [1, 512], BF16)
    rb = ctx.alloc("p11_rb", [128, 512], F32)
    rows = ctx.alloc("p11_rows", [1, 2 * 512], F32)
    a4r = ctx.alloc("p11_a4r", [1, 512], F32)
    nm4r = ctx.alloc("p11_nm4r", [1, 512], F32)
    abc = ctx.alloc("p11_abc", [128, 512], F32)
    nmbc = ctx.alloc("p11_nmbc", [128, 512], F32)
    gabc = ctx.alloc("p11_gabc", [128, 512], F32)
    tt = [ctx.alloc(f"p11_t{i}", [128, 512], F32) for i in range(2)]
    xs5c = [ctx.alloc(f"p11_x5{i}", [128, 512], BF16) for i in range(2)]
    for s in range(NSEG):
        sl = slice(s * 512, (s + 1) * 512)
        qfb = qf[s % 2]
        em.dma("sp", qfb[:].rearrange("p (c t) -> p c t", c=NCH), QF[:, :, sl], writes=[qfb])
        em.dma("sp", xr2[:].rearrange("p (c t) -> p c t", c=NCH), XR2[:, :, sl], writes=[xr2])
        em.dma("sp", a4r[0:1, :], ctx.dram["RA4"][0:1, sl], writes=[a4r])
        em.dma("sp", nm4r[0:1, :], ctx.dram["RNM4"][0:1, sl], writes=[nm4r])
        for h in range(16):
            c = h // 2
            rb0 = (h % 2) * 64
            psY = ctx.P[c % 2]
            qslab = qfb[rb0:rb0 + 64, c * 512:(c + 1) * 512]
            em.op("pe", lambda e, _o=psY[rb0:rb0 + 64, :],
                  _l=kvall[rb0:rb0 + 64, (h // 2) * 64:(h // 2 + 1) * 64],
                  _r=qslab:
                  e.matmul(_o, _l, _r, start=True, stop=True),
                  reads=[kvall, qfb], accum=[psY])
            em.op("dve", lambda e, _o=dent[rb0:rb0 + 64, :], _i=qslab,
                  _zc=zc[rb0:rb0 + 64, c:c + 1]:
                  e.tensor_scalar_mul(_o, _i, _zc), reads=[qfb, zc], writes=[dent])
            em.op("pe", lambda e, _o=ctx.P[4][0:1, :], _l=sb["onesb"][rb0:rb0 + 64, 0:1],
                  _r=dent[rb0:rb0 + 64, :]:
                  e.matmul(_o, _l, _r, start=True, stop=True),
                  reads=[dent, sb["onesb"]], writes=[ctx.P[4]])
            em.op("dve", lambda e, _o=rrow[0:1, :], _i=ctx.P[4][0:1, :]:
                  e.tensor_scalar_max(_o, _i, 1e-6), reads=[ctx.P[4]], writes=[rrow])
            em.op("act", lambda e, _o=rrow[0:1, :], _i=rrow[0:1, :]:
                  e.activation(_o, _i, AF.Ln), reads=[rrow], writes=[rrow])
            em.op("act", lambda e, _o=rrow[0:1, :], _i=rrow[0:1, :]:
                  e.activation(_o, _i, AF.Exp, scale=-1.0), reads=[rrow], writes=[rrow])
            em.op("dve", lambda e: e.tensor_copy(rrowb[0:1, :], rrow[0:1, :]),
                  reads=[rrow], writes=[rrowb])
            em.op("pe", lambda e, _o=ctx.P[5][rb0:rb0 + 64, :], _l=sb["ones_rowb"][0:1, 0:64],
                  _r=rrowb[0:1, :]:
                  e.matmul(_o, _l, _r, start=True, stop=True),
                  reads=[rrowb, sb["ones_rowb"]], writes=[ctx.P[5]])
            em.op("dve", lambda e, _o=rb[rb0:rb0 + 64, :], _i=ctx.P[5][rb0:rb0 + 64, :]:
                  e.tensor_copy(_o, _i), reads=[ctx.P[5]], writes=[rb])
            em.op("dve", lambda e, _o=ybf[rb0:rb0 + 64, c * 512:(c + 1) * 512],
                  _a=psY[rb0:rb0 + 64, :], _b=rb[rb0:rb0 + 64, :]:
                  e.tensor_mul(_o, _a, _b), reads=[psY, rb], accum=[ybf])
        arow = rows[0:1, 0:512]; nmrow = rows[0:1, 512:1024]
        bcast_row(ctx, a4r[0:1, :], abc, 4)
        bcast_row(ctx, nm4r[0:1, :], nmbc, 5)
        for m in range(NCH):
            ps = ctx.P[2 + m % 2]
            for k in range(NCH):
                em.op("pe", lambda e, _o=ps[:, :], _l=lhsT(wout, 1024, 1024, k, m),
                      _r=ybf[:, k * 512:(k + 1) * 512], _st=(k == 0), _sp=(k == NCH - 1):
                      e.matmul(_o, _l, _r, start=_st, stop=_sp),
                      reads=[wout, ybf], accum=[ps])
            t = tt[m % 2]
            em.op("dve", lambda e, _o=t[:], _a=xr2[:, m * 512:(m + 1) * 512], _b=nmbc[:]:
                  e.tensor_add(_o, _a, _b), reads=[xr2, nmbc], writes=[t])
            em.op("dve", lambda e, _o=t[:], _a=t[:], _b=abc[:]:
                  e.tensor_mul(_o, _a, _b), reads=[t, abc], writes=[t])
            em.op("dve", lambda e, _o=x3t[:, m * 512:(m + 1) * 512], _i=ps[:, :],
                  _b=bo[:, m:m + 1], _t=t[:]:
                  e.scalar_tensor_tensor(_o, _i, _b, _t, ALU.add, ALU.add),
                  reads=[ps, bo, t], accum=[x3t])
        em.dma("sp", X3[:, :, sl], x3t[:].rearrange("p (c t) -> p c t", c=NCH),
               reads=[x3t])
        srcl = [x3t[:, c * 512:(c + 1) * 512] for c in range(NCH)]
        ln_stats_seg(ctx, s, srcl, NCH, 1.0 / D, arow, nmrow)
        bcast_row(ctx, nmrow, nmbc, 5)
        em.op("dve", lambda e: e.tensor_copy(ctx.sb["_bcr"][0:1, 0:512], arow),
              reads=[rows], accum=[ctx.sb["_bcr"]])
        for c in range(NCH):
            em.op("pe", lambda e, _o=ctx.P[4][:, :], _l=ctx.g2rb[0:1, c * 128:(c + 1) * 128],
                  _r=ctx.sb["_bcr"][0:1, 0:512]:
                  e.matmul(_o, _l, _r, start=True, stop=True),
                  reads=[ctx.sb["_bcr"], ctx.g2rb], writes=[ctx.P[4]])
            em.op("dve", lambda e, _o=gabc[:], _i=ctx.P[4][:, :]:
                  e.tensor_copy(_o, _i), reads=[ctx.P[4]], writes=[gabc])
            t = tt[c % 2]
            em.op("dve", lambda e, _o=t[:], _a=x3t[:, c * 512:(c + 1) * 512], _b=nmbc[:]:
                  e.tensor_add(_o, _a, _b), reads=[x3t, nmbc], writes=[t])
            em.op("dve", lambda e, _o=t[:], _a=t[:], _b=gabc[:]:
                  e.tensor_mul(_o, _a, _b), reads=[t, gabc], writes=[t])
            x5 = xs5c[c % 2]
            em.op("dve", lambda e, _o=x5[:], _i=t[:], _b=be2c[:, c:c + 1]:
                  e.tensor_scalar_add(_o, _i, _b), reads=[t, be2c], writes=[x5])
            em.dma("sp", XS5[c * 128:(c + 1) * 128, sl], x5[:], reads=[x5])



def ph12a_mlp1(ctx):
    em, nc = ctx.em, ctx.nc
    XS5 = ctx.dram["XS5"].rearrange("(c p) t -> p c t", p=128)
    HH = ctx.dram["HH"]
    w1 = load_weight_bf16(ctx, "p12_w1", ctx.inp["w1"], 1024, 4096)
    b1 = ctx.alloc("p12_b1", [128, 32], F32)
    em.dma("sp", b1[:], ctx.inp["b1"][:, :], writes=[b1])
    xs = [ctx.alloc(f"p12_xs{i}", [128, NCH * 512], BF16) for i in range(2)]
    stg = [ctx.alloc(f"p12_st{i}", [128, 512], BF16) for i in range(4)]
    for s in range(NSEG):
        sl = slice(s * 512, (s + 1) * 512)
        xsb = xs[s % 2]
        em.dma("sp", xsb[:].rearrange("p (c t) -> p c t", c=NCH), XS5[:, :, sl], writes=[xsb])
        for m in range(32):
            ps = ctx.P[m % 4]
            for k in range(NCH):
                em.op("pe", lambda e, _o=ps[:, :], _l=lhsT(w1, 1024, 4096, k, m),
                      _r=xsb[:, k * 512:(k + 1) * 512], _st=(k == 0), _sp=(k == NCH - 1):
                      e.matmul(_o, _l, _r, start=_st, stop=_sp),
                      reads=[w1, xsb], accum=[ps])
            st = stg[m % 4]
            em.op("act", lambda e, _o=st[:], _i=ps[:, :], _b=b1[:, m:m + 1]:
                  e.activation(_o, _i, AF.Gelu, bias=_b), reads=[ps, b1], writes=[st])
            em.dma("sp", HH[m * 128:(m + 1) * 128, sl], st[:], reads=[st])



def ph12b_mlp2(ctx):
    em, nc, sb = ctx.em, ctx.nc, ctx.sb
    HH = ctx.dram["HH"].rearrange("(c p) t -> p c t", p=128)
    XR3 = ctx.dram["XR3"].rearrange("(c p) t -> p c t", p=128)
    XS7 = ctx.dram["XS7"]
    w2 = load_weight_bf16(ctx, "p12_w2", ctx.inp["w2"], 4096, 1024)
    b2 = ctx.alloc("p12_b2", [128, 8], F32)
    em.dma("sp", b2[:], ctx.inp["b2"][:, :], writes=[b2])
    hh = ctx.alloc("p12_hh", [128, 32 * 512], BF16)
    x3c = [ctx.alloc(f"p12_x3{i}", [128, 512], F32) for i in range(2)]
    xr2c = [ctx.alloc(f"p12_xr2{i}", [128, 512], F32) for i in range(2)]
    o1t = ctx.alloc("p12_o1", [128, NCH * 512], F32)
    xr3t = ctx.alloc("p12_xr3", [128, NCH * 512], F32)
    xs7c = [ctx.alloc(f"p12_x7{i}", [128, 512], BF16) for i in range(2)]
    tt = [ctx.alloc(f"p12_t{i}", [128, 512], F32) for i in range(2)]
    rows = ctx.alloc("p12_rows", [1, 2 * 512], F32)
    abc = ctx.alloc("p12_abc", [128, 512], F32)
    nmbc = ctx.alloc("p12_nmbc", [128, 512], F32)
    for s in range(NSEG):
        sl = slice(s * 512, (s + 1) * 512)
        em.dma("sp", hh[:].rearrange("p (c t) -> p c t", c=32), HH[:, :, sl], writes=[hh])
        for m in range(NCH):
            ps = ctx.P[m % 4]
            for k in range(32):
                em.op("pe", lambda e, _o=ps[:, :], _l=lhsT(w2, 4096, 1024, k, m),
                      _r=hh[:, k * 512:(k + 1) * 512], _st=(k == 0), _sp=(k == 31):
                      e.matmul(_o, _l, _r, start=_st, stop=_sp),
                      reads=[w2, hh], accum=[ps])
            x3 = x3c[m % 2]
            em.dma("sp", x3[:], ctx.dram["X3"][m * 128:(m + 1) * 128, sl], writes=[x3])
            em.op("dve", lambda e, _o=o1t[:, m * 512:(m + 1) * 512], _i=ps[:, :],
                  _b=b2[:, m:m + 1], _t=x3[:]:
                  e.scalar_tensor_tensor(_o, _i, _b, _t, ALU.add, ALU.add),
                  reads=[ps, b2, x3], accum=[o1t])
        arow = rows[0:1, 0:512]; nmrow = rows[0:1, 512:1024]
        srcl = [o1t[:, c * 512:(c + 1) * 512] for c in range(NCH)]
        ln_stats_seg(ctx, s, srcl, NCH, 1.0 / D, arow, nmrow)
        bcast_row(ctx, arow, abc, 4)
        bcast_row(ctx, nmrow, nmbc, 5)
        for c in range(NCH):
            x2 = xr2c[c % 2]
            em.dma("sp", x2[:], ctx.dram["XR2"][c * 128:(c + 1) * 128, sl], writes=[x2])
            t = tt[c % 2]
            em.op("dve", lambda e, _o=t[:], _a=o1t[:, c * 512:(c + 1) * 512], _b=nmbc[:]:
                  e.tensor_add(_o, _a, _b), reads=[o1t, nmbc], writes=[t])
            em.op("dve", lambda e, _o=t[:], _a=t[:], _b=abc[:]:
                  e.tensor_mul(_o, _a, _b), reads=[t, abc], writes=[t])
            em.op("dve", lambda e, _o=xr3t[:, c * 512:(c + 1) * 512], _a=t[:], _b=x2[:]:
                  e.tensor_add(_o, _a, _b), reads=[t, x2], accum=[xr3t])
        em.dma("sp", XR3[:, :, sl], xr3t[:].rearrange("p (c t) -> p c t", c=NCH),
               reads=[xr3t])
        srcl = [xr3t[:, c * 512:(c + 1) * 512] for c in range(NCH)]
        ln_stats_seg(ctx, s, srcl, NCH, 1.0 / D, arow, nmrow)
        bcast_row(ctx, arow, abc, 4)
        bcast_row(ctx, nmrow, nmbc, 5)
        for c in range(NCH):
            t = tt[c % 2]
            em.op("dve", lambda e, _o=t[:], _a=xr3t[:, c * 512:(c + 1) * 512], _b=nmbc[:]:
                  e.tensor_add(_o, _a, _b), reads=[xr3t, nmbc], writes=[t])
            x7 = xs7c[c % 2]
            em.op("dve", lambda e, _o=x7[:], _a=t[:], _b=abc[:]:
                  e.tensor_mul(_o, _a, _b), reads=[t, abc], writes=[x7])
            em.dma("sp", XS7[c * 128:(c + 1) * 128, sl], x7[:], reads=[x7])



def load_weight_int(ctx, name, dram_ap, K, M):
    em = ctx.em
    w = ctx.alloc(name, [128, (K // 128) * M], BF16)
    wi = [ctx.alloc(name + f"_i{j}", [128, M], I32) for j in range(2)]
    for k in range(K // 128):
        t = wi[k % 2]
        em.dma("sp", t[:], dram_ap[k * 128:(k + 1) * 128, :], writes=[t])
        em.op("act", lambda e, _o=w[:, k * M:(k + 1) * M], _i=t[:]:
              e.activation(_o, _i, AF.Copy, scale=float(1.0 / 127.0)),
              reads=[t], accum=[w])
    return w


def ph13_gateup(ctx):
    em, nc = ctx.em, ctx.nc
    XS7 = ctx.dram["XS7"].rearrange("(c p) t -> p c t", p=128)
    HU = ctx.dram["HU"]
    wg = load_weight_int(ctx, "p13_wg", ctx.inp["wg"], 1024, 3072)
    wu = load_weight_int(ctx, "p13_wu", ctx.inp["wu"], 1024, 3072)
    xs = [ctx.alloc(f"p13_xs{i}", [128, NCH * 512], BF16) for i in range(2)]
    sg = [ctx.alloc(f"p13_sg{i}", [128, 512], F32) for i in range(2)]
    stg = [ctx.alloc(f"p13_st{i}", [128, 512], BF16) for i in range(4)]
    for s in range(NSEG):
        sl = slice(s * 512, (s + 1) * 512)
        xsb = xs[s % 2]
        em.dma("sp", xsb[:].rearrange("p (c t) -> p c t", c=NCH), XS7[:, :, sl], writes=[xsb])
        for m in range(24):
            psg, psu = ctx.P[(m % 2) * 2], ctx.P[(m % 2) * 2 + 1]
            for k in range(NCH):
                em.op("pe", lambda e, _o=psg[:, :], _l=lhsT(wg, 1024, 3072, k, m),
                      _r=xsb[:, k * 512:(k + 1) * 512], _st=(k == 0), _sp=(k == NCH - 1):
                      e.matmul(_o, _l, _r, start=_st, stop=_sp),
                      reads=[wg, xsb], accum=[psg])
                em.op("pe", lambda e, _o=psu[:, :], _l=lhsT(wu, 1024, 3072, k, m),
                      _r=xsb[:, k * 512:(k + 1) * 512], _st=(k == 0), _sp=(k == NCH - 1):
                      e.matmul(_o, _l, _r, start=_st, stop=_sp),
                      reads=[wu, xsb], accum=[psu])
            g = sg[m % 2]
            em.op("act", lambda e, _o=g[:], _i=psg[:, :]:
                  e.activation(_o, _i, AF.Silu), reads=[psg], writes=[g])
            st = stg[m % 4]
            em.op("dve", lambda e, _o=st[:], _a=g[:], _b=psu[:, :]:
                  e.tensor_mul(_o, _a, _b), reads=[g, psu], writes=[st])
            em.dma("sp", HU[m * 128:(m + 1) * 128, sl], st[:], reads=[st])



def ph14_down(ctx):
    em, nc, sb = ctx.em, ctx.nc, ctx.sb
    HU = ctx.dram["HU"].rearrange("(c p) t -> p c t", p=128)
    wd = load_weight_int(ctx, "p14_wd", ctx.inp["wd"], 3072, 1024)
    hu = [ctx.alloc(f"p14_hu{i}", [128, 24 * 512], BF16) for i in range(2)]
    o2t = ctx.alloc("p14_o2", [128, NCH * 512], F32)
    xr3c = [ctx.alloc(f"p14_x3{i}", [128, 512], F32) for i in range(2)]
    xr4t = ctx.alloc("p14_xr4", [128, NCH * 512], F32)
    ot = [ctx.alloc(f"p14_ot{i}", [128, 1024], F32) for i in range(2)]
    tt = [ctx.alloc(f"p14_t{i}", [128, 512], F32) for i in range(2)]
    rows = ctx.alloc("p14_rows", [1, 2 * 512], F32)
    abc = ctx.alloc("p14_abc", [128, 512], F32)
    nmbc = ctx.alloc("p14_nmbc", [128, 512], F32)
    for s in range(NSEG):
        sl = slice(s * 512, (s + 1) * 512)
        hub = hu[s % 2]
        em.dma("sp", hub[:].rearrange("p (c t) -> p c t", c=24), HU[:, :, sl], writes=[hub])
        for m in range(NCH):
            ps = ctx.P[m % 2]
            for k in range(24):
                em.op("pe", lambda e, _o=ps[:, :], _l=lhsT(wd, 3072, 1024, k, m),
                      _r=hub[:, k * 512:(k + 1) * 512], _st=(k == 0), _sp=(k == 23):
                      e.matmul(_o, _l, _r, start=_st, stop=_sp),
                      reads=[wd, hub], accum=[ps])
            em.op("dve", lambda e, _o=o2t[:, m * 512:(m + 1) * 512], _i=ps[:, :]:
                  e.tensor_copy(_o, _i), reads=[ps], accum=[o2t])
        arow = rows[0:1, 0:512]; nmrow = rows[0:1, 512:1024]
        srcl = [o2t[:, c * 512:(c + 1) * 512] for c in range(NCH)]
        ln_stats_seg(ctx, s, srcl, NCH, 1.0 / D, arow, nmrow)
        bcast_row(ctx, arow, abc, 4)
        bcast_row(ctx, nmrow, nmbc, 5)
        for c in range(NCH):
            x3 = xr3c[c % 2]
            em.dma("sp", x3[:], ctx.dram["XR3"][c * 128:(c + 1) * 128, sl], writes=[x3])
            t = tt[c % 2]
            em.op("dve", lambda e, _o=t[:], _a=o2t[:, c * 512:(c + 1) * 512], _b=nmbc[:]:
                  e.tensor_add(_o, _a, _b), reads=[o2t, nmbc], writes=[t])
            em.op("dve", lambda e, _o=t[:], _a=t[:], _b=abc[:]:
                  e.tensor_mul(_o, _a, _b), reads=[t, abc], writes=[t])
            em.op("dve", lambda e, _o=xr4t[:, c * 512:(c + 1) * 512], _a=t[:], _b=x3[:]:
                  e.tensor_add(_o, _a, _b), reads=[t, x3], accum=[xr4t])
        for tc in range(4):
            o = ot[tc % 2]
            for c in range(NCH):
                ps = ctx.P[2 + (tc * NCH + c) % 2]
                em.op("pe", lambda e, _o=ps[:, 0:128],
                      _i=xr4t[:, c * 512 + tc * 128: c * 512 + (tc + 1) * 128],
                      _id=sb["identf"][:]:
                      e.transpose(_o, _i, _id), reads=[xr4t, sb["identf"]], writes=[ps])
                em.op("dve", lambda e, _o=o[:, c * 128:(c + 1) * 128], _i=ps[:, 0:128]:
                      e.tensor_copy(_o, _i), reads=[ps], accum=[o])
            em.dma("sp", ctx.out[s * 512 + tc * 128: s * 512 + (tc + 1) * 128, :], o[:],
                   reads=[o])



def ph012_entry(ctx):
    em, nc, sb = ctx.em, ctx.nc, ctx.sb
    x0 = ctx.inp["x0"]
    XR0 = ctx.dram["XR0"].rearrange("(c p) t -> p c t", p=128)
    XM = ctx.dram["XM"]
    XG = ctx.dram["XG"]
    w = load_weight_bf16(ctx, "p012_w", ctx.inp["w_in"], 1024, 4096)
    bcol = ctx.alloc("p012_b", [128, 32], F32)
    em.dma("sp", bcol[:], ctx.inp["b_in"][:, :], writes=[bcol])
    xt = [ctx.alloc(f"p012_xt{i}", [128, 1024], F32) for i in range(2)]
    xf = [ctx.alloc(f"p012_xf{i}", [128, NCH * 512], F32) for i in range(2)]
    xs = [ctx.alloc(f"p012_xs{i}", [128, NCH * 512], BF16) for i in range(2)]
    stg = [ctx.alloc(f"p012_st{i}", [128, 512], F32) for i in range(4)]
    ab = ctx.alloc("p012_ab", [128, 512], F32)
    nmb = ctx.alloc("p012_nmb", [128, 512], F32)
    tmp = [ctx.alloc(f"p012_tmp{i}", [128, 512], F32) for i in range(2)]
    rows = ctx.alloc("p012_rows", [1, 2 * 512], F32)
    ident = sb["identf"]

    def entry_stats(s):
        sl = slice(s * 512, (s + 1) * 512)
        fbuf = xf[s % 2]
        xsb = xs[s % 2]
        for q in range(4):
            tt = s * 4 + q
            tbuf = xt[tt % 2]
            em.dma("sp", tbuf[:], x0[tt * 128:(tt + 1) * 128, :], writes=[tbuf])
            for c in range(NCH):
                ps = ctx.P[2 + (tt * NCH + c) % 2]
                em.op("pe", lambda e, _o=ps[:, 0:128], _i=tbuf[:, c * 128:(c + 1) * 128], _id=ident[:]:
                      e.transpose(_o, _i, _id),
                      reads=[tbuf, ident], writes=[ps])
                em.op("dve", lambda e, _o=fbuf[:, c * 512 + q * 128: c * 512 + (q + 1) * 128], _i=ps[:, 0:128]:
                      e.tensor_copy(_o, _i),
                      reads=[ps], accum=[fbuf])
        em.dma("sp", XR0[:, :, sl], fbuf[:].rearrange("p (c t) -> p c t", c=NCH), reads=[fbuf])
        src = [fbuf[:, c * 512:(c + 1) * 512] for c in range(NCH)]
        arow = rows[0:1, 0:512]; nmrow = rows[0:1, 512:1024]
        ln_stats_seg(ctx, s, src, NCH, 1.0 / D, arow, nmrow)
        em.dma("sp", ctx.dram["RA0"][0:1, sl], arow, reads=[rows])
        em.dma("sp", ctx.dram["RNM0"][0:1, sl], nmrow, reads=[rows])
        bcast_row(ctx, arow, ab, 4)
        bcast_row(ctx, nmrow, nmb, 5)
        for c in range(NCH):
            t = tmp[c % 2]
            em.op("dve", lambda e, _o=t[:], _a=fbuf[:, c * 512:(c + 1) * 512], _b=nmb[:]:
                  e.tensor_add(_o, _a, _b), reads=[fbuf, nmb], writes=[t])
            em.op("dve", lambda e, _o=xsb[:, c * 512:(c + 1) * 512], _a=t[:], _b=ab[:]:
                  e.tensor_mul(_o, _a, _b), reads=[t, ab], accum=[xsb])

    def inproj(s):
        xsb = xs[s % 2]
        for m in range(32):
            ps = ctx.P[m % 2]
            for k in range(NCH):
                em.op("pe", lambda e, _o=ps[:, :], _l=lhsT(w, 1024, 4096, k, m),
                      _r=xsb[:, k * 512:(k + 1) * 512], _st=(k == 0), _sp=(k == NCH - 1):
                      e.matmul(_o, _l, _r, start=_st, stop=_sp),
                      reads=[w, xsb], accum=[ps])
            st = stg[m % 4]
            em.op("dve", lambda e, _o=st[:], _i=ps[:, :], _b=bcol[:, m:m + 1]:
                  e.tensor_scalar_add(_o, _i, _b),
                  reads=[ps, bcol], writes=[st])
            dst = XM if m < 16 else XG
            mm = m if m < 16 else m - 16
            em.dma("sp", dst[mm * 128:(mm + 1) * 128, s * 512:(s + 1) * 512], st[:],
                   reads=[st])

    entry_stats(0)
    for s in range(1, NSEG):
        inproj(s - 1)
        entry_stats(s)
    inproj(NSEG - 1)





ALL_PHASES = None


def kernel(**inputs):
    global ALL_PHASES
    ins = {k: np.asarray(v) for k, v in inputs.items()}
    prep = host_prep(ins)
    phases = [ph012_entry, ph4_scan, ph6_gateout,
              ph7_dense, ph8_experts, ph9_qkv, ph11_attnout,
              ph12a_mlp1, ph12b_mlp2, ph13_gateup, ph14_down]
    nc = build_program(phases)
    in_maps = []
    x = ins["L0_attn_x"]
    for b in range(8):
        m = dict(prep)
        m["x0"] = np.ascontiguousarray(x[b])
        in_maps.append(m)
    res = run_bass_kernel_spmd(nc, in_maps, core_ids=list(range(8)))
    out = np.stack([np.asarray(res.results[b]["out"]) for b in range(8)], axis=0)
    return out.astype(np.float32)
